# version 13
# baseline (speedup 1.0000x reference)
import contextlib
import numpy as np
import ml_dtypes
import concourse.bass as bass
import concourse.mybir as mybir
from concourse.bass_utils import run_bass_kernel_spmd

F32 = mybir.dt.float32
BF16 = mybir.dt.bfloat16
AF = mybir.ActivationFunctionType
ALU = mybir.AluOpType
AX = mybir.AxisListType
I32 = mybir.dt.int32
from concourse.bass_core_rust import IndirectOffsetOnAxis
bf = ml_dtypes.bfloat16

L = 8192
D = 2048
DH = 128
NCORES = 8
TPC = L // NCORES
NT = L // 128
NG = 8
NDC = D // 128
INTER = 8192
EPS = 1e-5
CSC = float(1.0 / np.sqrt(DH))
B0 = 8.0


class Sem:
    def __init__(self, nc, es, name):
        self.h = es.enter_context(nc.semaphore(name))
        self.c = 0
        self.name = name

    def inc(self, instr, n=1):
        instr.then_inc(self.h, n)
        self.c += n
        return self.c


ENGN = {"sync": 0, "gpsimd": 1, "vector": 2, "tensor": 3, "scalar": 4}


class Builder:
    def __init__(self, debug_taps=(), stage="full", nocc=False, preload=False):
        self.preload = preload
        self.nocc = nocc
        self.stage = stage
        self.debug = set(debug_taps)
        nc = self.nc = bass.Bass("TRN2", target_bir_lowering=False, debug=False)
        self.es = contextlib.ExitStack()
        self.seen = {e: {} for e in ENGN}

        d = nc.dram_tensor
        self.x = d("x", [L, D], F32, kind="ExternalInput").ap()
        self.wqkv = d("wqkv", [768, D], F32, kind="ExternalInput").ap()
        self.cos_t = d("cos_t", [DH, L], BF16, kind="ExternalInput").ap()
        self.sin_t = d("sin_t", [DH, L], BF16, kind="ExternalInput").ap()
        self.pmT = d("pmT", [128, 128], BF16, kind="ExternalInput").ap()
        self.ident = d("ident", [128, 128], BF16, kind="ExternalInput").ap()

        self.wqkv16 = d("wqkv16", [768, D], BF16, kind="Internal").ap()

        a = nc.alloc_sbuf_tensor
        self.sident = a("sident", [128, 128], BF16, side="right")
        self.ceps = a("ceps", [128, 1], F32, side="right")
        self.cnegb0 = a("cnegb0", [128, 1], F32, side="right")
        self.ptstg = [a(f"ptstg{i}", [128, 512], BF16, side="right")
                      for i in range(4)]

        es, S = self.es, lambda n: Sem(nc, es, n)
        self.s_ldx = [S(f"s_ldx{i}") for i in range(4)]
        self.s_cs = [S(f"s_cs{i}") for i in range(2)]
        self.s_ldc = S("s_ldc")
        self.s_cast = S("s_cast")
        self.s_ldw = S("s_ldw")
        self.s_act = S("s_act")
        self.s_dve = S("s_dve")
        self.s_pe = S("s_pe")
        self.s_pool = S("s_pool")
        self.s_st = S("s_st")
        self.s_yw = [S(f"s_yw{c}") for c in range(8)]
        self.s_zf = S("s_zf")
        self.s_yd = S("s_yd")
        self.s_cc = [S(f"s_cc{c}") for c in range(8)]
        self.s_cw = S("s_cw")
        self.s_cu = S("s_cu")
        self.s_cg = S("s_cg")
        self.s_cd = S("s_cd")
        self.s_l3 = [S(f"s_l3{i}") for i in range(2)]
        self.s_x2 = [S(f"s_x2_{i}") for i in range(2)]
        self.s_w4 = [S(f"s_w4{i}") for i in range(2)]
        self.s_aw = [S(f"s_aw{i}") for i in range(2)]
        self.s_ar = [S(f"s_ar{i}") for i in range(8)]
        self.s_wd = [S(f"s_wd{i}") for i in range(2)]
        self.s_x2r = [S(f"s_x2r{i}") for i in range(2)]
        self.s_so = [S(f"s_so{i}") for i in range(2)]
        self.s_pt = [S(f"s_pt{i}") for i in range(4)]
        self.s_ps = [S(f"s_ps{i}") for i in range(4)]
        self.wp = {}

    def load_act_set(self, sc, name):
        from concourse.hw_specs import get_activation_tables
        for idx, nm in enumerate(get_activation_tables(self.nc.m.arch)):
            if nm == name:
                return sc.add_instruction(mybir.InstLoadActFuncSet(
                    name=self.nc.get_next_instruction_name(),
                    act_func_set_id=idx, ins=[], outs=[]))
        raise ValueError(name)

    def W(self, eng, engname, sem, val):
        if val is None or val <= 0:
            return
        if self.seen[engname].get(sem.name, 0) >= val:
            return
        eng.wait_ge(sem.h, val)
        self.seen[engname][sem.name] = val

    def build(self):
        nc = self.nc
        with self.es, nc.Block() as block:
            with contextlib.ExitStack() as qkv_scope:
                sbq = lambda *a, **k: qkv_scope.enter_context(nc.sbuf_tensor(*a, **k))
                self.qT = [sbq(f"qT{h}", [128, L], BF16) for h in range(2)]
                self.kT = [sbq(f"kT{h}", [128, L], BF16) for h in range(2)]
                self.V = [sbq(f"V{h}", [128, NT * 132], BF16) for h in range(2)]
                with contextlib.ExitStack() as p1_scope:
                    with nc.named_scope("p1"):
                        self.emit_p1(block, p1_scope)
                    if "p1" in self.debug and self.stage == "p1":
                        self.emit_p1_taps(block)
                    if self.stage == "ln":
                        self.emit_ln_taps(block)
                if self.stage in ("p2", "p3", "full"):
                    self.phase_barrier(block)
                    with contextlib.ExitStack() as p2_scope:
                        with nc.named_scope("p2"):
                            self.emit_p2(block, p2_scope)
            if self.stage in ("p3", "full"):
                self.phase_barrier(block)
                with contextlib.ExitStack() as s34:
                    with nc.named_scope("p3"):
                        self.emit_consts(block, s34)
                    with contextlib.ExitStack() as zscope:
                        self.zpT = zscope.enter_context(
                            nc.sbuf_tensor("zpT", [128, NDC * 1024], BF16))
                        with nc.named_scope("p3b"):
                            with contextlib.ExitStack() as p3_scope:
                                self.emit_p3(block, p3_scope)
                        if self.stage == "p3":
                            self.emit_p3_taps(block)
                        else:
                            with nc.named_scope("p4"):
                                self.emit_p4a(block)
                    if self.stage != "p3":
                        with contextlib.ExitStack() as p4_scope:
                            with nc.named_scope("p4b"):
                                self.emit_p4b(block, p4_scope)
            self.emit_final_wait(block)
        return nc

    def emit_p3_taps(self, block):
        nc, W = self.nc, self.W
        o_x2 = nc.dram_tensor("o_x2", [1024, D], F32, kind="ExternalOutput").ap()
        o_zpT = nc.dram_tensor("o_zpT", [128, NDC * 1024], BF16,
                               kind="ExternalOutput").ap()
        cvt3 = nc.alloc_sbuf_tensor("cvt3", [128, D], F32)

        @block.sync
        def _(sy):
            W(sy, "sync", self.s_x2[0], self.wp["x2_done0"])
            W(sy, "sync", self.s_x2[1], self.wp["x2_done1"])
            W(sy, "sync", self.s_dve, self.wp["zpT_done"])
            for c in range(8):
                self.s_st.inc(sy.dma_start(
                    out=o_x2[c * 128:(c + 1) * 128, :],
                    in_=self.x2dram[c * 128:(c + 1) * 128, :]), 16)
            self.s_st.inc(sy.dma_start(out=o_zpT, in_=self.zpT[:]), 16)

    def phase_barrier(self, block):
        W = self.W
        marks = [(self.s_pe, self.s_pe.c), (self.s_dve, self.s_dve.c),
                 (self.s_act, self.s_act.c), (self.s_pool, self.s_pool.c),
                 (self.s_st, self.s_st.c), (self.s_cast, self.s_cast.c),
                 (self.s_ldw, self.s_ldw.c), (self.s_zf, self.s_zf.c),
                 (self.s_yd, self.s_yd.c)]
        marks += [(sw, sw.c) for sw in self.s_yw]
        marks = [(sem, cnt) for (sem, cnt) in marks if cnt > 0]
        for engname, sec in (("sync", block.sync), ("gpsimd", block.gpsimd),
                             ("vector", block.vector), ("tensor", block.tensor),
                             ("scalar", block.scalar)):
            @sec
            def _(e, engname=engname):
                for sem, cnt in marks:
                    W(e, engname, sem, cnt)

    def emit_ln_taps(self, block):
        nc = self.nc
        o_lnxT = nc.dram_tensor("o_lnxT", [128, NDC * 1024], BF16,
                                kind="ExternalOutput").ap()
        @block.scalar
        def _(sc):
            self.W(sc, "scalar", self.s_dve, self.wp_trev_last)
            self.s_st.inc(sc.dma_start(out=o_lnxT, in_=self.lnxT[:]), 16)

    def emit_p1(self, block, scope):
        wp = self.wp
        W = self.W
        nc = self.nc
        sb = lambda *args, **kw: scope.enter_context(nc.sbuf_tensor(*args, **kw))
        ps = lambda *args, **kw: scope.enter_context(nc.psum_tensor(*args, **kw))
        self.xr = sb("xr", [128, 3 * D], F32)
        self.lnx = sb("lnx", [128, 2 * D], BF16)
        self.lnxT2 = [sb(f"lnxT{i}", [128, NDC * 512], BF16) for i in range(2)]
        self.wqkvT = sb("wqkvT", [128, NDC * 768], BF16)
        self.cosb = sb("cosb", [128, 2 * 512], BF16)
        self.sinb = sb("sinb", [128, 2 * 512], BF16)
        self.sums = sb("sums", [128, 8], F32)
        self.sumsq = sb("sumsq", [128, 8], F32)
        self.stats = sb("stats", [128, 2 * 16], F32)
        self.sqscrap = sb("sqscrap", [128, D], BF16)
        self.qc = [sb(f"qc{i}", [128, 512], BF16) for i in range(2)]
        self.rs = [sb(f"rs{i}", [128, 512], BF16) for i in range(2)]
        self.roplain = [sb(f"roplain{i}", [128, 512], BF16) for i in range(2)]
        self.spmT = sb("spmT", [128, 128], BF16)
        self.p_tr = [ps(f"p_tr{i}", [128, 512], BF16) for i in range(2)]
        self.p_trv = ps("p_trv", [128, 512], BF16)

        self.p_qkv = [ps(f"p_qkv{i}", [128, 512], F32) for i in range(3)]
        self.p_rot = ps("p_rot", [128, 512], F32)

        if self.preload:
            @block.scalar
            def _(sc):
                sc.preload_act_table(AF.Sqrt)

        @block.gpsimd
        def _(g):
            self.s_cast.inc(g.dma_start(out=self.wqkv16, in_=self.wqkv), 16)
        wp["wqkv16"] = self.s_cast.c

        @block.sync
        def _(s):
            self.s_ldc.inc(s.dma_start(out=self.sident[:], in_=self.ident), 16)
            self.s_ldc.inc(s.dma_start(out=self.spmT[:], in_=self.pmT), 16)
            W(s, "sync", self.s_cast, wp["wqkv16"])
            for et in range(6):
                buf = self.lnxT2[0] if et < 4 else self.lnxT2[1]
                off = (et if et < 4 else et - 4) * D
                self.s_ldw.inc(s.dma_start(
                    out=buf[:, off:off + D],
                    in_=self.wqkv16[et * 128:(et + 1) * 128, :]), 16)
        wp["wq_st"] = self.s_ldw.c
        wp["consts"] = self.s_ldc.c

        for et in range(6):
            for dc2 in range(4):
                @block.tensor
                def _(t, et=et, dc2=dc2):
                    W(t, "tensor", self.s_ldw, wp["wq_st"])
                    W(t, "tensor", self.s_ldc, wp["consts"])
                    k = f"wqev_{et * 4 + dc2 - 2}"
                    if k in wp:
                        W(t, "tensor", self.s_dve, wp[k])
                    buf = self.lnxT2[0] if et < 4 else self.lnxT2[1]
                    off = (et if et < 4 else et - 4) * D
                    for j in range(4):
                        dc = dc2 * 4 + j
                        last = t.transpose(
                            self.p_tr[dc2 % 2][:, j * 128:(j + 1) * 128],
                            buf[:, off + dc * 128:off + (dc + 1) * 128],
                            self.sident[:])
                    self.s_pe.inc(last)
                    wp[f"wqtr_{et * 4 + dc2}"] = self.s_pe.c

                @block.vector
                def _(v, et=et, dc2=dc2):
                    W(v, "vector", self.s_pe, wp[f"wqtr_{et * 4 + dc2}"])
                    out_ap = self.wqkvT.ap().rearrange(
                        "p (dc e) -> p dc e", e=768)[
                        :, dc2 * 4:(dc2 + 1) * 4, et * 128:(et + 1) * 128]
                    in_ap = self.p_tr[dc2 % 2].ap().rearrange(
                        "p (a b) -> p a b", a=4)
                    wp[f"wqev_{et * 4 + dc2}"] = self.s_dve.inc(
                        v.tensor_copy(out_ap, in_ap))
        wp["wqkvT"] = self.s_dve.c
        wp["wq_pe"] = self.s_pe.c
        wp["wqlast0"] = wp["wqev_22"]
        wp["wqlast1"] = wp["wqev_23"]

        @block.vector
        def _(v):
            v.memset(self.ceps[:], EPS)
            v.memset(self.cnegb0[:], -B0)
            for h in range(2):
                ones = self.V[h].ap().rearrange("p (t c) -> p t c", c=132)[:, :, 128:129]
                v.memset(ones, 1.0)
            self.s_dve.inc(v.engine_nop())
        self.wp["vones"] = self.s_dve.c

        ldx_after = {}
        norm_after = {}
        sum_after = {}
        sq_after = {}
        trev_after = {}
        cos_after = {}
        self.rope_after = {}
        self.v_after = {}
        tr_slot_user = {0: wp["wqlast0"], 1: wp["wqlast1"]}
        pqkv_users = {0: [], 1: [], 2: []}
        roplain_users = {0: [], 1: []}
        qcrs_users = {0: [], 1: []}
        stats_rd = {0: None, 1: None}
        stats_rd_prev = {0: None, 1: None}
        sumsq_rd = {0: None, 1: None}
        p_rot_user = None
        p_trv_user = None

        for g in range(NG):
            for tau in range(8):
                T = g * 8 + tau

                @block.sync
                def _(s, T=T):
                    if T >= 3:
                        W(s, "sync", self.s_act, norm_after[T - 3])
                    ldx_after[T] = self.s_ldx[T % 4].inc(s.dma_start(
                        out=self.xr[:, (T % 3) * D:(T % 3 + 1) * D],
                        in_=self.x[T * 128:(T + 1) * 128, :]), 16)

                @block.vector
                def _(v, T=T, tau=tau):
                    W(v, "vector", self.s_ldx[T % 4], ldx_after[T])
                    sum_after[T] = self.s_dve.inc(v.reduce_sum(
                        self.sums[:, tau:tau + 1],
                        self.xr[:, (T % 3) * D:(T % 3 + 1) * D], axis=AX.X))

                if tau % 2 != 1:
                    continue
                b = (tau // 2) % 2
                bt = tau - 1
                cols = slice(bt, bt + 2)
                mean = self.stats[:, 8 * b + 0:8 * b + 2]
                var = self.stats[:, 8 * b + 2:8 * b + 4]
                rstd = self.stats[:, 8 * b + 4:8 * b + 6]
                negmr = self.stats[:, 8 * b + 6:8 * b + 8]

                var_cnt = self.s_dve.c + 3
                negmr_cnt = self.s_dve.c + 6

                @block.scalar
                def _(sc, g=g, b=b, bt=bt, var=var, rstd=rstd, negmr=negmr,
                      var_cnt=var_cnt, negmr_cnt=negmr_cnt):
                    for tt in range(2):
                        TT = g * 8 + bt + tt
                        W(sc, "scalar", self.s_ldx[TT % 4], ldx_after[TT])
                        if TT > 0:
                            W(sc, "scalar", self.s_act, sq_after[TT - 1])
                        if sumsq_rd[b] is not None:
                            W(sc, "scalar", self.s_dve, sumsq_rd[b])
                        sq_after[TT] = self.s_act.inc(sc.activation(
                            self.sqscrap[:],
                            self.xr[:, (TT % 3) * D:(TT % 3 + 1) * D],
                            AF.Square,
                            accum_out=self.sumsq[:, bt + tt:bt + tt + 1]))
                    W(sc, "scalar", self.s_dve, var_cnt)
                    self.s_act.inc(sc.activation(rstd, var, AF.Sqrt,
                                                 bias=self.ceps[:, 0:1]))
                    wp[f"rstd_{g}_{tau}"] = self.s_act.c
                    W(sc, "scalar", self.s_dve, negmr_cnt)
                    for tt in range(2):
                        TT = g * 8 + bt + tt
                        if TT >= 2:
                            W(sc, "scalar", self.s_dve, trev_after[TT - 2])
                        norm_after[TT] = self.s_act.inc(sc.activation(
                            self.lnx[:, (TT % 2) * D:(TT % 2 + 1) * D],
                            self.xr[:, (TT % 3) * D:(TT % 3 + 1) * D],
                            AF.Identity, bias=negmr[:, tt:tt + 1],
                            scale=rstd[:, tt:tt + 1]))
                stats_rd[b] = self.s_act.c

                @block.vector
                def _(v, g=g, b=b, T=T, cols=cols, mean=mean, var=var):
                    W(v, "vector", self.s_dve, sum_after[T])
                    W(v, "vector", self.s_act, sq_after[T])
                    if stats_rd_prev[b] is not None:
                        W(v, "vector", self.s_act, stats_rd_prev[b])
                    c1 = self.s_dve.inc(v.tensor_scalar_mul(mean, self.sums[:, cols], 1.0 / D))
                    W(v, "vector", self.s_dve, c1)
                    c2 = self.s_dve.inc(v.tensor_tensor(self.sums[:, cols], mean, mean, ALU.mult))
                    W(v, "vector", self.s_dve, c2)
                    sumsq_rd[b] = self.s_dve.inc(v.scalar_tensor_tensor(
                        var, self.sumsq[:, cols], 1.0 / D, self.sums[:, cols],
                        ALU.mult, ALU.subtract))
                assert self.s_dve.c == var_cnt

                @block.vector
                def _(v, g=g, b=b, var=var, mean=mean, rstd=rstd, negmr=negmr):
                    W(v, "vector", self.s_act, wp[f"rstd_{g}_{tau}"])
                    c1 = self.s_dve.inc(v.reciprocal(var, rstd))
                    W(v, "vector", self.s_dve, c1)
                    c1 = self.s_dve.inc(v.tensor_copy(rstd, var))
                    W(v, "vector", self.s_dve, c1)
                    self.s_dve.inc(v.scalar_tensor_tensor(
                        negmr, mean, -1.0, rstd, ALU.mult, ALU.mult))
                assert self.s_dve.c == negmr_cnt
                stats_rd_prev[b] = stats_rd[b]

                for tt in range(2):
                    TT = g * 8 + bt + tt
                    tau2 = bt + tt

                    for pb in range(4):
                        @block.tensor
                        def _(t, TT=TT, pb=pb):
                            W(t, "tensor", self.s_ldc, wp["consts"])
                            W(t, "tensor", self.s_act, norm_after[TT])
                            u = tr_slot_user[pb % 2]
                            if u is not None:
                                W(t, "tensor", self.s_dve, u)
                            for j in range(4):
                                dc = pb * 4 + j
                                last = t.transpose(
                                    self.p_tr[pb % 2][:, j * 128:(j + 1) * 128],
                                    self.lnx[:, (TT % 2) * D + dc * 128:
                                             (TT % 2) * D + (dc + 1) * 128],
                                    self.sident[:])
                            self.s_pe.inc(last)
                            self.wp[f"tr_{TT}_{pb}"] = self.s_pe.c

                        @block.vector
                        def _(v, g=g, TT=TT, tau2=tau2, pb=pb):
                            W(v, "vector", self.s_pe, self.wp["wq_pe"])
                            W(v, "vector", self.s_pe, self.wp[f"tr_{TT}_{pb}"])
                            hb2 = g * 2 + tau2 // 4
                            if f"qkvmm_{hb2 - 2}_5" in self.wp:
                                W(v, "vector", self.s_pe,
                                  self.wp[f"qkvmm_{hb2 - 2}_5"])
                            out_ap = self.lnxT2[hb2 % 2].ap().rearrange(
                                "p (dc t) -> p dc t", dc=NDC)[
                                :, pb * 4:(pb + 1) * 4,
                                (tau2 % 4) * 128:(tau2 % 4 + 1) * 128]
                            in_ap = self.p_tr[pb % 2].ap().rearrange(
                                "p (a b) -> p a b", a=4)
                            tr_slot_user[pb % 2] = self.s_dve.inc(
                                v.tensor_copy(out_ap, in_ap))
                            trev_after[TT] = self.s_dve.c
                            self.wp_trev_last = self.s_dve.c

            if g == 3 and self.stage in ("p3", "full"):
                self.emit_ffn_casts(block)
                self.emit_pretranspose(block)

            for b in (range(2) if self.stage != "ln" else range(0)):
                hb = g * 2 + b
                tok0 = g * 1024 + b * 512

                @block.sync
                def _(s, hb=hb, tok0=tok0):
                    if hb >= 2:
                        W(s, "sync", self.s_pool, self.rope_after.get(hb - 2))
                        W(s, "sync", self.s_dve, self.wp.get(f"qcrs_done_{hb - 2}"))
                    sl = (hb % 2) * 512
                    self.s_cs[hb % 2].inc(s.dma_start(out=self.cosb[:, sl:sl + 512],
                                                      in_=self.cos_t[:, tok0:tok0 + 512]), 16)
                    cos_after[hb] = self.s_cs[hb % 2].inc(
                        s.dma_start(out=self.sinb[:, sl:sl + 512],
                                    in_=self.sin_t[:, tok0:tok0 + 512]), 16)

                def rope_post(et):
                    slot = et % 3
                    rsl = et % 2
                    sec, h = et // 2, et % 2
                    if sec < 2:
                        dstT = (self.qT if sec == 0 else self.kT)[h]

                        @block.vector
                        def _(v, hb=hb, et=et, slot=slot, rsl=rsl):
                            W(v, "vector", self.s_pe, wp[f"qkvmm_{hb}_{et}"])
                            for (sem, cnt) in roplain_users[rsl]:
                                W(v, "vector", sem, cnt)
                            roplain_users[rsl] = []
                            cnt = self.s_dve.inc(v.tensor_copy(
                                self.roplain[rsl][:], self.p_qkv[slot][:]))
                            wp[f"qkvevac_{hb}_{et}"] = cnt

                        @block.tensor
                        def _(t, hb=hb, et=et, slot=slot, rsl=rsl):
                            nonlocal p_rot_user
                            W(t, "tensor", self.s_dve, wp[f"qkvevac_{hb}_{et}"])
                            if p_rot_user is not None:
                                W(t, "tensor", self.s_dve, p_rot_user)
                            mm = t.matmul(self.p_rot[:], self.spmT[:],
                                          self.roplain[rsl][:], start=True, stop=True)
                            cnt = self.s_pe.inc(mm)
                            roplain_users[rsl].append((self.s_pe, cnt))
                            wp[f"rot_{hb}_{et}"] = cnt

                        @block.vector
                        def _(v, hb=hb, et=et, slot=slot, rsl=rsl):
                            nonlocal p_rot_user
                            sl = (hb % 2) * 512
                            W(v, "vector", self.s_cs[hb % 2], cos_after[hb])
                            for (sem, cnt) in qcrs_users[rsl]:
                                W(v, "vector", sem, cnt)
                            qcrs_users[rsl] = []
                            cnt = self.s_dve.inc(v.tensor_tensor(
                                self.qc[rsl][:], self.p_qkv[slot][:],
                                self.cosb[:, sl:sl + 512], ALU.mult))
                            pqkv_users[slot].append((self.s_dve, cnt))
                            W(v, "vector", self.s_pe, wp[f"rot_{hb}_{et}"])
                            cnt = self.s_dve.inc(v.tensor_tensor(
                                self.rs[rsl][:], self.p_rot[:],
                                self.sinb[:, sl:sl + 512], ALU.mult))
                            p_rot_user = cnt
                            wp[f"rs_{hb}_{et}"] = cnt
                            wp[f"qcrs_done_{hb}"] = cnt

                        @block.gpsimd
                        def _(gp, hb=hb, et=et, rsl=rsl, dstT=dstT, tok0=tok0):
                            W(gp, "gpsimd", self.s_dve, wp[f"rs_{hb}_{et}"])
                            cnt = self.s_pool.inc(gp.tensor_tensor(
                                dstT[:, tok0:tok0 + 512], self.qc[rsl][:],
                                self.rs[rsl][:], ALU.add))
                            qcrs_users[rsl].append((self.s_pool, cnt))
                            self.rope_after[hb] = cnt
                    else:
                        h = et % 2

                        @block.vector
                        def _(v, hb=hb, et=et, slot=slot, rsl=rsl):
                            W(v, "vector", self.s_pe, wp[f"qkvmm_{hb}_{et}"])
                            for (sem, cnt) in roplain_users[rsl]:
                                W(v, "vector", sem, cnt)
                            roplain_users[rsl] = []
                            cnt = self.s_dve.inc(v.tensor_copy(
                                self.roplain[rsl][:], self.p_qkv[slot][:]))
                            pqkv_users[slot].append((self.s_dve, cnt))
                            wp[f"qkvevac_{hb}_{et}"] = cnt

                        @block.tensor
                        def _(t, hb=hb, et=et, rsl=rsl):
                            nonlocal p_trv_user
                            W(t, "tensor", self.s_dve, wp[f"qkvevac_{hb}_{et}"])
                            if p_trv_user is not None:
                                W(t, "tensor", self.s_dve, p_trv_user)
                            for j in range(4):
                                last = t.transpose(self.p_trv[:, j * 128:(j + 1) * 128],
                                                   self.roplain[rsl][:, j * 128:(j + 1) * 128],
                                                   self.sident[:])
                            cnt = self.s_pe.inc(last)
                            roplain_users[rsl].append((self.s_pe, cnt))
                            wp[f"vtr_{hb}_{et}"] = cnt

                        @block.vector
                        def _(v, hb=hb, et=et, h=h, tok0=tok0):
                            nonlocal p_trv_user
                            W(v, "vector", self.s_pe, wp[f"vtr_{hb}_{et}"])
                            t0 = tok0 // 128
                            dst = self.V[h].ap().rearrange(
                                "p (t c) -> p t c", c=132)[:, t0:t0 + 4, 0:128]
                            src = self.p_trv.ap().rearrange("p (a b) -> p a b", a=4)
                            p_trv_user = self.s_dve.inc(v.tensor_copy(dst, src))
                            self.v_after[hb] = p_trv_user

                for et in range(6):
                    slot = et % 3

                    @block.tensor
                    def _(t, g=g, b=b, hb=hb, et=et, slot=slot):
                        if et == 0:
                            W(t, "tensor", self.s_dve, wp["wqkvT"])
                            W(t, "tensor", self.s_dve, trev_after[g * 8 + b * 4 + 3])
                        for (sem, cnt) in pqkv_users[slot]:
                            W(t, "tensor", sem, cnt)
                        pqkv_users[slot] = []
                        for dc in range(NDC):
                            last = t.matmul(self.p_qkv[slot][:],
                                            self.wqkvT[:, dc * 768 + et * 128:
                                                       dc * 768 + (et + 1) * 128],
                                            self.lnxT2[hb % 2][:, dc * 512:
                                                               (dc + 1) * 512],
                                            start=(dc == 0), stop=(dc == NDC - 1))
                        self.s_pe.inc(last)
                        wp[f"qkvmm_{hb}_{et}"] = self.s_pe.c

                    if et >= 1:
                        rope_post(et - 1)
                rope_post(5)


    def emit_p2(self, block, scope):
        nc, wp, W = self.nc, self.wp, self.W
        sb = lambda *a, **k: scope.enter_context(nc.sbuf_tensor(*a, **k))
        ps = lambda *a, **k: scope.enter_context(nc.psum_tensor(*a, **k))

        p_S = [ps(f"p_S{i}", [128, 1024], F32) for i in range(2)]
        p_yq = [ps(f"p_yq{i}", [128, 132], F32) for i in range(2)]
        Pbuf = [sb(f"Pbuf{i}", [128, 1024], BF16) for i in range(2)]
        ystage = [sb(f"ystage{i}", [128, 128], BF16) for i in range(16)]
        sd_sb = [sb(f"sd_sb{i}", [128, 128], F32) for i in range(2)]
        mask_sb = sb("mask_sb", [128, 128], F32)
        st2 = sb("st2", [128, 4], F32)
        RSG = 0

        self.mask = nc.dram_tensor("mask", [128, 128], F32, kind="ExternalInput").ap()
        self.sidx = nc.dram_tensor("sidx", [128, 16], I32, kind="ExternalInput").ap()
        self.yin = [nc.dram_tensor(f"yin{c}", [1024, D], BF16, kind="Internal").ap()
                    for c in range(8)]
        self.rsout = [nc.dram_tensor(f"rsout{c}", [128, D], BF16, kind="Internal").ap()
                      for c in range(8)]
        si_sb = sb("si_sb", [128, 16], I32)
        zt_sb = sb("zt_sb", [128, D], BF16)

        if self.preload:
            @block.scalar
            def _(sc):
                W(sc, "scalar", self.s_act, self.s_act.c)
                sc.preload_act_table(AF.Exp)

        @block.sync
        def _(sy):
            self.s_ldc.inc(sy.dma_start(out=mask_sb[:], in_=self.mask), 16)
            self.s_ldc.inc(sy.dma_start(out=si_sb[:], in_=self.sidx), 16)
        wp["mask"] = self.s_ldc.c

        @block.vector
        def _(v):
            wp["zt"] = self.s_dve.inc(v.memset(zt_sb[:], 0.0))

        @block.scalar
        def _(sc):
            W(sc, "scalar", self.s_dve, wp["zt"])
            for c in range(8):
                for r in range(8):
                    self.s_zf.inc(sc.dma_start(
                        out=self.yin[c][r * 128:(r + 1) * 128, :], in_=zt_sb[:]), 16)
        wp["zf"] = self.s_zf.c

        u_pS = {0: [], 1: []}
        u_Pb = {0: [], 1: []}
        u_yq = {0: [], 1: []}
        u_sd = []
        self._last_yst = 0

        flat = []
        unit_of = []
        u = 0
        for tau in range(NT):
            for h in range(2):
                nfull = tau // 8
                for j in range(nfull + 1):
                    flat.append((tau, h, j, nfull + 1))
                    unit_of.append(u)
                u += 1

        def emit_S(idx):
            tau, h, j, nb = flat[idx]
            slot = idx % 2
            k0 = j * 1024
            Wd = 1024 if j < nb - 1 else (tau % 8 + 1) * 128
            nch = Wd // 128

            @block.tensor
            def _(t):
                for (sem, cnt) in u_pS[slot]:
                    W(t, "tensor", sem, cnt)
                u_pS[slot].clear()
                for jj in range(nch):
                    last = t.matmul(p_S[slot][:, jj * 128:(jj + 1) * 128],
                                    self.kT[h][:, k0 + jj * 128:k0 + (jj + 1) * 128],
                                    self.qT[h][:, tau * 128:(tau + 1) * 128],
                                    start=True, stop=True)
                self.s_pe.inc(last)
                wp[f"S_{idx}"] = self.s_pe.c

        def emit_sd(idx):
            tau, h, j, nb = flat[idx]
            slot = idx % 2
            is_last = (j == nb - 1)
            if not is_last:
                return
            Wd = (tau % 8 + 1) * 128

            @block.vector
            def _(v):
                W(v, "vector", self.s_ldc, wp["mask"])
                W(v, "vector", self.s_pe, wp[f"S_{idx}"])
                for (sem, cnt) in u_sd:
                    W(v, "vector", sem, cnt)
                u_sd.clear()
                wp[f"sd_{idx}"] = self.s_dve.inc(v.tensor_tensor(
                    sd_sb[idx % 2][:], p_S[slot][:, Wd - 128:Wd],
                    mask_sb[:], ALU.add))
                u_pS[slot].append((self.s_dve, wp[f"sd_{idx}"]))

        def emit_exp(sc, idx):
            tau, h, j, nb = flat[idx]
            slot = idx % 2
            is_last = (j == nb - 1)
            Wd = 1024 if not is_last else (tau % 8 + 1) * 128
            W(sc, "scalar", self.s_pe, wp[f"S_{idx}"])
            W(sc, "scalar", self.s_dve, self.wp["vones"])
            for (sem, cnt) in u_Pb[slot]:
                W(sc, "scalar", sem, cnt)
            u_Pb[slot].clear()
            bias = self.cnegb0[:, 0:1]
            if is_last:
                W(sc, "scalar", self.s_dve, wp[f"sd_{idx}"])
                if Wd > 128:
                    cnt = self.s_act.inc(sc.activation(
                        Pbuf[slot][:, 0:Wd - 128], p_S[slot][:, 0:Wd - 128],
                        AF.Exp, bias=bias, scale=CSC))
                    u_pS[slot].append((self.s_act, cnt))
                cnt = self.s_act.inc(sc.activation(
                    Pbuf[slot][:, Wd - 128:Wd], sd_sb[idx % 2][:],
                    AF.Exp, bias=bias, scale=CSC))
                u_sd.append((self.s_act, cnt))
            else:
                cnt = self.s_act.inc(sc.activation(
                    Pbuf[slot][:, 0:Wd], p_S[slot][:, 0:Wd],
                    AF.Exp, bias=bias, scale=CSC))
                u_pS[slot].append((self.s_act, cnt))
            wp[f"exp_{idx}"] = self.s_act.c

        def emit_chain(idx):
            nonlocal ycount
            tau, h, j, nb = flat[idx]
            up = unit_of[idx] % 2
            slot = idx % 2
            k0 = j * 1024
            is_last = (j == nb - 1)
            Wd = 1024 if not is_last else (tau % 8 + 1) * 128
            nch = Wd // 128

            @block.tensor
            def _(t):
                W(t, "tensor", self.s_act, wp[f"exp_{idx}"])
                yslot = unit_of[idx] % 2
                if j == 0:
                    for (sem, cnt) in u_yq[yslot]:
                        W(t, "tensor", sem, cnt)
                    u_yq[yslot].clear()
                for jj in range(nch):
                    kt = k0 // 128 + jj
                    last = t.matmul(
                        p_yq[yslot][:, 0:129],
                        Pbuf[slot][:, jj * 128:(jj + 1) * 128],
                        self.V[h].ap().rearrange("p (t c) -> p t c", c=132)[
                            :, kt, 0:129],
                        start=(j == 0 and jj == 0),
                        stop=(is_last and jj == nch - 1))
                cnt = self.s_pe.inc(last)
                u_Pb[slot].append((self.s_pe, cnt))
                wp[f"YQ_{idx}"] = cnt

            if is_last:
                ysl = (tau % 8) * 2 + h
                yslot = unit_of[idx] % 2
                @block.vector
                def _(v):
                    W(v, "vector", self.s_pe, wp[f"YQ_{idx}"])
                    W(v, "vector", self.s_dve, self._last_yst)
                    c1 = self.s_dve.inc(v.reciprocal(
                        st2[:, RSG:RSG + 1], p_yq[yslot][:, 128:129]))
                    W(v, "vector", self.s_dve, c1)
                    if tau >= 8:
                        W(v, "vector", self.s_yw[tau // 8 - 1],
                          self.s_yw[tau // 8 - 1].c)
                        if "p2y" in self.debug:
                            W(v, "vector", self.s_yd, self.s_yd.c)
                    cnt = self.s_dve.inc(v.tensor_scalar_mul(
                        ystage[ysl][:], p_yq[yslot][:, 0:128],
                        st2[:, RSG:RSG + 1]))
                    u_yq[yslot].append((self.s_dve, cnt))
                    wp[f"yst_{idx}"] = cnt
                    self._last_yst = cnt

                @block.gpsimd
                def _(gp):
                    W(gp, "gpsimd", self.s_dve, wp[f"yst_{idx}"])
                    W(gp, "gpsimd", self.s_ldc, wp["mask"])
                    W(gp, "gpsimd", self.s_zf, wp["zf"])
                    c = tau // 8
                    yview = self.yin[c].rearrange("a (s b) -> (a s) b", b=128)
                    kcol = (tau % 8) * 2 + h
                    self.s_yw[c].inc(gp.indirect_dma_start(
                        out=yview,
                        out_offset=IndirectOffsetOnAxis(si_sb[:, kcol:kcol + 1], 0),
                        in_=ystage[ysl][:], in_offset=None), 16)

                if "p2y" in self.debug:
                    @block.scalar
                    def _(sc):
                        W(sc, "scalar", self.s_dve, wp[f"yst_{idx}"])
                        self.s_yd.inc(sc.dma_start(
                            out=self.o_y[h, tau * 128:(tau + 1) * 128, :],
                            in_=ystage[ysl][:]), 16)
                ycount += 1

        if "p2y" in self.debug:
            self.o_y = nc.dram_tensor("o_y", [2, L, 128], BF16,
                                      kind="ExternalOutput").ap()

        ycount = 0
        nf = len(flat)
        emit_S(0)
        if nf > 1:
            emit_S(1)
        p = 0
        while p < nf:
            i0, i1 = p, (p + 1 if p + 1 < nf else None)
            emit_sd(i0)
            if i1 is not None:
                emit_sd(i1)

            @block.scalar
            def _(sc, i0=i0, i1=i1):
                emit_exp(sc, i0)
                if i1 is not None:
                    emit_exp(sc, i1)

            if p + 2 < nf:
                emit_S(p + 2)
            if p + 3 < nf:
                emit_S(p + 3)
            emit_chain(i0)
            if i1 is not None:
                emit_chain(i1)
            self.pump_pt(block, 3)
            p += 2
        self.finish_pt(block)

        nocc = self.nocc
        for c in range(8):
            @block.gpsimd
            def _(gp, c=c):
                W(gp, "gpsimd", self.s_yw[c], self.s_yw[c].c)
                if nocc:
                    gp.dma_start(out=self.rsout[c], in_=self.yin[c][0:128, :]
                                 ).then_inc(self.s_cc[c].h, 1, True)
                else:
                    gp.collective_compute(
                        "ReduceScatter", ALU.add, replica_groups=[list(range(8))],
                        ins=[self.yin[c]], outs=[self.rsout[c]],
                    ).then_inc(self.s_cc[c].h, 1)
                self.s_cc[c].c += 1

    def emit_consts(self, block, scope):
        nc, wp, W = self.nc, self.wp, self.W
        sb = lambda *a, **k: scope.enter_context(nc.sbuf_tensor(*a, **k))
        d = nc.dram_tensor
        self.i_sug = d("su_ug", [16, 128], F32, kind="ExternalInput").ap()
        self.i_sdp = d("su_dp", [64, 128], F32, kind="ExternalInput").ap()
        self.i_wug = d("wscale_ug", [128, 128], F32, kind="ExternalInput").ap()
        self.i_wdp = d("wscale_dp", [16, 128], F32, kind="ExternalInput").ap()
        self.i_h128 = d("h128", [128, 128], BF16, kind="ExternalInput").ap()

        self.sugT = sb("sugT", [128, 16], F32)
        self.sdpT = sb("sdpT", [128, 64], F32)
        self.wugT = sb("wugT", [128, 128], F32)
        self.wdpT = sb("wdpT", [128, 16], F32)
        self.h128_sb = sb("h128_sb", [128, 128], BF16)
        self.H1 = sb("H1", [128, 16 * 128], BF16)
        self.H2 = sb("H2", [128, 64 * 128], BF16)

        ld = sb("cstld", [128, 512], F32)
        with nc.psum_tensor("p_cst", [128, 128], F32) as p_cst:
            @block.sync
            def _(sy):
                self.s_ldc.inc(sy.dma_start(out=self.h128_sb[:], in_=self.i_h128), 16)
                self.s_ldc.inc(sy.dma_start(out=ld[0:16, 0:128], in_=self.i_sug), 16)
                self.s_ldc.inc(sy.dma_start(out=ld[0:64, 128:256],
                                            in_=self.i_sdp), 16)
                self.s_ldc.inc(sy.dma_start(out=ld[0:16, 256:384],
                                            in_=self.i_wdp), 16)
                self.s_ldc.inc(sy.dma_start(out=ld[0:128, 384:512],
                                            in_=self.i_wug), 16)
            wp["cst_ld"] = self.s_ldc.c

            idf = sb("idf", [128, 128], F32)

            @block.vector
            def _(v):
                W(v, "vector", self.s_ldc, wp["consts"])
                wp["idf"] = self.s_dve.inc(v.tensor_copy(idf[:], self.sident[:]))

            jobs = [
                (ld[0:16, 0:128], self.sugT, 16, 1.0),
                (ld[0:64, 128:256], self.sdpT, 64, 1.0),
                (ld[0:16, 256:384], self.wdpT, 16, float(1.0 / np.sqrt(8192.0))),
                (ld[0:128, 384:512], self.wugT, 128, float(1.0 / np.sqrt(2048.0))),
            ]
            for (src, dst, nrow, scl) in jobs:
                @block.tensor
                def _(t, src=src, nrow=nrow):
                    W(t, "tensor", self.s_ldc, wp["cst_ld"])
                    W(t, "tensor", self.s_dve, wp["idf"])
                    W(t, "tensor", self.s_dve, wp.get("cst_ev", 0))
                    wp["cst_tr"] = self.s_pe.inc(
                        t.transpose(p_cst[0:128, 0:nrow], src,
                                    idf[0:nrow, 0:nrow]))

                @block.vector
                def _(v, dst=dst, nrow=nrow, scl=scl):
                    W(v, "vector", self.s_pe, wp["cst_tr"])
                    wp["cst_ev"] = self.s_dve.inc(v.tensor_scalar_mul(
                        dst[:, 0:nrow], p_cst[:, 0:nrow], scl))
            wp["cst_done"] = self.s_dve.c

            @block.vector
            def _(v):
                W(v, "vector", self.s_dve, wp["cst_done"])
                W(v, "vector", self.s_ldc, wp["cst_ld"])
                for c in range(16):
                    v.tensor_scalar_mul(self.H1[:, c * 128:(c + 1) * 128],
                                        self.h128_sb[:], self.sugT[:, c:c + 1])
                for c in range(64):
                    last = v.tensor_scalar_mul(self.H2[:, c * 128:(c + 1) * 128],
                                               self.h128_sb[:], self.sdpT[:, c:c + 1])
                wp["Hmats"] = self.s_dve.inc(last)

    def emit_p3(self, block, scope):
        nc, wp, W = self.nc, self.wp, self.W
        sb = lambda *a, **k: scope.enter_context(nc.sbuf_tensor(*a, **k))
        ps = lambda *a, **k: scope.enter_context(nc.psum_tensor(*a, **k))
        d = nc.dram_tensor

        self.i_xres = d("xres", [1024, D], F32, kind="ExternalInput").ap()
        self.x2dram = d("x2dram", [1024, D], F32, kind="Internal").ap()

        woT = sb("woT", [128, NDC * D], BF16)
        ytm = [sb(f"ytm{i}", [128, D], BF16) for i in range(2)]
        yT = [sb(f"yT{i}", [128, D], BF16) for i in range(2)]
        xrs = [sb(f"xrs{i}", [128, D], F32) for i in range(2)]
        x2t = [sb(f"x2t{i}", [128, D], F32) for i in range(2)]
        zt = [sb(f"zt{i}", [128, D], BF16) for i in range(2)]
        zTt = [sb(f"zTt{i}", [128, D], BF16) for i in range(2)]
        s1 = sb("s1", [128, D], BF16)
        s2 = sb("s2", [128, D], BF16)
        st3 = sb("st3", [128, 8], F32)
        sq3 = sb("sq3", [128, D], BF16)

        p_tr3 = [ps(f"p_tr3{i}", [128, 512], BF16) for i in range(2)]
        p_wo = ps("p_wo", [128, D], F32)
        p_f1 = [ps(f"p_f1{i}", [128, 512], F32) for i in range(2)]

        @block.sync
        def _(sy):
            W(sy, "sync", self.s_cw, wp["wo16"])
            for dc in range(NDC):
                for q in range(4):
                    self.s_ldw.inc(sy.dma_start_transpose(
                        out=woT[:, dc * D + q * 512:dc * D + (q + 1) * 512],
                        in_=self.wo16[q * 512:(q + 1) * 512,
                                      dc * 128:(dc + 1) * 128]), 16)
        wp["woT"] = self.s_ldw.c

        u_ytm = {0: [], 1: []}
        u_yT = {0: [], 1: []}
        u_tr3 = {0: [], 1: []}
        u_xrs = {0: [], 1: []}
        u_x2t = {0: [], 1: []}
        u_zt = {0: [], 1: []}
        u_zTt = {0: [], 1: []}
        u_pwo = []
        u_f1 = {0: [], 1: []}
        u_s1 = []
        u_s2 = []

        for c in range(8):
            sl = c % 2
            @block.sync
            def _(sy, c=c, sl=sl):
                W(sy, "sync", self.s_cc[c], 1)
                for (sem, cnt) in u_ytm[sl]:
                    W(sy, "sync", sem, cnt)
                u_ytm[sl].clear()
                self.s_l3[sl].inc(sy.dma_start(out=ytm[sl][:], in_=self.rsout[c]), 16)
                for (sem, cnt) in u_xrs[sl]:
                    W(sy, "sync", sem, cnt)
                u_xrs[sl].clear()
                self.s_l3[sl].inc(sy.dma_start(
                    out=xrs[sl][:], in_=self.i_xres[c * 128:(c + 1) * 128, :]), 16)
                wp[f"l3_{c}"] = self.s_l3[sl].c

            for pb in range(4):
                @block.tensor
                def _(t, c=c, sl=sl, pb=pb):
                    W(t, "tensor", self.s_l3[sl], wp[f"l3_{c}"])
                    W(t, "tensor", self.s_dve, wp["Hmats"])
                    for (sem, cnt) in u_tr3[pb % 2]:
                        W(t, "tensor", sem, cnt)
                    u_tr3[pb % 2].clear()
                    if pb == 0:
                        for (sem, cnt) in u_yT[sl]:
                            W(t, "tensor", sem, cnt)
                        u_yT[sl].clear()
                    for j in range(4):
                        dc = pb * 4 + j
                        last = t.transpose(p_tr3[pb % 2][:, j * 128:(j + 1) * 128],
                                           ytm[sl][:, dc * 128:(dc + 1) * 128],
                                           self.sident[:])
                    cnt = self.s_pe.inc(last)
                    u_ytm[sl].append((self.s_pe, cnt))
                    wp[f"ytr_{c}_{pb}"] = cnt

                @block.vector
                def _(v, c=c, sl=sl, pb=pb):
                    W(v, "vector", self.s_pe, wp[f"ytr_{c}_{pb}"])
                    cnt = self.s_dve.inc(v.tensor_copy(
                        yT[sl][:, pb * 512:(pb + 1) * 512], p_tr3[pb % 2][:]))
                    u_tr3[pb % 2].append((self.s_dve, cnt))
                    wp[f"yev_{c}_{pb}"] = cnt

            @block.tensor
            def _(t, c=c, sl=sl):
                W(t, "tensor", self.s_ldw, wp["woT"])
                W(t, "tensor", self.s_dve, wp[f"yev_{c}_3"])
                for (sem, cnt) in u_pwo:
                    W(t, "tensor", sem, cnt)
                u_pwo.clear()
                for dc in range(NDC):
                    for ec in range(4):
                        last = t.matmul(
                            p_wo[:, ec * 512:(ec + 1) * 512],
                            yT[sl][:, dc * 128 + 0:dc * 128 + 128],
                            woT[:, dc * D + ec * 512:dc * D + (ec + 1) * 512],
                            start=(dc == 0), stop=(dc == NDC - 1))
                cnt = self.s_pe.inc(last)
                u_yT[sl].append((self.s_pe, cnt))
                wp[f"wo_{c}"] = cnt

            @block.vector
            def _(v, c=c, sl=sl):
                W(v, "vector", self.s_pe, wp[f"wo_{c}"])
                W(v, "vector", self.s_l3[sl], wp[f"l3_{c}"])
                for (sem, cnt) in u_x2t[sl]:
                    W(v, "vector", sem, cnt)
                u_x2t[sl].clear()
                cnt = self.s_dve.inc(v.tensor_tensor(
                    x2t[sl][:], xrs[sl][:], p_wo[:], ALU.add))
                u_pwo.append((self.s_dve, cnt))
                u_xrs[sl].append((self.s_dve, cnt))
                wp[f"x2_{c}"] = cnt

            @block.scalar
            def _(sc, c=c, sl=sl):
                W(sc, "scalar", self.s_dve, wp[f"x2_{c}"])
                cnt = self.s_x2[sl].inc(sc.dma_start(
                    out=self.x2dram[c * 128:(c + 1) * 128, :], in_=x2t[sl][:]), 16)
                u_x2t[sl].append((self.s_x2[sl], cnt))
                W(sc, "scalar", self.s_act, wp.get("sq3", 0))
                W(sc, "scalar", self.s_dve, wp.get(f"stats_rd_{c - 1}", 0))
                wp["sq3"] = self.s_act.inc(sc.activation(
                    sq3[:], x2t[sl][:], AF.Square,
                    accum_out=st3[:, 1:2]))

            @block.vector
            def _(v, c=c, sl=sl):
                W(v, "vector", self.s_dve, wp[f"x2_{c}"])
                W(v, "vector", self.s_act, wp.get(f"norm_{c - 1}", 0))
                c1 = self.s_dve.inc(v.reduce_sum(st3[:, 0:1], x2t[sl][:], axis=AX.X))
                W(v, "vector", self.s_dve, c1)
                c1 = self.s_dve.inc(v.tensor_scalar_mul(st3[:, 2:3], st3[:, 0:1],
                                                        1.0 / D))
                W(v, "vector", self.s_act, wp["sq3"])
                W(v, "vector", self.s_dve, c1)
                c1 = self.s_dve.inc(v.tensor_tensor(st3[:, 3:4], st3[:, 2:3],
                                                    st3[:, 2:3], ALU.mult))
                W(v, "vector", self.s_dve, c1)
                wp[f"var3_{c}"] = self.s_dve.inc(v.scalar_tensor_tensor(
                    st3[:, 3:4], st3[:, 1:2], 1.0 / D, st3[:, 3:4],
                    ALU.mult, ALU.subtract))

            @block.scalar
            def _(sc, c=c):
                W(sc, "scalar", self.s_dve, wp[f"var3_{c}"])
                wp[f"rstd3_{c}"] = self.s_act.inc(sc.activation(
                    st3[:, 4:5], st3[:, 3:4], AF.Sqrt, bias=self.ceps[:, 0:1]))

            @block.vector
            def _(v, c=c):
                W(v, "vector", self.s_act, wp[f"rstd3_{c}"])
                c1 = self.s_dve.inc(v.reciprocal(st3[:, 6:7], st3[:, 4:5]))
                W(v, "vector", self.s_dve, c1)
                c1 = self.s_dve.inc(v.tensor_copy(st3[:, 4:5], st3[:, 6:7]))
                W(v, "vector", self.s_dve, c1)
                wp[f"nmr3_{c}"] = self.s_dve.inc(v.scalar_tensor_tensor(
                    st3[:, 5:6], st3[:, 2:3], -1.0, st3[:, 4:5],
                    ALU.mult, ALU.mult))
                wp[f"stats_rd_{c}"] = self.s_dve.c

            @block.scalar
            def _(sc, c=c, sl=sl):
                W(sc, "scalar", self.s_dve, wp[f"nmr3_{c}"])
                for (sem, cnt) in u_zt[sl]:
                    W(sc, "scalar", sem, cnt)
                u_zt[sl].clear()
                wp[f"norm_{c}"] = self.s_act.inc(sc.activation(
                    zt[sl][:], x2t[sl][:], AF.Identity,
                    bias=st3[:, 5:6], scale=st3[:, 4:5]))

            for pb in range(4):
                @block.tensor
                def _(t, c=c, sl=sl, pb=pb):
                    W(t, "tensor", self.s_act, wp[f"norm_{c}"])
                    for (sem, cnt) in u_tr3[pb % 2]:
                        W(t, "tensor", sem, cnt)
                    u_tr3[pb % 2].clear()
                    if pb == 0:
                        for (sem, cnt) in u_zTt[sl]:
                            W(t, "tensor", sem, cnt)
                        u_zTt[sl].clear()
                    for j in range(4):
                        dc = pb * 4 + j
                        last = t.transpose(p_tr3[pb % 2][:, j * 128:(j + 1) * 128],
                                           zt[sl][:, dc * 128:(dc + 1) * 128],
                                           self.sident[:])
                    cnt = self.s_pe.inc(last)
                    u_zt[sl].append((self.s_pe, cnt))
                    wp[f"ztr_{c}_{pb}"] = cnt

                @block.vector
                def _(v, c=c, sl=sl, pb=pb):
                    W(v, "vector", self.s_pe, wp[f"ztr_{c}_{pb}"])
                    cnt = self.s_dve.inc(v.tensor_copy(
                        zTt[sl][:, pb * 512:(pb + 1) * 512], p_tr3[pb % 2][:]))
                    u_tr3[pb % 2].append((self.s_dve, cnt))
                    wp[f"zev_{c}_{pb}"] = cnt

            for pb in range(4):
                @block.tensor
                def _(t, c=c, sl=sl, pb=pb):
                    W(t, "tensor", self.s_dve, wp[f"zev_{c}_{pb}"])
                    W(t, "tensor", self.s_dve, self.wp["Hmats"])
                    for (sem, cnt) in u_f1[pb % 2]:
                        W(t, "tensor", sem, cnt)
                    u_f1[pb % 2].clear()
                    for j in range(4):
                        dc = pb * 4 + j
                        last = t.matmul(p_f1[pb % 2][:, j * 128:(j + 1) * 128],
                                        self.H1[:, dc * 128:(dc + 1) * 128],
                                        zTt[sl][:, dc * 128:(dc + 1) * 128],
                                        start=True, stop=True)
                    cnt = self.s_pe.inc(last)
                    u_zTt[sl].append((self.s_pe, cnt))
                    wp[f"f1_{c}_{pb}"] = cnt

                @block.vector
                def _(v, c=c, pb=pb):
                    W(v, "vector", self.s_pe, wp[f"f1_{c}_{pb}"])
                    if pb == 0:
                        for (sem, cnt) in u_s1:
                            W(v, "vector", sem, cnt)
                        u_s1.clear()
                    cnt = self.s_dve.inc(v.tensor_copy(
                        s1[:, pb * 512:(pb + 1) * 512], p_f1[pb % 2][:]))
                    u_f1[pb % 2].append((self.s_dve, cnt))
                    wp[f"f1ev_{c}"] = cnt

            def bf_level(eng, engname, src, dst, m, c, width=128, nch=16,
                         dst_slices=None):
                lastc = None
                for ch in range(nch):
                    a = ch & ~m
                    b_ = ch | m
                    sa = src[:, a * width:(a + 1) * width]
                    sbp = src[:, b_ * width:(b_ + 1) * width]
                    if dst_slices is None:
                        do = dst[:, ch * width:(ch + 1) * width]
                    else:
                        do = dst_slices(ch)
                    op = ALU.add if (ch & m) == 0 else ALU.subtract
                    if (ch & m) == 0:
                        lastc = eng.tensor_tensor(do, sa, sbp, ALU.add)
                    else:
                        lastc = eng.tensor_tensor(do, sa, sbp, ALU.subtract)
                return lastc

            @block.vector
            def _(v, c=c):
                W(v, "vector", self.s_dve, wp[f"f1ev_{c}"])
                for (sem, cnt) in u_s2:
                    W(v, "vector", sem, cnt)
                u_s2.clear()
                last = bf_level(v, "vector", s1, s2, 1, c)
                c1 = self.s_dve.inc(last)
                W(v, "vector", self.s_dve, c1)
                last = bf_level(v, "vector", s2, s1, 2, c)
                c1 = self.s_dve.inc(last)
                W(v, "vector", self.s_dve, c1)
                last = bf_level(v, "vector", s1, s2, 4, c)
                c1 = self.s_dve.inc(last)
                W(v, "vector", self.s_dve, c1)
                last = bf_level(
                    v, "vector", s2, None, 8, c,
                    dst_slices=lambda ch: self.zpT[:, ch * 1024 + c * 128:
                                                   ch * 1024 + c * 128 + 128])
                cnt = self.s_dve.inc(last)
                u_s1.append((self.s_dve, cnt))
                u_s2.append((self.s_dve, cnt))
                wp[f"zp_{c}"] = cnt
        wp["zpT_done"] = self.s_dve.c
        wp["p3_pe"] = self.s_pe.c
        wp["p3_act"] = self.s_act.c
        wp["x2_done0"] = self.s_x2[0].c
        wp["x2_done1"] = self.s_x2[1].c

    def emit_ffn_casts(self, block):
        nc, wp = self.nc, self.wp
        d = nc.dram_tensor
        self.i_wo = d("wo", [D, D], F32, kind="ExternalInput").ap()
        self.i_wup = d("wup", [INTER, D], F32, kind="ExternalInput").ap()
        self.i_wga = d("wga", [INTER, D], F32, kind="ExternalInput").ap()
        self.i_wdn = d("wdn", [D, INTER], F32, kind="ExternalInput").ap()
        self.wo16 = d("wo16", [D, D], BF16, kind="Internal").ap()
        self.wup16 = d("wup16", [INTER, D], BF16, kind="Internal").ap()
        self.wga16 = d("wga16", [INTER, D], BF16, kind="Internal").ap()
        self.wdn16 = d("wdn16", [D, INTER], BF16, kind="Internal").ap()

        @block.gpsimd
        def _(g):
            self.s_cw.inc(g.dma_start(out=self.wo16, in_=self.i_wo), 16)
            for k in range(4):
                self.s_cu.inc(g.dma_start(
                    out=self.wup16[k * 2048:(k + 1) * 2048, :],
                    in_=self.i_wup[k * 2048:(k + 1) * 2048, :]), 16)
            for k in range(4):
                self.s_cg.inc(g.dma_start(
                    out=self.wga16[k * 2048:(k + 1) * 2048, :],
                    in_=self.i_wga[k * 2048:(k + 1) * 2048, :]), 16)
            for k in range(4):
                self.s_cd.inc(g.dma_start(
                    out=self.wdn16[k * 512:(k + 1) * 512, :],
                    in_=self.i_wdn[k * 512:(k + 1) * 512, :]), 16)
        wp["wo16"] = self.s_cw.c
        wp["wup16"] = self.s_cu.c
        wp["wga16"] = self.s_cg.c
        wp["wdn16"] = self.s_cd.c

    def emit_pretranspose(self, block):
        nc, wp, W = self.nc, self.wp, self.W
        d = nc.dram_tensor
        self.wupT16 = d("wupT16", [D, INTER], BF16, kind="Internal").ap()
        self.wgaT16 = d("wgaT16", [D, INTER], BF16, kind="Internal").ap()
        self.wdnT16 = d("wdnT16", [INTER, D], BF16, kind="Internal").ap()
        jobs = []
        for i4 in range(16):
            for dc in range(16):
                jobs.append((self.wup16[i4 * 512:(i4 + 1) * 512,
                                        dc * 128:(dc + 1) * 128],
                             self.wupT16[dc * 128:(dc + 1) * 128,
                                         i4 * 512:(i4 + 1) * 512], self.s_cu))
        for i4 in range(16):
            for dc in range(16):
                jobs.append((self.wga16[i4 * 512:(i4 + 1) * 512,
                                        dc * 128:(dc + 1) * 128],
                             self.wgaT16[dc * 128:(dc + 1) * 128,
                                         i4 * 512:(i4 + 1) * 512], self.s_cg))
        for ic in range(64):
            for eg in range(4):
                jobs.append((self.wdn16[eg * 512:(eg + 1) * 512,
                                        ic * 128:(ic + 1) * 128],
                             self.wdnT16[ic * 128:(ic + 1) * 128,
                                         eg * 512:(eg + 1) * 512], self.s_cd))
        self._pt_jobs = jobs
        self._pt_k = 0

    def pump_pt(self, block, n):
        nc, wp, W = self.nc, self.wp, self.W
        jobs = self._pt_jobs[self._pt_k:self._pt_k + n]
        if not jobs:
            return
        k0 = self._pt_k
        self._pt_k += len(jobs)
        nslot = 4

        @block.sync
        def _(sy):
            for i, (src, dst, gate) in enumerate(jobs):
                k = k0 + i
                sl = k % nslot
                W(sy, "sync", gate, gate.c)
                W(sy, "sync", self.s_ps[sl], self.s_ps[sl].c)
                self.s_pt[sl].inc(sy.dma_start_transpose(
                    out=self.ptstg[sl][:], in_=src), 16)
                wp[f"pt_{k}"] = self.s_pt[sl].c

        @block.gpsimd
        def _(gp):
            for i, (src, dst, gate) in enumerate(jobs):
                k = k0 + i
                sl = k % nslot
                W(gp, "gpsimd", self.s_pt[sl], wp[f"pt_{k}"])
                self.s_ps[sl].inc(gp.dma_start(out=dst, in_=self.ptstg[sl][:]), 16)

    def finish_pt(self, block):
        self.pump_pt(block, len(self._pt_jobs) - self._pt_k)
        self.wp["ptdone"] = [(self.s_ps[i], self.s_ps[i].c) for i in range(4)]

    def emit_p4a(self, block):
        nc, wp, W = self.nc, self.wp, self.W
        d = nc.dram_tensor
        self.adram = d("adram", [INTER, 1024], BF16, kind="Internal").ap()
        self.out = d("out", [1024, D], F32, kind="ExternalOutput").ap()

        if self.preload:
            @block.scalar
            def _(sc):
                W(sc, "scalar", self.s_act, wp["p3_act"])
                sc.preload_act_table(AF.Silu)

        with contextlib.ExitStack() as s4a:
            sb = lambda *a, **k: s4a.enter_context(nc.sbuf_tensor(*a, **k))
            ps = lambda *a, **k: s4a.enter_context(nc.psum_tensor(*a, **k))
            wT = [sb(f"wT{i}", [128, NDC * 512], BF16) for i in range(4)]
            atile = [sb(f"atile{i}", [128, 1024], BF16) for i in range(2)]
            usil = sb("usil", [128, 1024], BF16)
            gsil = sb("gsil", [128, 1024], BF16)
            p_u = [ps(f"p_u{i}", [128, 1024], F32) for i in range(2)]
            p_g = [ps(f"p_g{i}", [128, 1024], F32) for i in range(2)]

            u_wT = {i: [] for i in range(4)}
            u_at = {0: [], 1: []}
            u_pu = {0: [], 1: []}
            u_pg = {0: [], 1: []}
            u_us = []
            u_gs = []

            for i4 in range(16):
                dsl = i4 % 2
                @block.sync
                def _(sy, i4=i4, dsl=dsl):
                    if i4 == 0:
                        for (sem, cnt) in wp["ptdone"]:
                            W(sy, "sync", sem, cnt)
                        W(sy, "sync", self.s_pe, wp["p3_pe"])
                        W(sy, "sync", self.s_dve, wp["zpT_done"])
                        W(sy, "sync", self.s_act, wp["p3_act"])
                    for (sem, cnt) in u_wT[dsl * 2] + u_wT[dsl * 2 + 1]:
                        W(sy, "sync", sem, cnt)
                    u_wT[dsl * 2].clear()
                    u_wT[dsl * 2 + 1].clear()
                    src = self.wupT16[:, i4 * 512:(i4 + 1) * 512]
                    src = src.rearrange("(dc p) i -> p dc i", p=128)
                    dst = wT[dsl * 2].ap().rearrange("p (dc i) -> p dc i", dc=NDC)
                    self.s_w4[dsl].inc(sy.dma_start(out=dst, in_=src), 16)
                    src = self.wgaT16[:, i4 * 512:(i4 + 1) * 512]
                    src = src.rearrange("(dc p) i -> p dc i", p=128)
                    dst = wT[dsl * 2 + 1].ap().rearrange("p (dc i) -> p dc i", dc=NDC)
                    self.s_w4[dsl].inc(sy.dma_start(out=dst, in_=src), 16)
                    wp[f"w4_{i4}"] = self.s_w4[dsl].c

                for it4 in range(4):
                    it = i4 * 4 + it4
                    psl = it % 2
                    @block.tensor
                    def _(t, i4=i4, it4=it4, it=it, psl=psl, dsl=dsl):
                        W(t, "tensor", self.s_w4[dsl], wp[f"w4_{i4}"])
                        W(t, "tensor", self.s_dve, wp["zpT_done"])
                        for (sem, cnt) in u_pu[psl]:
                            W(t, "tensor", sem, cnt)
                        u_pu[psl].clear()
                        for dc in range(NDC):
                            for tb in range(2):
                                last = t.matmul(
                                    p_u[psl][:, tb * 512:(tb + 1) * 512],
                                    wT[dsl * 2][:, dc * 512 + it4 * 128:
                                                dc * 512 + (it4 + 1) * 128],
                                    self.zpT[:, dc * 1024 + tb * 512:
                                             dc * 1024 + (tb + 1) * 512],
                                    start=(dc == 0),
                                    stop=(dc == NDC - 1))
                        cnt = self.s_pe.inc(last)
                        u_wT[dsl * 2].append((self.s_pe, cnt))
                        wp[f"umm_{it}"] = cnt
                        for (sem, cnt) in u_pg[psl]:
                            W(t, "tensor", sem, cnt)
                        u_pg[psl].clear()
                        for dc in range(NDC):
                            for tb in range(2):
                                last = t.matmul(
                                    p_g[psl][:, tb * 512:(tb + 1) * 512],
                                    wT[dsl * 2 + 1][:, dc * 512 + it4 * 128:
                                                    dc * 512 + (it4 + 1) * 128],
                                    self.zpT[:, dc * 1024 + tb * 512:
                                             dc * 1024 + (tb + 1) * 512],
                                    start=(dc == 0),
                                    stop=(dc == NDC - 1))
                        cnt = self.s_pe.inc(last)
                        u_wT[dsl * 2 + 1].append((self.s_pe, cnt))
                        wp[f"gmm_{it}"] = cnt

                    @block.scalar
                    def _(sc, it=it, psl=psl):
                        W(sc, "scalar", self.s_pe, wp[f"umm_{it}"])
                        W(sc, "scalar", self.s_dve, wp["cst_done"])
                        for (sem, cnt) in u_us:
                            W(sc, "scalar", sem, cnt)
                        u_us.clear()
                        c1 = self.s_act.inc(sc.activation(
                            usil[:], p_u[psl][:], AF.Identity,
                            scale=self.wugT[:, it:it + 1]))
                        u_pu[psl].append((self.s_act, c1))
                        W(sc, "scalar", self.s_pe, wp[f"gmm_{it}"])
                        for (sem, cnt) in u_gs:
                            W(sc, "scalar", sem, cnt)
                        u_gs.clear()
                        c2 = self.s_act.inc(sc.activation(
                            gsil[:], p_g[psl][:], AF.Silu,
                            scale=self.wugT[:, 64 + it:64 + it + 1]))
                        u_pg[psl].append((self.s_act, c2))
                        wp[f"sil_{it}"] = self.s_act.c

                    asl = it % 2
                    @block.vector
                    def _(v, it=it, asl=asl):
                        W(v, "vector", self.s_act, wp[f"sil_{it}"])
                        for (sem, cnt) in u_at[asl]:
                            W(v, "vector", sem, cnt)
                        u_at[asl].clear()
                        cnt = self.s_dve.inc(v.tensor_tensor(
                            atile[asl][:], usil[:], gsil[:], ALU.mult))
                        u_us.append((self.s_dve, cnt))
                        u_gs.append((self.s_dve, cnt))
                        wp[f"amul_{it}"] = cnt

                    @block.scalar
                    def _(sc, it=it, asl=asl):
                        W(sc, "scalar", self.s_dve, wp[f"amul_{it}"])
                        cnt = self.s_aw[asl].inc(sc.dma_start(
                            out=self.adram[it * 128:(it + 1) * 128, :],
                            in_=atile[asl][:]), 16)
                        u_at[asl].append((self.s_aw[asl], cnt))
            wp["adram_done0"] = self.s_aw[0].c
            wp["adram_done1"] = self.s_aw[1].c
            wp["p4a_act"] = self.s_act.c
            wp["p4a_pe"] = self.s_pe.c
            wp["p4a_dve"] = self.s_dve.c

    def emit_p4b(self, block, scope):
        nc, wp, W = self.nc, self.wp, self.W
        sb = lambda *a, **k: scope.enter_context(nc.sbuf_tensor(*a, **k))
        ps = lambda *a, **k: scope.enter_context(nc.psum_tensor(*a, **k))
        IN = sb("IN", [128, 64 * 256], BF16)
        SC = sb("SC", [128, 64 * 256], BF16)
        OB = sb("OB", [128, 64 * 256], BF16)
        OC = sb("OC", [128, 64 * 256], BF16)
        wdT = [sb(f"wdT{i}", [128, 1024], BF16) for i in range(2)]
        xoT = sb("xoT", [128, 16 * 256], BF16)
        x2r = [sb(f"x2r{i}", [128, D], F32) for i in range(2)]
        xfin = [sb(f"xfin{i}", [128, D], F32) for i in range(2)]
        p_a = [ps(f"p_a{i}", [128, 256], F32) for i in range(2)]
        p_dn = ps("p_dn", [128, 8 * 256], F32)
        p_ft = [ps(f"p_ft{i}", [128, 512], BF16) for i in range(2)]

        u_pa = {0: [], 1: []}
        u_pdn = []
        u_pft = {0: [], 1: []}
        u_wdT = {0: [], 1: []}
        u_xoT = []
        u_x2r = {0: [], 1: []}
        u_xfin = {0: [], 1: []}
        NDVE = 46

        def emit_stage(q):
            X = OB if q % 2 == 0 else OC
            for g8 in range(8):
                @block.sync
                def _(sy, q=q, g8=g8):
                    W(sy, "sync", self.s_aw[0], wp["adram_done0"])
                    W(sy, "sync", self.s_aw[1], wp["adram_done1"])
                    if q == 0 and g8 == 0:
                        W(sy, "sync", self.s_act, wp["p4a_act"])
                        W(sy, "sync", self.s_pe, wp["p4a_pe"])
                        W(sy, "sync", self.s_dve, wp["p4a_dve"])
                        W(sy, "sync", self.s_x2[0], wp["x2_done0"])
                        W(sy, "sync", self.s_x2[1], wp["x2_done1"])
                    if q > 0:
                        W(sy, "sync", self.s_pe, wp[f"fa_{q - 1}"])
                    src = self.adram[g8 * 1024:(g8 + 1) * 1024,
                                     q * 256:(q + 1) * 256]
                    src = src.rearrange("(c p) t -> p c t", p=128)
                    dst = IN[:, g8 * 8 * 256:(g8 + 1) * 8 * 256]
                    dst = dst.rearrange("p (c t) -> p c t", c=8)
                    self.s_ar[g8].inc(sy.dma_start(out=dst, in_=src), 16)
                    wp[f"ar_{q}_{g8}"] = self.s_ar[g8].c

            for ic in range(64):
                psl = ic % 2
                @block.tensor
                def _(t, q=q, ic=ic, psl=psl):
                    W(t, "tensor", self.s_ar[ic // 8], wp[f"ar_{q}_{ic // 8}"])
                    W(t, "tensor", self.s_dve, wp["Hmats"])
                    if ic == 0 and q >= 2:
                        W(t, "tensor", self.s_pe, wp[f"dn_{q - 2}_1"])
                        for (sem, cnt) in u_pdn:
                            W(t, "tensor", sem, cnt)
                    for (sem, cnt) in u_pa[psl]:
                        W(t, "tensor", sem, cnt)
                    u_pa[psl].clear()
                    wp[f"fa_{q}_{ic}"] = self.s_pe.inc(t.matmul(
                        p_a[psl][:], self.H2[:, ic * 128:(ic + 1) * 128],
                        IN[:, ic * 256:(ic + 1) * 256], start=True, stop=True))
                    if ic == 63:
                        wp[f"fa_{q}"] = self.s_pe.c

                @block.vector
                def _(v, q=q, ic=ic, psl=psl, X=X):
                    W(v, "vector", self.s_pe, wp[f"fa_{q}_{ic}"])
                    if ic == 0 and q >= 2:
                        W(v, "vector", self.s_pe, wp[f"dn_{q - 2}_1"])
                    cnt = self.s_dve.inc(v.tensor_copy(
                        X[:, ic * 256:(ic + 1) * 256], p_a[psl][:]))
                    u_pa[psl].append((self.s_dve, cnt))
                    wp[f"faev_{q}"] = cnt

        def emit_bf(q):
            X = OB if q % 2 == 0 else OC
            levels = [(X, SC, 1), (SC, X, 2), (X, SC, 4), (SC, X, 8),
                      (X, SC, 16), (SC, X, 32)]
            for li, (src, dst, m) in enumerate(levels):
                def emit_half(sec, en, chunks, q=q, li=li, src=src, dst=dst, m=m):
                    hold = {}
                    @sec
                    def _(e):
                        if li == 0:
                            W(e, en, self.s_dve, wp[f"faev_{q}"])
                            if q > 0:
                                for (psem, pcnt) in wp[f"bf_{q - 1}_5"]:
                                    W(e, en, psem, pcnt)
                        else:
                            for (psem, pcnt) in wp[f"bf_{q}_{li - 1}"]:
                                W(e, en, psem, pcnt)
                        lastc = None
                        for ch in chunks:
                            a_ = ch & ~m
                            b_ = ch | m
                            sa = src[:, a_ * 256:(a_ + 1) * 256]
                            sbp = src[:, b_ * 256:(b_ + 1) * 256]
                            do = dst[:, ch * 256:(ch + 1) * 256]
                            op = ALU.add if (ch & m) == 0 else ALU.subtract
                            lastc = e.tensor_tensor(do, sa, sbp, op)
                        if en == "gpsimd":
                            hold["r"] = (self.s_pool, self.s_pool.inc(lastc))
                        else:
                            hold["r"] = (self.s_dve, self.s_dve.inc(lastc))
                    return hold["r"]
                r1 = emit_half(block.vector, "vector", range(NDVE))
                r2 = emit_half(block.gpsimd, "gpsimd", range(NDVE, 64))
                wp[f"bf_{q}_{li}"] = [r1, r2]

        def emit_down(q):
            X = OB if q % 2 == 0 else OC
            for eh in range(2):
                for ic in range(64):
                    wsl = ic % 2
                    @block.sync
                    def _(sy, q=q, eh=eh, ic=ic, wsl=wsl):
                        W(sy, "sync", self.s_aw[0], wp["adram_done0"])
                        W(sy, "sync", self.s_aw[1], wp["adram_done1"])
                        W(sy, "sync", self.s_x2[0], wp["x2_done0"])
                        W(sy, "sync", self.s_x2[1], wp["x2_done1"])
                        W(sy, "sync", self.s_act, wp["p4a_act"])
                        W(sy, "sync", self.s_pe, wp["p4a_pe"])
                        W(sy, "sync", self.s_dve, wp["p4a_dve"])
                        for (sem, cnt) in u_wdT[wsl]:
                            W(sy, "sync", sem, cnt)
                        u_wdT[wsl].clear()
                        self.s_wd[wsl].inc(sy.dma_start(
                            out=wdT[wsl][:],
                            in_=self.wdnT16[ic * 128:(ic + 1) * 128,
                                            eh * 1024:(eh + 1) * 1024]), 16)
                        wp[f"wd_{q}_{eh}_{ic}"] = self.s_wd[wsl].c

                    @block.tensor
                    def _(t, q=q, eh=eh, ic=ic, wsl=wsl, X=X):
                        W(t, "tensor", self.s_wd[wsl], wp[f"wd_{q}_{eh}_{ic}"])
                        for (psem, pcnt) in wp[f"bf_{q}_5"]:
                            W(t, "tensor", psem, pcnt)
                        if ic == 0:
                            for (sem, cnt) in u_pdn:
                                W(t, "tensor", sem, cnt)
                            u_pdn.clear()
                        for et in range(8):
                            last = t.matmul(
                                p_dn[:, et * 256:(et + 1) * 256],
                                wdT[wsl][:, et * 128:(et + 1) * 128],
                                X[:, ic * 256:(ic + 1) * 256],
                                start=(ic == 0 and et % 2 == 0),
                                stop=(ic == 63 and et % 2 == 1))
                        cnt = self.s_pe.inc(last)
                        u_wdT[wsl].append((self.s_pe, cnt))
                        wp[f"dn_{q}_{eh}"] = cnt

                @block.scalar
                def _(sc, q=q, eh=eh):
                    W(sc, "scalar", self.s_pe, wp[f"dn_{q}_{eh}"])
                    if eh == 0:
                        for (sem, cnt) in u_xoT:
                            W(sc, "scalar", sem, cnt)
                        u_xoT.clear()
                    for et8 in range(8):
                        et = eh * 8 + et8
                        cnt = self.s_act.inc(sc.activation(
                            xoT[:, et * 256:(et + 1) * 256],
                            p_dn[:, et8 * 256:(et8 + 1) * 256],
                            AF.Identity, scale=self.wdpT[:, et:et + 1]))
                    u_pdn.append((self.s_act, cnt))
                    wp[f"xoev_{q}"] = cnt

            for tt in range(2):
                gt = q * 2 + tt
                xsl = gt % 2
                @block.sync
                def _(sy, gt=gt, xsl=xsl):
                    for (sem, cnt) in u_x2r[xsl]:
                        W(sy, "sync", sem, cnt)
                    u_x2r[xsl].clear()
                    W(sy, "sync", self.s_x2[0], wp["x2_done0"])
                    W(sy, "sync", self.s_x2[1], wp["x2_done1"])
                    self.s_x2r[xsl].inc(sy.dma_start(
                        out=x2r[xsl][:],
                        in_=self.x2dram[gt * 128:(gt + 1) * 128, :]), 16)
                    wp[f"x2r_{gt}"] = self.s_x2r[xsl].c

                for pb in range(4):
                    @block.tensor
                    def _(t, q=q, tt=tt, pb=pb, gt=gt):
                        W(t, "tensor", self.s_act, wp[f"xoev_{q}"])
                        for (sem, cnt) in u_pft[pb % 2]:
                            W(t, "tensor", sem, cnt)
                        u_pft[pb % 2].clear()
                        for j in range(4):
                            et = pb * 4 + j
                            last = t.transpose(
                                p_ft[pb % 2][:, j * 128:(j + 1) * 128],
                                xoT[:, et * 256 + tt * 128:
                                    et * 256 + tt * 128 + 128],
                                self.sident[:])
                        cnt = self.s_pe.inc(last)
                        u_xoT.append((self.s_pe, cnt))
                        wp[f"ftr_{gt}_{pb}"] = cnt

                    @block.vector
                    def _(v, gt=gt, pb=pb, xsl=xsl):
                        W(v, "vector", self.s_pe, wp[f"ftr_{gt}_{pb}"])
                        W(v, "vector", self.s_x2r[xsl], wp[f"x2r_{gt}"])
                        if pb == 0:
                            for (sem, cnt) in u_xfin[xsl]:
                                W(v, "vector", sem, cnt)
                            u_xfin[xsl].clear()
                        cnt = self.s_dve.inc(v.tensor_tensor(
                            xfin[xsl][:, pb * 512:(pb + 1) * 512],
                            x2r[xsl][:, pb * 512:(pb + 1) * 512],
                            p_ft[pb % 2][:], ALU.add))
                        u_pft[pb % 2].append((self.s_dve, cnt))
                        u_x2r[xsl].append((self.s_dve, cnt))
                        wp[f"fin_{gt}"] = cnt

                @block.scalar
                def _(sc, gt=gt, xsl=xsl):
                    W(sc, "scalar", self.s_dve, wp[f"fin_{gt}"])
                    cnt = self.s_so[xsl].inc(sc.dma_start(
                        out=self.out[gt * 128:(gt + 1) * 128, :],
                        in_=xfin[xsl][:]), 16)
                    u_xfin[xsl].append((self.s_so[xsl], cnt))

        emit_stage(0)
        emit_bf(0)
        for q in range(4):
            if q + 1 < 4:
                emit_stage(q + 1)
                emit_bf(q + 1)
            emit_down(q)

    def emit_p1_taps(self, block):
        nc = self.nc
        o_qT = nc.dram_tensor("o_qT", [2, DH, L], BF16, kind="ExternalOutput").ap()
        o_kT = nc.dram_tensor("o_kT", [2, DH, L], BF16, kind="ExternalOutput").ap()
        o_V = nc.dram_tensor("o_V", [2, 128, L], BF16, kind="ExternalOutput").ap()
        W = self.W
        last_hb = NG * 2 - 1
        for h in range(2):
            for (src, dst) in [(self.qT[h], o_qT), (self.kT[h], o_kT),
                               (self.V[h], o_V)]:
                @block.scalar
                def _(sc, h=h, src=src, dst=dst):
                    W(sc, "scalar", self.s_pool, self.rope_after[last_hb])
                    W(sc, "scalar", self.s_dve, self.v_after[last_hb])
                    self.s_st.inc(sc.dma_start(out=dst[h, :, :], in_=src[:]), 16)

    def emit_final_wait(self, block):
        @block.sync
        def _(s):
            s.wait_ge(self.s_st.h, self.s_st.c)
            for c in range(8):
                if self.s_cc[c].c:
                    s.wait_ge(self.s_cc[c].h, self.s_cc[c].c)
            for c in range(8):
                if self.s_yw[c].c:
                    s.wait_ge(self.s_yw[c].h, self.s_yw[c].c)
            for i in range(2):
                if self.s_so[i].c:
                    s.wait_ge(self.s_so[i].h, self.s_so[i].c)



def hadamard(n):
    H = np.array([[1.0]], dtype=np.float32)
    while H.shape[0] < n:
        H = np.block([[H, H], [H, -H]])
    return H.astype(np.float32)


def make_consts():
    inv = 1.0 / (10000.0 ** (np.arange(0, DH, 2, dtype=np.float64) / DH))
    ang = np.arange(L, dtype=np.float64)[:, None] * inv[None, :]
    ang2 = np.repeat(ang, 2, axis=1)
    cos_t = np.cos(ang2).T.astype(bf)
    sin_t = np.sin(ang2).T.astype(bf)
    pmT = np.zeros((128, 128), dtype=np.float32)
    for j in range(64):
        pmT[2 * j + 1, 2 * j] = -1.0
        pmT[2 * j, 2 * j + 1] = 1.0
    ident = np.eye(128, dtype=np.float32)
    return cos_t, sin_t, pmT.astype(bf), ident.astype(bf)


def make_mask():
    m = np.zeros((128, 128), dtype=np.float32)
    for k in range(128):
        m[k, 0:k] = -1e9
    return m


def wqkv_slice(W_qkv, core):
    h0 = 2 * core
    rows = []
    for sec in range(3):
        for h in (h0, h0 + 1):
            rows.append(W_qkv[sec * D + h * DH: sec * D + (h + 1) * DH, :])
    return np.ascontiguousarray(np.concatenate(rows, axis=0))


_CACHE = {}


def _get_nc():
    if "nc" not in _CACHE:
        _CACHE["nc"] = Builder().build()
    return _CACHE["nc"]


def kernel(L0_attn_x, L0_attn_W_qkv, L0_attn_W_o, L0_ffn_SU_ug, L0_ffn_SU_dp,
           L0_ffn_Wscale_ug, L0_ffn_Wscale_dp, L0_ffn_W_up, L0_ffn_W_gate,
           L0_ffn_W_down):
    x = np.ascontiguousarray(np.asarray(L0_attn_x, dtype=np.float32).reshape(L, D))
    W_qkv = np.ascontiguousarray(np.asarray(L0_attn_W_qkv, dtype=np.float32))
    Wo = np.ascontiguousarray(np.asarray(L0_attn_W_o, dtype=np.float32))
    SUug = np.asarray(L0_ffn_SU_ug, dtype=np.float32)
    SUdp = np.asarray(L0_ffn_SU_dp, dtype=np.float32)
    WSug = np.asarray(L0_ffn_Wscale_ug, dtype=np.float32)
    WSdp = np.asarray(L0_ffn_Wscale_dp, dtype=np.float32)
    Wup = np.ascontiguousarray(np.asarray(L0_ffn_W_up, dtype=np.float32))
    Wga = np.ascontiguousarray(np.asarray(L0_ffn_W_gate, dtype=np.float32))
    Wdn = np.ascontiguousarray(np.asarray(L0_ffn_W_down, dtype=np.float32))
    cos_t, sin_t, pmT, ident = make_consts()
    mask = make_mask()
    h128 = hadamard(128).astype(bf)

    nc = _get_nc()
    in_maps = []
    rows_of = {}
    for c in range(NCORES):
        idx = np.zeros((128, 16), dtype=np.int32)
        for t8 in range(8):
            for h in range(2):
                idx[:, t8 * 2 + h] = (t8 * 128 + np.arange(128)) * 16 + c * 2 + h
        myrows = np.concatenate([np.arange(cc * 1024 + c * 128,
                                           cc * 1024 + c * 128 + 128)
                                 for cc in range(8)])
        rows_of[c] = myrows
        in_maps.append({
            "x": x, "wqkv": wqkv_slice(W_qkv, c),
            "cos_t": cos_t, "sin_t": sin_t, "pmT": pmT, "ident": ident,
            "mask": mask, "sidx": idx,
            "xres": np.ascontiguousarray(x[myrows]),
            "wo": Wo, "wup": Wup, "wga": Wga, "wdn": Wdn,
            "wscale_ug": np.ascontiguousarray(WSug.reshape(128, 128)),
            "wscale_dp": np.ascontiguousarray(WSdp.reshape(16, 128)),
            "su_ug": np.ascontiguousarray(SUug.reshape(16, 128)),
            "su_dp": np.ascontiguousarray(SUdp.reshape(64, 128)),
            "h128": h128,
        })
    res = run_bass_kernel_spmd(nc, in_maps, core_ids=list(range(NCORES)))
    out = np.empty((L, D), dtype=np.float32)
    for c in range(NCORES):
        out[rows_of[c]] = np.asarray(res.results[c]["out"], dtype=np.float32)
    return out.reshape(1, L, D)


# version 14
# speedup vs baseline: 1.0524x; 1.0524x over previous
import contextlib
import numpy as np
import ml_dtypes
import concourse.bass as bass
import concourse.mybir as mybir
from concourse.bass_utils import run_bass_kernel_spmd

F32 = mybir.dt.float32
BF16 = mybir.dt.bfloat16
AF = mybir.ActivationFunctionType
ALU = mybir.AluOpType
AX = mybir.AxisListType
I32 = mybir.dt.int32
from concourse.bass_core_rust import IndirectOffsetOnAxis
bf = ml_dtypes.bfloat16

L = 8192
D = 2048
DH = 128
NCORES = 8
TPC = L // NCORES
NT = L // 128
NG = 8
NDC = D // 128
INTER = 8192
EPS = 1e-5
CSC = float(1.0 / np.sqrt(DH))
B0 = 8.0


class Sem:
    def __init__(self, nc, es, name):
        self.h = es.enter_context(nc.semaphore(name))
        self.c = 0
        self.name = name

    def inc(self, instr, n=1):
        instr.then_inc(self.h, n)
        self.c += n
        return self.c


ENGN = {"sync": 0, "gpsimd": 1, "vector": 2, "tensor": 3, "scalar": 4}


class Builder:
    def __init__(self, debug_taps=(), stage="full", nocc=False, preload=False):
        self.preload = preload
        self.nocc = nocc
        self.stage = stage
        self.debug = set(debug_taps)
        nc = self.nc = bass.Bass("TRN2", target_bir_lowering=False, debug=False)
        self.es = contextlib.ExitStack()
        self.seen = {e: {} for e in ENGN}

        d = nc.dram_tensor
        self.x = d("x", [L, D], F32, kind="ExternalInput").ap()
        self.wqkv = d("wqkv", [768, D], F32, kind="ExternalInput").ap()
        self.cos_t = d("cos_t", [DH, L], BF16, kind="ExternalInput").ap()
        self.sin_t = d("sin_t", [DH, L], BF16, kind="ExternalInput").ap()
        self.pmT = d("pmT", [128, 128], BF16, kind="ExternalInput").ap()
        self.ident = d("ident", [128, 128], BF16, kind="ExternalInput").ap()

        self.wqkv16 = d("wqkv16", [768, D], BF16, kind="Internal").ap()

        a = nc.alloc_sbuf_tensor
        self.sident = a("sident", [128, 128], BF16, side="right")
        self.ceps = a("ceps", [128, 1], F32, side="right")
        self.cnegb0 = a("cnegb0", [128, 1], F32, side="right")
        self.ptstg = [a(f"ptstg{i}", [128, 512], BF16, side="right")
                      for i in range(4)]

        es, S = self.es, lambda n: Sem(nc, es, n)
        self.s_ldx = [S(f"s_ldx{i}") for i in range(4)]
        self.s_cs = [S(f"s_cs{i}") for i in range(2)]
        self.s_ldc = S("s_ldc")
        self.s_cast = S("s_cast")
        self.s_ldw = S("s_ldw")
        self.s_act = S("s_act")
        self.s_dve = S("s_dve")
        self.s_pe = S("s_pe")
        self.s_pool = S("s_pool")
        self.s_st = S("s_st")
        self.s_yw = [S(f"s_yw{c}") for c in range(8)]
        self.s_zf = S("s_zf")
        self.s_yd = S("s_yd")
        self.s_cc = [S(f"s_cc{c}") for c in range(8)]
        self.s_cw = S("s_cw")
        self.s_cu = S("s_cu")
        self.s_cg = S("s_cg")
        self.s_cd = S("s_cd")
        self.s_l3 = [S(f"s_l3{i}") for i in range(2)]
        self.s_x2 = [S(f"s_x2_{i}") for i in range(2)]
        self.s_w4 = [S(f"s_w4{i}") for i in range(2)]
        self.s_aw = [S(f"s_aw{i}") for i in range(2)]
        self.s_ar = [S(f"s_ar{i}") for i in range(8)]
        self.s_wd = [S(f"s_wd{i}") for i in range(2)]
        self.s_x2r = [S(f"s_x2r{i}") for i in range(2)]
        self.s_so = [S(f"s_so{i}") for i in range(2)]
        self.s_pt = [S(f"s_pt{i}") for i in range(4)]
        self.s_ps = [S(f"s_ps{i}") for i in range(4)]
        self.wp = {}

    def load_act_set(self, sc, name):
        from concourse.hw_specs import get_activation_tables
        for idx, nm in enumerate(get_activation_tables(self.nc.m.arch)):
            if nm == name:
                return sc.add_instruction(mybir.InstLoadActFuncSet(
                    name=self.nc.get_next_instruction_name(),
                    act_func_set_id=idx, ins=[], outs=[]))
        raise ValueError(name)

    def W(self, eng, engname, sem, val):
        if val is None or val <= 0:
            return
        if self.seen[engname].get(sem.name, 0) >= val:
            return
        eng.wait_ge(sem.h, val)
        self.seen[engname][sem.name] = val

    def build(self):
        nc = self.nc
        with self.es, nc.Block() as block:
            with contextlib.ExitStack() as qkv_scope:
                sbq = lambda *a, **k: qkv_scope.enter_context(nc.sbuf_tensor(*a, **k))
                self.qT = [sbq(f"qT{h}", [128, L], BF16) for h in range(2)]
                self.kT = [sbq(f"kT{h}", [128, L], BF16) for h in range(2)]
                self.V = [sbq(f"V{h}", [128, NT * 132], BF16) for h in range(2)]
                with contextlib.ExitStack() as p1_scope:
                    with nc.named_scope("p1"):
                        self.emit_p1(block, p1_scope)
                    if "p1" in self.debug and self.stage == "p1":
                        self.emit_p1_taps(block)
                    if self.stage == "ln":
                        self.emit_ln_taps(block)
                if self.stage in ("p2", "p3", "full"):
                    self.phase_barrier(block)
                    with contextlib.ExitStack() as p2_scope:
                        with nc.named_scope("p2"):
                            self.emit_p2(block, p2_scope)
            if self.stage in ("p3", "full"):
                self.phase_barrier(block)
                with contextlib.ExitStack() as s34:
                    with nc.named_scope("p3"):
                        self.emit_consts(block, s34)
                    with contextlib.ExitStack() as zscope:
                        self.zpT = zscope.enter_context(
                            nc.sbuf_tensor("zpT", [128, NDC * 1024], BF16))
                        with nc.named_scope("p3b"):
                            with contextlib.ExitStack() as p3_scope:
                                self.emit_p3(block, p3_scope)
                        if self.stage == "p3":
                            self.emit_p3_taps(block)
                        else:
                            with nc.named_scope("p4"):
                                self.emit_p4a(block)
                    if self.stage != "p3":
                        with contextlib.ExitStack() as p4_scope:
                            with nc.named_scope("p4b"):
                                self.emit_p4b(block, p4_scope)
            self.emit_final_wait(block)
        return nc

    def emit_p3_taps(self, block):
        nc, W = self.nc, self.W
        o_x2 = nc.dram_tensor("o_x2", [1024, D], F32, kind="ExternalOutput").ap()
        o_zpT = nc.dram_tensor("o_zpT", [128, NDC * 1024], BF16,
                               kind="ExternalOutput").ap()
        cvt3 = nc.alloc_sbuf_tensor("cvt3", [128, D], F32)

        @block.sync
        def _(sy):
            W(sy, "sync", self.s_x2[0], self.wp["x2_done0"])
            W(sy, "sync", self.s_x2[1], self.wp["x2_done1"])
            W(sy, "sync", self.s_dve, self.wp["zpT_done"])
            for c in range(8):
                self.s_st.inc(sy.dma_start(
                    out=o_x2[c * 128:(c + 1) * 128, :],
                    in_=self.x2dram[c * 128:(c + 1) * 128, :]), 16)
            self.s_st.inc(sy.dma_start(out=o_zpT, in_=self.zpT[:]), 16)

    def phase_barrier(self, block):
        W = self.W
        marks = [(self.s_pe, self.s_pe.c), (self.s_dve, self.s_dve.c),
                 (self.s_act, self.s_act.c), (self.s_pool, self.s_pool.c),
                 (self.s_st, self.s_st.c), (self.s_cast, self.s_cast.c),
                 (self.s_ldw, self.s_ldw.c), (self.s_zf, self.s_zf.c),
                 (self.s_yd, self.s_yd.c)]
        marks += [(sw, sw.c) for sw in self.s_yw]
        marks = [(sem, cnt) for (sem, cnt) in marks if cnt > 0]
        for engname, sec in (("sync", block.sync), ("gpsimd", block.gpsimd),
                             ("vector", block.vector), ("tensor", block.tensor),
                             ("scalar", block.scalar)):
            @sec
            def _(e, engname=engname):
                for sem, cnt in marks:
                    W(e, engname, sem, cnt)

    def emit_ln_taps(self, block):
        nc = self.nc
        o_lnxT = nc.dram_tensor("o_lnxT", [128, NDC * 1024], BF16,
                                kind="ExternalOutput").ap()
        @block.scalar
        def _(sc):
            self.W(sc, "scalar", self.s_dve, self.wp_trev_last)
            self.s_st.inc(sc.dma_start(out=o_lnxT, in_=self.lnxT[:]), 16)

    def emit_p1(self, block, scope):
        wp = self.wp
        W = self.W
        nc = self.nc
        sb = lambda *args, **kw: scope.enter_context(nc.sbuf_tensor(*args, **kw))
        ps = lambda *args, **kw: scope.enter_context(nc.psum_tensor(*args, **kw))
        self.xr = sb("xr", [128, 3 * D], F32)
        self.lnx = sb("lnx", [128, 2 * D], BF16)
        self.lnxT2 = [sb(f"lnxT{i}", [128, NDC * 512], BF16) for i in range(2)]
        self.wqkvT = sb("wqkvT", [128, NDC * 768], BF16)
        self.cosb = sb("cosb", [128, 2 * 512], BF16)
        self.sinb = sb("sinb", [128, 2 * 512], BF16)
        self.sums = sb("sums", [128, 8], F32)
        self.sumsq = sb("sumsq", [128, 8], F32)
        self.stats = sb("stats", [128, 2 * 16], F32)
        self.sqscrap = sb("sqscrap", [128, D], BF16)
        self.qc = [sb(f"qc{i}", [128, 512], BF16) for i in range(2)]
        self.rs = [sb(f"rs{i}", [128, 512], BF16) for i in range(2)]
        self.roplain = [sb(f"roplain{i}", [128, 512], BF16) for i in range(2)]
        self.spmT = sb("spmT", [128, 128], BF16)
        self.p_tr = [ps(f"p_tr{i}", [128, 512], BF16) for i in range(2)]
        self.p_trv = ps("p_trv", [128, 512], BF16)

        self.p_qkv = [ps(f"p_qkv{i}", [128, 512], F32) for i in range(3)]
        self.p_rot = ps("p_rot", [128, 512], F32)

        if self.preload:
            @block.scalar
            def _(sc):
                sc.preload_act_table(AF.Sqrt)

        @block.gpsimd
        def _(g):
            self.s_cast.inc(g.dma_start(out=self.wqkv16, in_=self.wqkv), 16)
        wp["wqkv16"] = self.s_cast.c

        @block.sync
        def _(s):
            self.s_ldc.inc(s.dma_start(out=self.sident[:], in_=self.ident), 16)
            self.s_ldc.inc(s.dma_start(out=self.spmT[:], in_=self.pmT), 16)
            W(s, "sync", self.s_cast, wp["wqkv16"])
            for et in range(6):
                buf = self.lnxT2[0] if et < 4 else self.lnxT2[1]
                off = (et if et < 4 else et - 4) * D
                self.s_ldw.inc(s.dma_start(
                    out=buf[:, off:off + D],
                    in_=self.wqkv16[et * 128:(et + 1) * 128, :]), 16)
        wp["wq_st"] = self.s_ldw.c
        wp["consts"] = self.s_ldc.c

        for et in range(6):
            for dc2 in range(4):
                @block.tensor
                def _(t, et=et, dc2=dc2):
                    W(t, "tensor", self.s_ldw, wp["wq_st"])
                    W(t, "tensor", self.s_ldc, wp["consts"])
                    k = f"wqev_{et * 4 + dc2 - 2}"
                    if k in wp:
                        W(t, "tensor", self.s_dve, wp[k])
                    buf = self.lnxT2[0] if et < 4 else self.lnxT2[1]
                    off = (et if et < 4 else et - 4) * D
                    for j in range(4):
                        dc = dc2 * 4 + j
                        last = t.transpose(
                            self.p_tr[dc2 % 2][:, j * 128:(j + 1) * 128],
                            buf[:, off + dc * 128:off + (dc + 1) * 128],
                            self.sident[:])
                    self.s_pe.inc(last)
                    wp[f"wqtr_{et * 4 + dc2}"] = self.s_pe.c

                @block.vector
                def _(v, et=et, dc2=dc2):
                    W(v, "vector", self.s_pe, wp[f"wqtr_{et * 4 + dc2}"])
                    out_ap = self.wqkvT.ap().rearrange(
                        "p (dc e) -> p dc e", e=768)[
                        :, dc2 * 4:(dc2 + 1) * 4, et * 128:(et + 1) * 128]
                    in_ap = self.p_tr[dc2 % 2].ap().rearrange(
                        "p (a b) -> p a b", a=4)
                    wp[f"wqev_{et * 4 + dc2}"] = self.s_dve.inc(
                        v.tensor_copy(out_ap, in_ap))
        wp["wqkvT"] = self.s_dve.c
        wp["wq_pe"] = self.s_pe.c
        wp["wqlast0"] = wp["wqev_22"]
        wp["wqlast1"] = wp["wqev_23"]

        @block.vector
        def _(v):
            v.memset(self.ceps[:], EPS)
            v.memset(self.cnegb0[:], -B0)
            for h in range(2):
                ones = self.V[h].ap().rearrange("p (t c) -> p t c", c=132)[:, :, 128:129]
                v.memset(ones, 1.0)
            self.s_dve.inc(v.engine_nop())
        self.wp["vones"] = self.s_dve.c

        ldx_after = {}
        norm_after = {}
        sum_after = {}
        sq_after = {}
        trev_after = {}
        cos_after = {}
        self.rope_after = {}
        self.v_after = {}
        tr_slot_user = {0: wp["wqlast0"], 1: wp["wqlast1"]}
        pqkv_users = {0: [], 1: [], 2: []}
        roplain_users = {0: [], 1: []}
        qcrs_users = {0: [], 1: []}
        stats_rd = {0: None, 1: None}
        stats_rd_prev = {0: None, 1: None}
        sumsq_rd = {0: None, 1: None}
        p_rot_user = None
        p_trv_user = None

        for g in range(NG):
            for tau in range(8):
                T = g * 8 + tau

                @block.sync
                def _(s, T=T):
                    if T >= 3:
                        W(s, "sync", self.s_act, norm_after[T - 3])
                    self.s_ldx[T % 4].inc(s.dma_start(
                        out=self.xr[0:64, (T % 3) * D:(T % 3 + 1) * D],
                        in_=self.x[T * 128:T * 128 + 64, :]), 16)

                @block.scalar
                def _(sc, T=T):
                    if T >= 3:
                        W(sc, "scalar", self.s_act, norm_after[T - 3])
                    ldx_after[T] = self.s_ldx[T % 4].inc(sc.dma_start(
                        out=self.xr[64:128, (T % 3) * D:(T % 3 + 1) * D],
                        in_=self.x[T * 128 + 64:(T + 1) * 128, :]), 16)

                @block.vector
                def _(v, T=T, tau=tau):
                    W(v, "vector", self.s_ldx[T % 4], ldx_after[T])
                    sum_after[T] = self.s_dve.inc(v.reduce_sum(
                        self.sums[:, tau:tau + 1],
                        self.xr[:, (T % 3) * D:(T % 3 + 1) * D], axis=AX.X))

                if tau % 2 != 1:
                    continue
                b = (tau // 2) % 2
                bt = tau - 1
                cols = slice(bt, bt + 2)
                mean = self.stats[:, 8 * b + 0:8 * b + 2]
                var = self.stats[:, 8 * b + 2:8 * b + 4]
                rstd = self.stats[:, 8 * b + 4:8 * b + 6]
                negmr = self.stats[:, 8 * b + 6:8 * b + 8]

                var_cnt = self.s_dve.c + 3
                negmr_cnt = self.s_dve.c + 6

                @block.scalar
                def _(sc, g=g, b=b, bt=bt, var=var, rstd=rstd, negmr=negmr,
                      var_cnt=var_cnt, negmr_cnt=negmr_cnt):
                    for tt in range(2):
                        TT = g * 8 + bt + tt
                        W(sc, "scalar", self.s_ldx[TT % 4], ldx_after[TT])
                        if TT > 0:
                            W(sc, "scalar", self.s_act, sq_after[TT - 1])
                        if sumsq_rd[b] is not None:
                            W(sc, "scalar", self.s_dve, sumsq_rd[b])
                        sq_after[TT] = self.s_act.inc(sc.activation(
                            self.sqscrap[:],
                            self.xr[:, (TT % 3) * D:(TT % 3 + 1) * D],
                            AF.Square,
                            accum_out=self.sumsq[:, bt + tt:bt + tt + 1]))
                    W(sc, "scalar", self.s_dve, var_cnt)
                    self.s_act.inc(sc.activation(rstd, var, AF.Sqrt,
                                                 bias=self.ceps[:, 0:1]))
                    wp[f"rstd_{g}_{tau}"] = self.s_act.c
                    W(sc, "scalar", self.s_dve, negmr_cnt)
                    for tt in range(2):
                        TT = g * 8 + bt + tt
                        if TT >= 2:
                            W(sc, "scalar", self.s_dve, trev_after[TT - 2])
                        norm_after[TT] = self.s_act.inc(sc.activation(
                            self.lnx[:, (TT % 2) * D:(TT % 2 + 1) * D],
                            self.xr[:, (TT % 3) * D:(TT % 3 + 1) * D],
                            AF.Identity, bias=negmr[:, tt:tt + 1],
                            scale=rstd[:, tt:tt + 1]))
                stats_rd[b] = self.s_act.c

                @block.vector
                def _(v, g=g, b=b, T=T, cols=cols, mean=mean, var=var):
                    W(v, "vector", self.s_dve, sum_after[T])
                    W(v, "vector", self.s_act, sq_after[T])
                    if stats_rd_prev[b] is not None:
                        W(v, "vector", self.s_act, stats_rd_prev[b])
                    c1 = self.s_dve.inc(v.tensor_scalar_mul(mean, self.sums[:, cols], 1.0 / D))
                    W(v, "vector", self.s_dve, c1)
                    c2 = self.s_dve.inc(v.tensor_tensor(self.sums[:, cols], mean, mean, ALU.mult))
                    W(v, "vector", self.s_dve, c2)
                    sumsq_rd[b] = self.s_dve.inc(v.scalar_tensor_tensor(
                        var, self.sumsq[:, cols], 1.0 / D, self.sums[:, cols],
                        ALU.mult, ALU.subtract))
                assert self.s_dve.c == var_cnt

                @block.vector
                def _(v, g=g, b=b, var=var, mean=mean, rstd=rstd, negmr=negmr):
                    W(v, "vector", self.s_act, wp[f"rstd_{g}_{tau}"])
                    c1 = self.s_dve.inc(v.reciprocal(var, rstd))
                    W(v, "vector", self.s_dve, c1)
                    c1 = self.s_dve.inc(v.tensor_copy(rstd, var))
                    W(v, "vector", self.s_dve, c1)
                    self.s_dve.inc(v.scalar_tensor_tensor(
                        negmr, mean, -1.0, rstd, ALU.mult, ALU.mult))
                assert self.s_dve.c == negmr_cnt
                stats_rd_prev[b] = stats_rd[b]

                for tt in range(2):
                    TT = g * 8 + bt + tt
                    tau2 = bt + tt

                    for pb in range(4):
                        @block.tensor
                        def _(t, TT=TT, pb=pb):
                            W(t, "tensor", self.s_ldc, wp["consts"])
                            W(t, "tensor", self.s_act, norm_after[TT])
                            u = tr_slot_user[pb % 2]
                            if u is not None:
                                W(t, "tensor", self.s_dve, u)
                            for j in range(4):
                                dc = pb * 4 + j
                                last = t.transpose(
                                    self.p_tr[pb % 2][:, j * 128:(j + 1) * 128],
                                    self.lnx[:, (TT % 2) * D + dc * 128:
                                             (TT % 2) * D + (dc + 1) * 128],
                                    self.sident[:])
                            self.s_pe.inc(last)
                            self.wp[f"tr_{TT}_{pb}"] = self.s_pe.c

                        @block.vector
                        def _(v, g=g, TT=TT, tau2=tau2, pb=pb):
                            W(v, "vector", self.s_pe, self.wp["wq_pe"])
                            W(v, "vector", self.s_pe, self.wp[f"tr_{TT}_{pb}"])
                            hb2 = g * 2 + tau2 // 4
                            if f"qkvmm_{hb2 - 2}_5" in self.wp:
                                W(v, "vector", self.s_pe,
                                  self.wp[f"qkvmm_{hb2 - 2}_5"])
                            out_ap = self.lnxT2[hb2 % 2].ap().rearrange(
                                "p (dc t) -> p dc t", dc=NDC)[
                                :, pb * 4:(pb + 1) * 4,
                                (tau2 % 4) * 128:(tau2 % 4 + 1) * 128]
                            in_ap = self.p_tr[pb % 2].ap().rearrange(
                                "p (a b) -> p a b", a=4)
                            tr_slot_user[pb % 2] = self.s_dve.inc(
                                v.tensor_copy(out_ap, in_ap))
                            trev_after[TT] = self.s_dve.c
                            self.wp_trev_last = self.s_dve.c

            if g == 3 and self.stage in ("p3", "full"):
                self.emit_ffn_casts(block)
                self.emit_pretranspose(block)

            for b in (range(2) if self.stage != "ln" else range(0)):
                hb = g * 2 + b
                tok0 = g * 1024 + b * 512

                @block.sync
                def _(s, hb=hb, tok0=tok0):
                    if hb >= 2:
                        W(s, "sync", self.s_pool, self.rope_after.get(hb - 2))
                        W(s, "sync", self.s_dve, self.wp.get(f"qcrs_done_{hb - 2}"))
                    sl = (hb % 2) * 512
                    self.s_cs[hb % 2].inc(s.dma_start(out=self.cosb[:, sl:sl + 512],
                                                      in_=self.cos_t[:, tok0:tok0 + 512]), 16)
                    cos_after[hb] = self.s_cs[hb % 2].inc(
                        s.dma_start(out=self.sinb[:, sl:sl + 512],
                                    in_=self.sin_t[:, tok0:tok0 + 512]), 16)

                def rope_post(et):
                    slot = et % 3
                    rsl = et % 2
                    sec, h = et // 2, et % 2
                    if sec < 2:
                        dstT = (self.qT if sec == 0 else self.kT)[h]

                        @block.vector
                        def _(v, hb=hb, et=et, slot=slot, rsl=rsl):
                            W(v, "vector", self.s_pe, wp[f"qkvmm_{hb}_{et}"])
                            for (sem, cnt) in roplain_users[rsl]:
                                W(v, "vector", sem, cnt)
                            roplain_users[rsl] = []
                            cnt = self.s_dve.inc(v.tensor_copy(
                                self.roplain[rsl][:], self.p_qkv[slot][:]))
                            wp[f"qkvevac_{hb}_{et}"] = cnt

                        @block.tensor
                        def _(t, hb=hb, et=et, slot=slot, rsl=rsl):
                            nonlocal p_rot_user
                            W(t, "tensor", self.s_dve, wp[f"qkvevac_{hb}_{et}"])
                            if p_rot_user is not None:
                                W(t, "tensor", self.s_dve, p_rot_user)
                            mm = t.matmul(self.p_rot[:], self.spmT[:],
                                          self.roplain[rsl][:], start=True, stop=True)
                            cnt = self.s_pe.inc(mm)
                            roplain_users[rsl].append((self.s_pe, cnt))
                            wp[f"rot_{hb}_{et}"] = cnt

                        @block.vector
                        def _(v, hb=hb, et=et, slot=slot, rsl=rsl):
                            nonlocal p_rot_user
                            sl = (hb % 2) * 512
                            W(v, "vector", self.s_cs[hb % 2], cos_after[hb])
                            for (sem, cnt) in qcrs_users[rsl]:
                                W(v, "vector", sem, cnt)
                            qcrs_users[rsl] = []
                            cnt = self.s_dve.inc(v.tensor_tensor(
                                self.qc[rsl][:], self.p_qkv[slot][:],
                                self.cosb[:, sl:sl + 512], ALU.mult))
                            pqkv_users[slot].append((self.s_dve, cnt))
                            W(v, "vector", self.s_pe, wp[f"rot_{hb}_{et}"])
                            cnt = self.s_dve.inc(v.tensor_tensor(
                                self.rs[rsl][:], self.p_rot[:],
                                self.sinb[:, sl:sl + 512], ALU.mult))
                            p_rot_user = cnt
                            wp[f"rs_{hb}_{et}"] = cnt
                            wp[f"qcrs_done_{hb}"] = cnt

                        @block.gpsimd
                        def _(gp, hb=hb, et=et, rsl=rsl, dstT=dstT, tok0=tok0):
                            W(gp, "gpsimd", self.s_dve, wp[f"rs_{hb}_{et}"])
                            cnt = self.s_pool.inc(gp.tensor_tensor(
                                dstT[:, tok0:tok0 + 512], self.qc[rsl][:],
                                self.rs[rsl][:], ALU.add))
                            qcrs_users[rsl].append((self.s_pool, cnt))
                            self.rope_after[hb] = cnt
                    else:
                        h = et % 2

                        @block.vector
                        def _(v, hb=hb, et=et, slot=slot, rsl=rsl):
                            W(v, "vector", self.s_pe, wp[f"qkvmm_{hb}_{et}"])
                            for (sem, cnt) in roplain_users[rsl]:
                                W(v, "vector", sem, cnt)
                            roplain_users[rsl] = []
                            cnt = self.s_dve.inc(v.tensor_copy(
                                self.roplain[rsl][:], self.p_qkv[slot][:]))
                            pqkv_users[slot].append((self.s_dve, cnt))
                            wp[f"qkvevac_{hb}_{et}"] = cnt

                        @block.tensor
                        def _(t, hb=hb, et=et, rsl=rsl):
                            nonlocal p_trv_user
                            W(t, "tensor", self.s_dve, wp[f"qkvevac_{hb}_{et}"])
                            if p_trv_user is not None:
                                W(t, "tensor", self.s_dve, p_trv_user)
                            for j in range(4):
                                last = t.transpose(self.p_trv[:, j * 128:(j + 1) * 128],
                                                   self.roplain[rsl][:, j * 128:(j + 1) * 128],
                                                   self.sident[:])
                            cnt = self.s_pe.inc(last)
                            roplain_users[rsl].append((self.s_pe, cnt))
                            wp[f"vtr_{hb}_{et}"] = cnt

                        @block.vector
                        def _(v, hb=hb, et=et, h=h, tok0=tok0):
                            nonlocal p_trv_user
                            W(v, "vector", self.s_pe, wp[f"vtr_{hb}_{et}"])
                            t0 = tok0 // 128
                            dst = self.V[h].ap().rearrange(
                                "p (t c) -> p t c", c=132)[:, t0:t0 + 4, 0:128]
                            src = self.p_trv.ap().rearrange("p (a b) -> p a b", a=4)
                            p_trv_user = self.s_dve.inc(v.tensor_copy(dst, src))
                            self.v_after[hb] = p_trv_user

                for et in range(6):
                    slot = et % 3

                    @block.tensor
                    def _(t, g=g, b=b, hb=hb, et=et, slot=slot):
                        if et == 0:
                            W(t, "tensor", self.s_dve, wp["wqkvT"])
                            W(t, "tensor", self.s_dve, trev_after[g * 8 + b * 4 + 3])
                        for (sem, cnt) in pqkv_users[slot]:
                            W(t, "tensor", sem, cnt)
                        pqkv_users[slot] = []
                        for dc in range(NDC):
                            last = t.matmul(self.p_qkv[slot][:],
                                            self.wqkvT[:, dc * 768 + et * 128:
                                                       dc * 768 + (et + 1) * 128],
                                            self.lnxT2[hb % 2][:, dc * 512:
                                                               (dc + 1) * 512],
                                            start=(dc == 0), stop=(dc == NDC - 1))
                        self.s_pe.inc(last)
                        wp[f"qkvmm_{hb}_{et}"] = self.s_pe.c

                    if et >= 1:
                        rope_post(et - 1)
                rope_post(5)


    def emit_p2(self, block, scope):
        nc, wp, W = self.nc, self.wp, self.W
        sb = lambda *a, **k: scope.enter_context(nc.sbuf_tensor(*a, **k))
        ps = lambda *a, **k: scope.enter_context(nc.psum_tensor(*a, **k))

        p_S = [ps(f"p_S{i}", [128, 1024], F32) for i in range(2)]
        p_yq = [ps(f"p_yq{i}", [128, 132], F32) for i in range(2)]
        Pbuf = [sb(f"Pbuf{i}", [128, 1024], BF16) for i in range(2)]
        ystage = [sb(f"ystage{i}", [128, 128], BF16) for i in range(16)]
        sd_sb = [sb(f"sd_sb{i}", [128, 128], F32) for i in range(2)]
        mask_sb = sb("mask_sb", [128, 128], F32)
        st2 = sb("st2", [128, 4], F32)
        RSG = 0

        self.mask = nc.dram_tensor("mask", [128, 128], F32, kind="ExternalInput").ap()
        self.sidx = nc.dram_tensor("sidx", [128, 16], I32, kind="ExternalInput").ap()
        self.yin = [nc.dram_tensor(f"yin{c}", [1024, D], BF16, kind="Internal").ap()
                    for c in range(8)]
        self.rsout = [nc.dram_tensor(f"rsout{c}", [128, D], BF16, kind="Internal").ap()
                      for c in range(8)]
        si_sb = sb("si_sb", [128, 16], I32)
        zt_sb = sb("zt_sb", [128, D], BF16)

        if self.preload:
            @block.scalar
            def _(sc):
                W(sc, "scalar", self.s_act, self.s_act.c)
                sc.preload_act_table(AF.Exp)

        @block.sync
        def _(sy):
            self.s_ldc.inc(sy.dma_start(out=mask_sb[:], in_=self.mask), 16)
            self.s_ldc.inc(sy.dma_start(out=si_sb[:], in_=self.sidx), 16)
        wp["mask"] = self.s_ldc.c

        @block.vector
        def _(v):
            wp["zt"] = self.s_dve.inc(v.memset(zt_sb[:], 0.0))

        @block.scalar
        def _(sc):
            W(sc, "scalar", self.s_dve, wp["zt"])
            for c in range(8):
                for r in range(8):
                    self.s_zf.inc(sc.dma_start(
                        out=self.yin[c][r * 128:(r + 1) * 128, :], in_=zt_sb[:]), 16)
        wp["zf"] = self.s_zf.c

        u_pS = {0: [], 1: []}
        u_Pb = {0: [], 1: []}
        u_yq = {0: [], 1: []}
        u_sd = []
        self._last_yst = 0

        flat = []
        unit_of = []
        u = 0
        for tau in range(NT):
            for h in range(2):
                nfull = tau // 8
                for j in range(nfull + 1):
                    flat.append((tau, h, j, nfull + 1))
                    unit_of.append(u)
                u += 1

        def emit_S(idx):
            tau, h, j, nb = flat[idx]
            slot = idx % 2
            k0 = j * 1024
            Wd = 1024 if j < nb - 1 else (tau % 8 + 1) * 128
            nch = Wd // 128

            @block.tensor
            def _(t):
                for (sem, cnt) in u_pS[slot]:
                    W(t, "tensor", sem, cnt)
                u_pS[slot].clear()
                for jj in range(nch):
                    last = t.matmul(p_S[slot][:, jj * 128:(jj + 1) * 128],
                                    self.kT[h][:, k0 + jj * 128:k0 + (jj + 1) * 128],
                                    self.qT[h][:, tau * 128:(tau + 1) * 128],
                                    start=True, stop=True)
                self.s_pe.inc(last)
                wp[f"S_{idx}"] = self.s_pe.c

        def emit_sd(idx):
            tau, h, j, nb = flat[idx]
            slot = idx % 2
            is_last = (j == nb - 1)
            if not is_last:
                return
            Wd = (tau % 8 + 1) * 128

            @block.vector
            def _(v):
                W(v, "vector", self.s_ldc, wp["mask"])
                W(v, "vector", self.s_pe, wp[f"S_{idx}"])
                for (sem, cnt) in u_sd:
                    W(v, "vector", sem, cnt)
                u_sd.clear()
                wp[f"sd_{idx}"] = self.s_dve.inc(v.tensor_tensor(
                    sd_sb[idx % 2][:], p_S[slot][:, Wd - 128:Wd],
                    mask_sb[:], ALU.add))
                u_pS[slot].append((self.s_dve, wp[f"sd_{idx}"]))

        def emit_exp(sc, idx):
            tau, h, j, nb = flat[idx]
            slot = idx % 2
            is_last = (j == nb - 1)
            Wd = 1024 if not is_last else (tau % 8 + 1) * 128
            W(sc, "scalar", self.s_pe, wp[f"S_{idx}"])
            W(sc, "scalar", self.s_dve, self.wp["vones"])
            for (sem, cnt) in u_Pb[slot]:
                W(sc, "scalar", sem, cnt)
            u_Pb[slot].clear()
            bias = self.cnegb0[:, 0:1]
            if is_last:
                W(sc, "scalar", self.s_dve, wp[f"sd_{idx}"])
                if Wd > 128:
                    cnt = self.s_act.inc(sc.activation(
                        Pbuf[slot][:, 0:Wd - 128], p_S[slot][:, 0:Wd - 128],
                        AF.Exp, bias=bias, scale=CSC))
                    u_pS[slot].append((self.s_act, cnt))
                cnt = self.s_act.inc(sc.activation(
                    Pbuf[slot][:, Wd - 128:Wd], sd_sb[idx % 2][:],
                    AF.Exp, bias=bias, scale=CSC))
                u_sd.append((self.s_act, cnt))
            else:
                cnt = self.s_act.inc(sc.activation(
                    Pbuf[slot][:, 0:Wd], p_S[slot][:, 0:Wd],
                    AF.Exp, bias=bias, scale=CSC))
                u_pS[slot].append((self.s_act, cnt))
            wp[f"exp_{idx}"] = self.s_act.c

        def emit_chain(idx):
            nonlocal ycount
            tau, h, j, nb = flat[idx]
            up = unit_of[idx] % 2
            slot = idx % 2
            k0 = j * 1024
            is_last = (j == nb - 1)
            Wd = 1024 if not is_last else (tau % 8 + 1) * 128
            nch = Wd // 128

            @block.tensor
            def _(t):
                W(t, "tensor", self.s_act, wp[f"exp_{idx}"])
                yslot = unit_of[idx] % 2
                if j == 0:
                    for (sem, cnt) in u_yq[yslot]:
                        W(t, "tensor", sem, cnt)
                    u_yq[yslot].clear()
                for jj in range(nch):
                    kt = k0 // 128 + jj
                    last = t.matmul(
                        p_yq[yslot][:, 0:129],
                        Pbuf[slot][:, jj * 128:(jj + 1) * 128],
                        self.V[h].ap().rearrange("p (t c) -> p t c", c=132)[
                            :, kt, 0:129],
                        start=(j == 0 and jj == 0),
                        stop=(is_last and jj == nch - 1))
                cnt = self.s_pe.inc(last)
                u_Pb[slot].append((self.s_pe, cnt))
                wp[f"YQ_{idx}"] = cnt

            if is_last:
                ysl = (tau % 8) * 2 + h
                yslot = unit_of[idx] % 2
                @block.vector
                def _(v):
                    W(v, "vector", self.s_pe, wp[f"YQ_{idx}"])
                    W(v, "vector", self.s_dve, self._last_yst)
                    c1 = self.s_dve.inc(v.reciprocal(
                        st2[:, RSG:RSG + 1], p_yq[yslot][:, 128:129]))
                    W(v, "vector", self.s_dve, c1)
                    if tau >= 8:
                        W(v, "vector", self.s_yw[tau // 8 - 1],
                          self.s_yw[tau // 8 - 1].c)
                        if "p2y" in self.debug:
                            W(v, "vector", self.s_yd, self.s_yd.c)
                    cnt = self.s_dve.inc(v.tensor_scalar_mul(
                        ystage[ysl][:], p_yq[yslot][:, 0:128],
                        st2[:, RSG:RSG + 1]))
                    u_yq[yslot].append((self.s_dve, cnt))
                    wp[f"yst_{idx}"] = cnt
                    self._last_yst = cnt

                @block.gpsimd
                def _(gp):
                    W(gp, "gpsimd", self.s_dve, wp[f"yst_{idx}"])
                    W(gp, "gpsimd", self.s_ldc, wp["mask"])
                    W(gp, "gpsimd", self.s_zf, wp["zf"])
                    c = tau // 8
                    yview = self.yin[c].rearrange("a (s b) -> (a s) b", b=128)
                    kcol = (tau % 8) * 2 + h
                    self.s_yw[c].inc(gp.indirect_dma_start(
                        out=yview,
                        out_offset=IndirectOffsetOnAxis(si_sb[:, kcol:kcol + 1], 0),
                        in_=ystage[ysl][:], in_offset=None), 16)

                if "p2y" in self.debug:
                    @block.scalar
                    def _(sc):
                        W(sc, "scalar", self.s_dve, wp[f"yst_{idx}"])
                        self.s_yd.inc(sc.dma_start(
                            out=self.o_y[h, tau * 128:(tau + 1) * 128, :],
                            in_=ystage[ysl][:]), 16)
                ycount += 1

        if "p2y" in self.debug:
            self.o_y = nc.dram_tensor("o_y", [2, L, 128], BF16,
                                      kind="ExternalOutput").ap()

        ycount = 0
        nf = len(flat)
        emit_S(0)
        if nf > 1:
            emit_S(1)
        p = 0
        while p < nf:
            i0, i1 = p, (p + 1 if p + 1 < nf else None)
            emit_sd(i0)
            if i1 is not None:
                emit_sd(i1)

            @block.scalar
            def _(sc, i0=i0, i1=i1):
                emit_exp(sc, i0)
                if i1 is not None:
                    emit_exp(sc, i1)

            if p + 2 < nf:
                emit_S(p + 2)
            if p + 3 < nf:
                emit_S(p + 3)
            emit_chain(i0)
            if i1 is not None:
                emit_chain(i1)
            self.pump_pt(block, 3)
            p += 2
        self.finish_pt(block)

        nocc = self.nocc
        for c in range(8):
            @block.gpsimd
            def _(gp, c=c):
                W(gp, "gpsimd", self.s_yw[c], self.s_yw[c].c)
                if nocc:
                    gp.dma_start(out=self.rsout[c], in_=self.yin[c][0:128, :]
                                 ).then_inc(self.s_cc[c].h, 1, True)
                else:
                    gp.collective_compute(
                        "ReduceScatter", ALU.add, replica_groups=[list(range(8))],
                        ins=[self.yin[c]], outs=[self.rsout[c]],
                    ).then_inc(self.s_cc[c].h, 1)
                self.s_cc[c].c += 1

    def emit_consts(self, block, scope):
        nc, wp, W = self.nc, self.wp, self.W
        sb = lambda *a, **k: scope.enter_context(nc.sbuf_tensor(*a, **k))
        d = nc.dram_tensor
        self.i_sug = d("su_ug", [16, 128], F32, kind="ExternalInput").ap()
        self.i_sdp = d("su_dp", [64, 128], F32, kind="ExternalInput").ap()
        self.i_wug = d("wscale_ug", [128, 128], F32, kind="ExternalInput").ap()
        self.i_wdp = d("wscale_dp", [16, 128], F32, kind="ExternalInput").ap()
        self.i_h128 = d("h128", [128, 128], BF16, kind="ExternalInput").ap()

        self.sugT = sb("sugT", [128, 16], F32)
        self.sdpT = sb("sdpT", [128, 64], F32)
        self.wugT = sb("wugT", [128, 128], F32)
        self.wdpT = sb("wdpT", [128, 16], F32)
        self.h128_sb = sb("h128_sb", [128, 128], BF16)
        self.H1 = sb("H1", [128, 16 * 128], BF16)
        self.H2 = sb("H2", [128, 64 * 128], BF16)

        ld = sb("cstld", [128, 512], F32)
        with nc.psum_tensor("p_cst", [128, 128], F32) as p_cst:
            @block.sync
            def _(sy):
                self.s_ldc.inc(sy.dma_start(out=self.h128_sb[:], in_=self.i_h128), 16)
                self.s_ldc.inc(sy.dma_start(out=ld[0:16, 0:128], in_=self.i_sug), 16)
                self.s_ldc.inc(sy.dma_start(out=ld[0:64, 128:256],
                                            in_=self.i_sdp), 16)
                self.s_ldc.inc(sy.dma_start(out=ld[0:16, 256:384],
                                            in_=self.i_wdp), 16)
                self.s_ldc.inc(sy.dma_start(out=ld[0:128, 384:512],
                                            in_=self.i_wug), 16)
            wp["cst_ld"] = self.s_ldc.c

            idf = sb("idf", [128, 128], F32)

            @block.vector
            def _(v):
                W(v, "vector", self.s_ldc, wp["consts"])
                wp["idf"] = self.s_dve.inc(v.tensor_copy(idf[:], self.sident[:]))

            jobs = [
                (ld[0:16, 0:128], self.sugT, 16, 1.0),
                (ld[0:64, 128:256], self.sdpT, 64, 1.0),
                (ld[0:16, 256:384], self.wdpT, 16, float(1.0 / np.sqrt(8192.0))),
                (ld[0:128, 384:512], self.wugT, 128, float(1.0 / np.sqrt(2048.0))),
            ]
            for (src, dst, nrow, scl) in jobs:
                @block.tensor
                def _(t, src=src, nrow=nrow):
                    W(t, "tensor", self.s_ldc, wp["cst_ld"])
                    W(t, "tensor", self.s_dve, wp["idf"])
                    W(t, "tensor", self.s_dve, wp.get("cst_ev", 0))
                    wp["cst_tr"] = self.s_pe.inc(
                        t.transpose(p_cst[0:128, 0:nrow], src,
                                    idf[0:nrow, 0:nrow]))

                @block.vector
                def _(v, dst=dst, nrow=nrow, scl=scl):
                    W(v, "vector", self.s_pe, wp["cst_tr"])
                    wp["cst_ev"] = self.s_dve.inc(v.tensor_scalar_mul(
                        dst[:, 0:nrow], p_cst[:, 0:nrow], scl))
            wp["cst_done"] = self.s_dve.c

            @block.vector
            def _(v):
                W(v, "vector", self.s_dve, wp["cst_done"])
                W(v, "vector", self.s_ldc, wp["cst_ld"])
                for c in range(16):
                    v.tensor_scalar_mul(self.H1[:, c * 128:(c + 1) * 128],
                                        self.h128_sb[:], self.sugT[:, c:c + 1])
                for c in range(64):
                    last = v.tensor_scalar_mul(self.H2[:, c * 128:(c + 1) * 128],
                                               self.h128_sb[:], self.sdpT[:, c:c + 1])
                wp["Hmats"] = self.s_dve.inc(last)

    def emit_p3(self, block, scope):
        nc, wp, W = self.nc, self.wp, self.W
        sb = lambda *a, **k: scope.enter_context(nc.sbuf_tensor(*a, **k))
        ps = lambda *a, **k: scope.enter_context(nc.psum_tensor(*a, **k))
        d = nc.dram_tensor

        self.i_xres = d("xres", [1024, D], F32, kind="ExternalInput").ap()
        self.x2dram = d("x2dram", [1024, D], F32, kind="Internal").ap()

        woT = sb("woT", [128, NDC * D], BF16)
        ytm = [sb(f"ytm{i}", [128, D], BF16) for i in range(2)]
        yT = [sb(f"yT{i}", [128, D], BF16) for i in range(2)]
        xrs = [sb(f"xrs{i}", [128, D], F32) for i in range(2)]
        x2t = [sb(f"x2t{i}", [128, D], F32) for i in range(2)]
        zt = [sb(f"zt{i}", [128, D], BF16) for i in range(2)]
        zTt = [sb(f"zTt{i}", [128, D], BF16) for i in range(2)]
        s1 = sb("s1", [128, D], BF16)
        s2 = sb("s2", [128, D], BF16)
        st3 = sb("st3", [128, 8], F32)
        sq3 = sb("sq3", [128, D], BF16)

        p_tr3 = [ps(f"p_tr3{i}", [128, 512], BF16) for i in range(2)]
        p_wo = ps("p_wo", [128, D], F32)
        p_f1 = [ps(f"p_f1{i}", [128, 512], F32) for i in range(2)]

        @block.sync
        def _(sy):
            W(sy, "sync", self.s_cw, wp["wo16"])
            for dc in range(NDC):
                for q in range(4):
                    self.s_ldw.inc(sy.dma_start_transpose(
                        out=woT[:, dc * D + q * 512:dc * D + (q + 1) * 512],
                        in_=self.wo16[q * 512:(q + 1) * 512,
                                      dc * 128:(dc + 1) * 128]), 16)
        wp["woT"] = self.s_ldw.c

        u_ytm = {0: [], 1: []}
        u_yT = {0: [], 1: []}
        u_tr3 = {0: [], 1: []}
        u_xrs = {0: [], 1: []}
        u_x2t = {0: [], 1: []}
        u_zt = {0: [], 1: []}
        u_zTt = {0: [], 1: []}
        u_pwo = []
        u_f1 = {0: [], 1: []}
        u_s1 = []
        u_s2 = []

        for c in range(8):
            sl = c % 2
            @block.sync
            def _(sy, c=c, sl=sl):
                W(sy, "sync", self.s_cc[c], 1)
                for (sem, cnt) in u_ytm[sl]:
                    W(sy, "sync", sem, cnt)
                u_ytm[sl].clear()
                self.s_l3[sl].inc(sy.dma_start(out=ytm[sl][:], in_=self.rsout[c]), 16)
                for (sem, cnt) in u_xrs[sl]:
                    W(sy, "sync", sem, cnt)
                u_xrs[sl].clear()
                self.s_l3[sl].inc(sy.dma_start(
                    out=xrs[sl][:], in_=self.i_xres[c * 128:(c + 1) * 128, :]), 16)
                wp[f"l3_{c}"] = self.s_l3[sl].c

            for pb in range(4):
                @block.tensor
                def _(t, c=c, sl=sl, pb=pb):
                    W(t, "tensor", self.s_l3[sl], wp[f"l3_{c}"])
                    W(t, "tensor", self.s_dve, wp["Hmats"])
                    for (sem, cnt) in u_tr3[pb % 2]:
                        W(t, "tensor", sem, cnt)
                    u_tr3[pb % 2].clear()
                    if pb == 0:
                        for (sem, cnt) in u_yT[sl]:
                            W(t, "tensor", sem, cnt)
                        u_yT[sl].clear()
                    for j in range(4):
                        dc = pb * 4 + j
                        last = t.transpose(p_tr3[pb % 2][:, j * 128:(j + 1) * 128],
                                           ytm[sl][:, dc * 128:(dc + 1) * 128],
                                           self.sident[:])
                    cnt = self.s_pe.inc(last)
                    u_ytm[sl].append((self.s_pe, cnt))
                    wp[f"ytr_{c}_{pb}"] = cnt

                @block.vector
                def _(v, c=c, sl=sl, pb=pb):
                    W(v, "vector", self.s_pe, wp[f"ytr_{c}_{pb}"])
                    cnt = self.s_dve.inc(v.tensor_copy(
                        yT[sl][:, pb * 512:(pb + 1) * 512], p_tr3[pb % 2][:]))
                    u_tr3[pb % 2].append((self.s_dve, cnt))
                    wp[f"yev_{c}_{pb}"] = cnt

            @block.tensor
            def _(t, c=c, sl=sl):
                W(t, "tensor", self.s_ldw, wp["woT"])
                W(t, "tensor", self.s_dve, wp[f"yev_{c}_3"])
                for (sem, cnt) in u_pwo:
                    W(t, "tensor", sem, cnt)
                u_pwo.clear()
                for dc in range(NDC):
                    for ec in range(4):
                        last = t.matmul(
                            p_wo[:, ec * 512:(ec + 1) * 512],
                            yT[sl][:, dc * 128 + 0:dc * 128 + 128],
                            woT[:, dc * D + ec * 512:dc * D + (ec + 1) * 512],
                            start=(dc == 0), stop=(dc == NDC - 1))
                cnt = self.s_pe.inc(last)
                u_yT[sl].append((self.s_pe, cnt))
                wp[f"wo_{c}"] = cnt

            @block.vector
            def _(v, c=c, sl=sl):
                W(v, "vector", self.s_pe, wp[f"wo_{c}"])
                W(v, "vector", self.s_l3[sl], wp[f"l3_{c}"])
                for (sem, cnt) in u_x2t[sl]:
                    W(v, "vector", sem, cnt)
                u_x2t[sl].clear()
                cnt = self.s_dve.inc(v.tensor_tensor(
                    x2t[sl][:], xrs[sl][:], p_wo[:], ALU.add))
                u_pwo.append((self.s_dve, cnt))
                u_xrs[sl].append((self.s_dve, cnt))
                wp[f"x2_{c}"] = cnt

            @block.scalar
            def _(sc, c=c, sl=sl):
                W(sc, "scalar", self.s_dve, wp[f"x2_{c}"])
                cnt = self.s_x2[sl].inc(sc.dma_start(
                    out=self.x2dram[c * 128:(c + 1) * 128, :], in_=x2t[sl][:]), 16)
                u_x2t[sl].append((self.s_x2[sl], cnt))
                W(sc, "scalar", self.s_act, wp.get("sq3", 0))
                W(sc, "scalar", self.s_dve, wp.get(f"stats_rd_{c - 1}", 0))
                wp["sq3"] = self.s_act.inc(sc.activation(
                    sq3[:], x2t[sl][:], AF.Square,
                    accum_out=st3[:, 1:2]))

            @block.vector
            def _(v, c=c, sl=sl):
                W(v, "vector", self.s_dve, wp[f"x2_{c}"])
                W(v, "vector", self.s_act, wp.get(f"norm_{c - 1}", 0))
                c1 = self.s_dve.inc(v.reduce_sum(st3[:, 0:1], x2t[sl][:], axis=AX.X))
                W(v, "vector", self.s_dve, c1)
                c1 = self.s_dve.inc(v.tensor_scalar_mul(st3[:, 2:3], st3[:, 0:1],
                                                        1.0 / D))
                W(v, "vector", self.s_act, wp["sq3"])
                W(v, "vector", self.s_dve, c1)
                c1 = self.s_dve.inc(v.tensor_tensor(st3[:, 3:4], st3[:, 2:3],
                                                    st3[:, 2:3], ALU.mult))
                W(v, "vector", self.s_dve, c1)
                wp[f"var3_{c}"] = self.s_dve.inc(v.scalar_tensor_tensor(
                    st3[:, 3:4], st3[:, 1:2], 1.0 / D, st3[:, 3:4],
                    ALU.mult, ALU.subtract))

            @block.scalar
            def _(sc, c=c):
                W(sc, "scalar", self.s_dve, wp[f"var3_{c}"])
                wp[f"rstd3_{c}"] = self.s_act.inc(sc.activation(
                    st3[:, 4:5], st3[:, 3:4], AF.Sqrt, bias=self.ceps[:, 0:1]))

            @block.vector
            def _(v, c=c):
                W(v, "vector", self.s_act, wp[f"rstd3_{c}"])
                c1 = self.s_dve.inc(v.reciprocal(st3[:, 6:7], st3[:, 4:5]))
                W(v, "vector", self.s_dve, c1)
                c1 = self.s_dve.inc(v.tensor_copy(st3[:, 4:5], st3[:, 6:7]))
                W(v, "vector", self.s_dve, c1)
                wp[f"nmr3_{c}"] = self.s_dve.inc(v.scalar_tensor_tensor(
                    st3[:, 5:6], st3[:, 2:3], -1.0, st3[:, 4:5],
                    ALU.mult, ALU.mult))
                wp[f"stats_rd_{c}"] = self.s_dve.c

            @block.scalar
            def _(sc, c=c, sl=sl):
                W(sc, "scalar", self.s_dve, wp[f"nmr3_{c}"])
                for (sem, cnt) in u_zt[sl]:
                    W(sc, "scalar", sem, cnt)
                u_zt[sl].clear()
                wp[f"norm_{c}"] = self.s_act.inc(sc.activation(
                    zt[sl][:], x2t[sl][:], AF.Identity,
                    bias=st3[:, 5:6], scale=st3[:, 4:5]))

            for pb in range(4):
                @block.tensor
                def _(t, c=c, sl=sl, pb=pb):
                    W(t, "tensor", self.s_act, wp[f"norm_{c}"])
                    for (sem, cnt) in u_tr3[pb % 2]:
                        W(t, "tensor", sem, cnt)
                    u_tr3[pb % 2].clear()
                    if pb == 0:
                        for (sem, cnt) in u_zTt[sl]:
                            W(t, "tensor", sem, cnt)
                        u_zTt[sl].clear()
                    for j in range(4):
                        dc = pb * 4 + j
                        last = t.transpose(p_tr3[pb % 2][:, j * 128:(j + 1) * 128],
                                           zt[sl][:, dc * 128:(dc + 1) * 128],
                                           self.sident[:])
                    cnt = self.s_pe.inc(last)
                    u_zt[sl].append((self.s_pe, cnt))
                    wp[f"ztr_{c}_{pb}"] = cnt

                @block.vector
                def _(v, c=c, sl=sl, pb=pb):
                    W(v, "vector", self.s_pe, wp[f"ztr_{c}_{pb}"])
                    cnt = self.s_dve.inc(v.tensor_copy(
                        zTt[sl][:, pb * 512:(pb + 1) * 512], p_tr3[pb % 2][:]))
                    u_tr3[pb % 2].append((self.s_dve, cnt))
                    wp[f"zev_{c}_{pb}"] = cnt

            for pb in range(4):
                @block.tensor
                def _(t, c=c, sl=sl, pb=pb):
                    W(t, "tensor", self.s_dve, wp[f"zev_{c}_{pb}"])
                    W(t, "tensor", self.s_dve, self.wp["Hmats"])
                    for (sem, cnt) in u_f1[pb % 2]:
                        W(t, "tensor", sem, cnt)
                    u_f1[pb % 2].clear()
                    for j in range(4):
                        dc = pb * 4 + j
                        last = t.matmul(p_f1[pb % 2][:, j * 128:(j + 1) * 128],
                                        self.H1[:, dc * 128:(dc + 1) * 128],
                                        zTt[sl][:, dc * 128:(dc + 1) * 128],
                                        start=True, stop=True)
                    cnt = self.s_pe.inc(last)
                    u_zTt[sl].append((self.s_pe, cnt))
                    wp[f"f1_{c}_{pb}"] = cnt

                @block.vector
                def _(v, c=c, pb=pb):
                    W(v, "vector", self.s_pe, wp[f"f1_{c}_{pb}"])
                    if pb == 0:
                        for (sem, cnt) in u_s1:
                            W(v, "vector", sem, cnt)
                        u_s1.clear()
                    cnt = self.s_dve.inc(v.tensor_copy(
                        s1[:, pb * 512:(pb + 1) * 512], p_f1[pb % 2][:]))
                    u_f1[pb % 2].append((self.s_dve, cnt))
                    wp[f"f1ev_{c}"] = cnt

            def bf_level(eng, engname, src, dst, m, c, width=128, nch=16,
                         dst_slices=None):
                lastc = None
                for ch in range(nch):
                    a = ch & ~m
                    b_ = ch | m
                    sa = src[:, a * width:(a + 1) * width]
                    sbp = src[:, b_ * width:(b_ + 1) * width]
                    if dst_slices is None:
                        do = dst[:, ch * width:(ch + 1) * width]
                    else:
                        do = dst_slices(ch)
                    op = ALU.add if (ch & m) == 0 else ALU.subtract
                    if (ch & m) == 0:
                        lastc = eng.tensor_tensor(do, sa, sbp, ALU.add)
                    else:
                        lastc = eng.tensor_tensor(do, sa, sbp, ALU.subtract)
                return lastc

            @block.vector
            def _(v, c=c):
                W(v, "vector", self.s_dve, wp[f"f1ev_{c}"])
                for (sem, cnt) in u_s2:
                    W(v, "vector", sem, cnt)
                u_s2.clear()
                last = bf_level(v, "vector", s1, s2, 1, c)
                c1 = self.s_dve.inc(last)
                W(v, "vector", self.s_dve, c1)
                last = bf_level(v, "vector", s2, s1, 2, c)
                c1 = self.s_dve.inc(last)
                W(v, "vector", self.s_dve, c1)
                last = bf_level(v, "vector", s1, s2, 4, c)
                c1 = self.s_dve.inc(last)
                W(v, "vector", self.s_dve, c1)
                last = bf_level(
                    v, "vector", s2, None, 8, c,
                    dst_slices=lambda ch: self.zpT[:, ch * 1024 + c * 128:
                                                   ch * 1024 + c * 128 + 128])
                cnt = self.s_dve.inc(last)
                u_s1.append((self.s_dve, cnt))
                u_s2.append((self.s_dve, cnt))
                wp[f"zp_{c}"] = cnt
        wp["zpT_done"] = self.s_dve.c
        wp["p3_pe"] = self.s_pe.c
        wp["p3_act"] = self.s_act.c
        wp["x2_done0"] = self.s_x2[0].c
        wp["x2_done1"] = self.s_x2[1].c

    def emit_ffn_casts(self, block):
        nc, wp = self.nc, self.wp
        d = nc.dram_tensor
        self.i_wo = d("wo", [D, D], F32, kind="ExternalInput").ap()
        self.i_wup = d("wup", [INTER, D], F32, kind="ExternalInput").ap()
        self.i_wga = d("wga", [INTER, D], F32, kind="ExternalInput").ap()
        self.i_wdn = d("wdn", [D, INTER], F32, kind="ExternalInput").ap()
        self.wo16 = d("wo16", [D, D], BF16, kind="Internal").ap()
        self.wup16 = d("wup16", [INTER, D], BF16, kind="Internal").ap()
        self.wga16 = d("wga16", [INTER, D], BF16, kind="Internal").ap()
        self.wdn16 = d("wdn16", [D, INTER], BF16, kind="Internal").ap()

        @block.gpsimd
        def _(g):
            self.s_cw.inc(g.dma_start(out=self.wo16, in_=self.i_wo), 16)
            for k in range(4):
                self.s_cu.inc(g.dma_start(
                    out=self.wup16[k * 2048:(k + 1) * 2048, :],
                    in_=self.i_wup[k * 2048:(k + 1) * 2048, :]), 16)
            for k in range(4):
                self.s_cg.inc(g.dma_start(
                    out=self.wga16[k * 2048:(k + 1) * 2048, :],
                    in_=self.i_wga[k * 2048:(k + 1) * 2048, :]), 16)
            for k in range(4):
                self.s_cd.inc(g.dma_start(
                    out=self.wdn16[k * 512:(k + 1) * 512, :],
                    in_=self.i_wdn[k * 512:(k + 1) * 512, :]), 16)
        wp["wo16"] = self.s_cw.c
        wp["wup16"] = self.s_cu.c
        wp["wga16"] = self.s_cg.c
        wp["wdn16"] = self.s_cd.c

    def emit_pretranspose(self, block):
        nc, wp, W = self.nc, self.wp, self.W
        d = nc.dram_tensor
        self.wupT16 = d("wupT16", [D, INTER], BF16, kind="Internal").ap()
        self.wgaT16 = d("wgaT16", [D, INTER], BF16, kind="Internal").ap()
        self.wdnT16 = d("wdnT16", [INTER, D], BF16, kind="Internal").ap()
        jobs = []
        for i4 in range(16):
            for dc in range(16):
                jobs.append((self.wup16[i4 * 512:(i4 + 1) * 512,
                                        dc * 128:(dc + 1) * 128],
                             self.wupT16[dc * 128:(dc + 1) * 128,
                                         i4 * 512:(i4 + 1) * 512], self.s_cu))
        for i4 in range(16):
            for dc in range(16):
                jobs.append((self.wga16[i4 * 512:(i4 + 1) * 512,
                                        dc * 128:(dc + 1) * 128],
                             self.wgaT16[dc * 128:(dc + 1) * 128,
                                         i4 * 512:(i4 + 1) * 512], self.s_cg))
        for ic in range(64):
            for eg in range(4):
                jobs.append((self.wdn16[eg * 512:(eg + 1) * 512,
                                        ic * 128:(ic + 1) * 128],
                             self.wdnT16[ic * 128:(ic + 1) * 128,
                                         eg * 512:(eg + 1) * 512], self.s_cd))
        self._pt_jobs = jobs
        self._pt_k = 0

    def pump_pt(self, block, n):
        nc, wp, W = self.nc, self.wp, self.W
        jobs = self._pt_jobs[self._pt_k:self._pt_k + n]
        if not jobs:
            return
        k0 = self._pt_k
        self._pt_k += len(jobs)
        nslot = 4

        @block.sync
        def _(sy):
            for i, (src, dst, gate) in enumerate(jobs):
                k = k0 + i
                sl = k % nslot
                W(sy, "sync", gate, gate.c)
                W(sy, "sync", self.s_ps[sl], self.s_ps[sl].c)
                self.s_pt[sl].inc(sy.dma_start_transpose(
                    out=self.ptstg[sl][:], in_=src), 16)
                wp[f"pt_{k}"] = self.s_pt[sl].c

        @block.gpsimd
        def _(gp):
            for i, (src, dst, gate) in enumerate(jobs):
                k = k0 + i
                sl = k % nslot
                W(gp, "gpsimd", self.s_pt[sl], wp[f"pt_{k}"])
                self.s_ps[sl].inc(gp.dma_start(out=dst, in_=self.ptstg[sl][:]), 16)

    def finish_pt(self, block):
        self.pump_pt(block, len(self._pt_jobs) - self._pt_k)
        self.wp["ptdone"] = [(self.s_ps[i], self.s_ps[i].c) for i in range(4)]

    def emit_p4a(self, block):
        nc, wp, W = self.nc, self.wp, self.W
        d = nc.dram_tensor
        self.adram = d("adram", [INTER, 1024], BF16, kind="Internal").ap()
        self.out = d("out", [1024, D], F32, kind="ExternalOutput").ap()

        if self.preload:
            @block.scalar
            def _(sc):
                W(sc, "scalar", self.s_act, wp["p3_act"])
                sc.preload_act_table(AF.Silu)

        with contextlib.ExitStack() as s4a:
            sb = lambda *a, **k: s4a.enter_context(nc.sbuf_tensor(*a, **k))
            ps = lambda *a, **k: s4a.enter_context(nc.psum_tensor(*a, **k))
            wT = [sb(f"wT{i}", [128, NDC * 512], BF16) for i in range(4)]
            atile = [sb(f"atile{i}", [128, 1024], BF16) for i in range(2)]
            usil = sb("usil", [128, 1024], BF16)
            gsil = sb("gsil", [128, 1024], BF16)
            p_u = [ps(f"p_u{i}", [128, 1024], F32) for i in range(2)]
            p_g = [ps(f"p_g{i}", [128, 1024], F32) for i in range(2)]

            u_wT = {i: [] for i in range(4)}
            u_at = {0: [], 1: []}
            u_pu = {0: [], 1: []}
            u_pg = {0: [], 1: []}
            u_us = []
            u_gs = []

            for i4 in range(16):
                dsl = i4 % 2
                @block.sync
                def _(sy, i4=i4, dsl=dsl):
                    if i4 == 0:
                        for (sem, cnt) in wp["ptdone"]:
                            W(sy, "sync", sem, cnt)
                        W(sy, "sync", self.s_pe, wp["p3_pe"])
                        W(sy, "sync", self.s_dve, wp["zpT_done"])
                        W(sy, "sync", self.s_act, wp["p3_act"])
                    for (sem, cnt) in u_wT[dsl * 2] + u_wT[dsl * 2 + 1]:
                        W(sy, "sync", sem, cnt)
                    u_wT[dsl * 2].clear()
                    u_wT[dsl * 2 + 1].clear()
                    src = self.wupT16[:, i4 * 512:(i4 + 1) * 512]
                    src = src.rearrange("(dc p) i -> p dc i", p=128)
                    dst = wT[dsl * 2].ap().rearrange("p (dc i) -> p dc i", dc=NDC)
                    self.s_w4[dsl].inc(sy.dma_start(out=dst, in_=src), 16)
                    src = self.wgaT16[:, i4 * 512:(i4 + 1) * 512]
                    src = src.rearrange("(dc p) i -> p dc i", p=128)
                    dst = wT[dsl * 2 + 1].ap().rearrange("p (dc i) -> p dc i", dc=NDC)
                    self.s_w4[dsl].inc(sy.dma_start(out=dst, in_=src), 16)
                    wp[f"w4_{i4}"] = self.s_w4[dsl].c

                for it4 in range(4):
                    it = i4 * 4 + it4
                    psl = it % 2
                    @block.tensor
                    def _(t, i4=i4, it4=it4, it=it, psl=psl, dsl=dsl):
                        W(t, "tensor", self.s_w4[dsl], wp[f"w4_{i4}"])
                        W(t, "tensor", self.s_dve, wp["zpT_done"])
                        for (sem, cnt) in u_pu[psl]:
                            W(t, "tensor", sem, cnt)
                        u_pu[psl].clear()
                        for dc in range(NDC):
                            for tb in range(2):
                                last = t.matmul(
                                    p_u[psl][:, tb * 512:(tb + 1) * 512],
                                    wT[dsl * 2][:, dc * 512 + it4 * 128:
                                                dc * 512 + (it4 + 1) * 128],
                                    self.zpT[:, dc * 1024 + tb * 512:
                                             dc * 1024 + (tb + 1) * 512],
                                    start=(dc == 0),
                                    stop=(dc == NDC - 1))
                        cnt = self.s_pe.inc(last)
                        u_wT[dsl * 2].append((self.s_pe, cnt))
                        wp[f"umm_{it}"] = cnt
                        for (sem, cnt) in u_pg[psl]:
                            W(t, "tensor", sem, cnt)
                        u_pg[psl].clear()
                        for dc in range(NDC):
                            for tb in range(2):
                                last = t.matmul(
                                    p_g[psl][:, tb * 512:(tb + 1) * 512],
                                    wT[dsl * 2 + 1][:, dc * 512 + it4 * 128:
                                                    dc * 512 + (it4 + 1) * 128],
                                    self.zpT[:, dc * 1024 + tb * 512:
                                             dc * 1024 + (tb + 1) * 512],
                                    start=(dc == 0),
                                    stop=(dc == NDC - 1))
                        cnt = self.s_pe.inc(last)
                        u_wT[dsl * 2 + 1].append((self.s_pe, cnt))
                        wp[f"gmm_{it}"] = cnt

                    @block.scalar
                    def _(sc, it=it, psl=psl):
                        W(sc, "scalar", self.s_pe, wp[f"umm_{it}"])
                        W(sc, "scalar", self.s_dve, wp["cst_done"])
                        for (sem, cnt) in u_us:
                            W(sc, "scalar", sem, cnt)
                        u_us.clear()
                        c1 = self.s_act.inc(sc.activation(
                            usil[:], p_u[psl][:], AF.Identity,
                            scale=self.wugT[:, it:it + 1]))
                        u_pu[psl].append((self.s_act, c1))
                        W(sc, "scalar", self.s_pe, wp[f"gmm_{it}"])
                        for (sem, cnt) in u_gs:
                            W(sc, "scalar", sem, cnt)
                        u_gs.clear()
                        c2 = self.s_act.inc(sc.activation(
                            gsil[:], p_g[psl][:], AF.Silu,
                            scale=self.wugT[:, 64 + it:64 + it + 1]))
                        u_pg[psl].append((self.s_act, c2))
                        wp[f"sil_{it}"] = self.s_act.c

                    asl = it % 2
                    @block.vector
                    def _(v, it=it, asl=asl):
                        W(v, "vector", self.s_act, wp[f"sil_{it}"])
                        for (sem, cnt) in u_at[asl]:
                            W(v, "vector", sem, cnt)
                        u_at[asl].clear()
                        cnt = self.s_dve.inc(v.tensor_tensor(
                            atile[asl][:], usil[:], gsil[:], ALU.mult))
                        u_us.append((self.s_dve, cnt))
                        u_gs.append((self.s_dve, cnt))
                        wp[f"amul_{it}"] = cnt

                    @block.scalar
                    def _(sc, it=it, asl=asl):
                        W(sc, "scalar", self.s_dve, wp[f"amul_{it}"])
                        cnt = self.s_aw[asl].inc(sc.dma_start(
                            out=self.adram[it * 128:(it + 1) * 128, :],
                            in_=atile[asl][:]), 16)
                        u_at[asl].append((self.s_aw[asl], cnt))
            wp["adram_done0"] = self.s_aw[0].c
            wp["adram_done1"] = self.s_aw[1].c
            wp["p4a_act"] = self.s_act.c
            wp["p4a_pe"] = self.s_pe.c
            wp["p4a_dve"] = self.s_dve.c

    def emit_p4b(self, block, scope):
        nc, wp, W = self.nc, self.wp, self.W
        sb = lambda *a, **k: scope.enter_context(nc.sbuf_tensor(*a, **k))
        ps = lambda *a, **k: scope.enter_context(nc.psum_tensor(*a, **k))
        IN = sb("IN", [128, 64 * 256], BF16)
        SC = sb("SC", [128, 64 * 256], BF16)
        OB = sb("OB", [128, 64 * 256], BF16)
        OC = sb("OC", [128, 64 * 256], BF16)
        wdT = [sb(f"wdT{i}", [128, 1024], BF16) for i in range(2)]
        xoT = sb("xoT", [128, 16 * 256], BF16)
        x2r = [sb(f"x2r{i}", [128, D], F32) for i in range(2)]
        xfin = [sb(f"xfin{i}", [128, D], F32) for i in range(2)]
        p_a = [ps(f"p_a{i}", [128, 256], F32) for i in range(2)]
        p_dn = ps("p_dn", [128, 8 * 256], F32)
        p_ft = [ps(f"p_ft{i}", [128, 512], BF16) for i in range(2)]

        u_pa = {0: [], 1: []}
        u_pdn = []
        u_pft = {0: [], 1: []}
        u_wdT = {0: [], 1: []}
        u_xoT = []
        u_x2r = {0: [], 1: []}
        u_xfin = {0: [], 1: []}
        NDVE = 46

        def emit_stage(q):
            X = OB if q % 2 == 0 else OC
            for g8 in range(8):
                @block.sync
                def _(sy, q=q, g8=g8):
                    W(sy, "sync", self.s_aw[0], wp["adram_done0"])
                    W(sy, "sync", self.s_aw[1], wp["adram_done1"])
                    if q == 0 and g8 == 0:
                        W(sy, "sync", self.s_act, wp["p4a_act"])
                        W(sy, "sync", self.s_pe, wp["p4a_pe"])
                        W(sy, "sync", self.s_dve, wp["p4a_dve"])
                        W(sy, "sync", self.s_x2[0], wp["x2_done0"])
                        W(sy, "sync", self.s_x2[1], wp["x2_done1"])
                    if q > 0:
                        W(sy, "sync", self.s_pe, wp[f"fa_{q - 1}"])
                    src = self.adram[g8 * 1024:(g8 + 1) * 1024,
                                     q * 256:(q + 1) * 256]
                    src = src.rearrange("(c p) t -> p c t", p=128)
                    dst = IN[:, g8 * 8 * 256:(g8 + 1) * 8 * 256]
                    dst = dst.rearrange("p (c t) -> p c t", c=8)
                    self.s_ar[g8].inc(sy.dma_start(out=dst, in_=src), 16)
                    wp[f"ar_{q}_{g8}"] = self.s_ar[g8].c

            for ic in range(64):
                psl = ic % 2
                @block.tensor
                def _(t, q=q, ic=ic, psl=psl):
                    W(t, "tensor", self.s_ar[ic // 8], wp[f"ar_{q}_{ic // 8}"])
                    W(t, "tensor", self.s_dve, wp["Hmats"])
                    if ic == 0 and q >= 2:
                        W(t, "tensor", self.s_pe, wp[f"dn_{q - 2}_1"])
                        for (sem, cnt) in u_pdn:
                            W(t, "tensor", sem, cnt)
                    for (sem, cnt) in u_pa[psl]:
                        W(t, "tensor", sem, cnt)
                    u_pa[psl].clear()
                    wp[f"fa_{q}_{ic}"] = self.s_pe.inc(t.matmul(
                        p_a[psl][:], self.H2[:, ic * 128:(ic + 1) * 128],
                        IN[:, ic * 256:(ic + 1) * 256], start=True, stop=True))
                    if ic == 63:
                        wp[f"fa_{q}"] = self.s_pe.c

                @block.vector
                def _(v, q=q, ic=ic, psl=psl, X=X):
                    W(v, "vector", self.s_pe, wp[f"fa_{q}_{ic}"])
                    if ic == 0 and q >= 2:
                        W(v, "vector", self.s_pe, wp[f"dn_{q - 2}_1"])
                    cnt = self.s_dve.inc(v.tensor_copy(
                        X[:, ic * 256:(ic + 1) * 256], p_a[psl][:]))
                    u_pa[psl].append((self.s_dve, cnt))
                    wp[f"faev_{q}"] = cnt

        def emit_bf(q):
            X = OB if q % 2 == 0 else OC
            levels = [(X, SC, 1), (SC, X, 2), (X, SC, 4), (SC, X, 8),
                      (X, SC, 16), (SC, X, 32)]
            for li, (src, dst, m) in enumerate(levels):
                def emit_half(sec, en, chunks, q=q, li=li, src=src, dst=dst, m=m):
                    hold = {}
                    @sec
                    def _(e):
                        if li == 0:
                            W(e, en, self.s_dve, wp[f"faev_{q}"])
                            if q > 0:
                                for (psem, pcnt) in wp[f"bf_{q - 1}_5"]:
                                    W(e, en, psem, pcnt)
                        else:
                            for (psem, pcnt) in wp[f"bf_{q}_{li - 1}"]:
                                W(e, en, psem, pcnt)
                        lastc = None
                        for ch in chunks:
                            a_ = ch & ~m
                            b_ = ch | m
                            sa = src[:, a_ * 256:(a_ + 1) * 256]
                            sbp = src[:, b_ * 256:(b_ + 1) * 256]
                            do = dst[:, ch * 256:(ch + 1) * 256]
                            op = ALU.add if (ch & m) == 0 else ALU.subtract
                            lastc = e.tensor_tensor(do, sa, sbp, op)
                        if en == "gpsimd":
                            hold["r"] = (self.s_pool, self.s_pool.inc(lastc))
                        else:
                            hold["r"] = (self.s_dve, self.s_dve.inc(lastc))
                    return hold["r"]
                r1 = emit_half(block.vector, "vector", range(NDVE))
                r2 = emit_half(block.gpsimd, "gpsimd", range(NDVE, 64))
                wp[f"bf_{q}_{li}"] = [r1, r2]

        def emit_down(q):
            X = OB if q % 2 == 0 else OC
            for eh in range(2):
                for ic in range(64):
                    wsl = ic % 2
                    @block.sync
                    def _(sy, q=q, eh=eh, ic=ic, wsl=wsl):
                        W(sy, "sync", self.s_aw[0], wp["adram_done0"])
                        W(sy, "sync", self.s_aw[1], wp["adram_done1"])
                        W(sy, "sync", self.s_x2[0], wp["x2_done0"])
                        W(sy, "sync", self.s_x2[1], wp["x2_done1"])
                        W(sy, "sync", self.s_act, wp["p4a_act"])
                        W(sy, "sync", self.s_pe, wp["p4a_pe"])
                        W(sy, "sync", self.s_dve, wp["p4a_dve"])
                        for (sem, cnt) in u_wdT[wsl]:
                            W(sy, "sync", sem, cnt)
                        u_wdT[wsl].clear()
                        self.s_wd[wsl].inc(sy.dma_start(
                            out=wdT[wsl][:],
                            in_=self.wdnT16[ic * 128:(ic + 1) * 128,
                                            eh * 1024:(eh + 1) * 1024]), 16)
                        wp[f"wd_{q}_{eh}_{ic}"] = self.s_wd[wsl].c

                    @block.tensor
                    def _(t, q=q, eh=eh, ic=ic, wsl=wsl, X=X):
                        W(t, "tensor", self.s_wd[wsl], wp[f"wd_{q}_{eh}_{ic}"])
                        for (psem, pcnt) in wp[f"bf_{q}_5"]:
                            W(t, "tensor", psem, pcnt)
                        if ic == 0:
                            for (sem, cnt) in u_pdn:
                                W(t, "tensor", sem, cnt)
                            u_pdn.clear()
                        for et in range(8):
                            last = t.matmul(
                                p_dn[:, et * 256:(et + 1) * 256],
                                wdT[wsl][:, et * 128:(et + 1) * 128],
                                X[:, ic * 256:(ic + 1) * 256],
                                start=(ic == 0 and et % 2 == 0),
                                stop=(ic == 63 and et % 2 == 1))
                        cnt = self.s_pe.inc(last)
                        u_wdT[wsl].append((self.s_pe, cnt))
                        wp[f"dn_{q}_{eh}"] = cnt

                @block.scalar
                def _(sc, q=q, eh=eh):
                    W(sc, "scalar", self.s_pe, wp[f"dn_{q}_{eh}"])
                    if eh == 0:
                        for (sem, cnt) in u_xoT:
                            W(sc, "scalar", sem, cnt)
                        u_xoT.clear()
                    for et8 in range(8):
                        et = eh * 8 + et8
                        cnt = self.s_act.inc(sc.activation(
                            xoT[:, et * 256:(et + 1) * 256],
                            p_dn[:, et8 * 256:(et8 + 1) * 256],
                            AF.Identity, scale=self.wdpT[:, et:et + 1]))
                    u_pdn.append((self.s_act, cnt))
                    wp[f"xoev_{q}"] = cnt

            for tt in range(2):
                gt = q * 2 + tt
                xsl = gt % 2
                @block.sync
                def _(sy, gt=gt, xsl=xsl):
                    for (sem, cnt) in u_x2r[xsl]:
                        W(sy, "sync", sem, cnt)
                    u_x2r[xsl].clear()
                    W(sy, "sync", self.s_x2[0], wp["x2_done0"])
                    W(sy, "sync", self.s_x2[1], wp["x2_done1"])
                    self.s_x2r[xsl].inc(sy.dma_start(
                        out=x2r[xsl][:],
                        in_=self.x2dram[gt * 128:(gt + 1) * 128, :]), 16)
                    wp[f"x2r_{gt}"] = self.s_x2r[xsl].c

                for pb in range(4):
                    @block.tensor
                    def _(t, q=q, tt=tt, pb=pb, gt=gt):
                        W(t, "tensor", self.s_act, wp[f"xoev_{q}"])
                        for (sem, cnt) in u_pft[pb % 2]:
                            W(t, "tensor", sem, cnt)
                        u_pft[pb % 2].clear()
                        for j in range(4):
                            et = pb * 4 + j
                            last = t.transpose(
                                p_ft[pb % 2][:, j * 128:(j + 1) * 128],
                                xoT[:, et * 256 + tt * 128:
                                    et * 256 + tt * 128 + 128],
                                self.sident[:])
                        cnt = self.s_pe.inc(last)
                        u_xoT.append((self.s_pe, cnt))
                        wp[f"ftr_{gt}_{pb}"] = cnt

                    @block.vector
                    def _(v, gt=gt, pb=pb, xsl=xsl):
                        W(v, "vector", self.s_pe, wp[f"ftr_{gt}_{pb}"])
                        W(v, "vector", self.s_x2r[xsl], wp[f"x2r_{gt}"])
                        if pb == 0:
                            for (sem, cnt) in u_xfin[xsl]:
                                W(v, "vector", sem, cnt)
                            u_xfin[xsl].clear()
                        cnt = self.s_dve.inc(v.tensor_tensor(
                            xfin[xsl][:, pb * 512:(pb + 1) * 512],
                            x2r[xsl][:, pb * 512:(pb + 1) * 512],
                            p_ft[pb % 2][:], ALU.add))
                        u_pft[pb % 2].append((self.s_dve, cnt))
                        u_x2r[xsl].append((self.s_dve, cnt))
                        wp[f"fin_{gt}"] = cnt

                @block.scalar
                def _(sc, gt=gt, xsl=xsl):
                    W(sc, "scalar", self.s_dve, wp[f"fin_{gt}"])
                    cnt = self.s_so[xsl].inc(sc.dma_start(
                        out=self.out[gt * 128:(gt + 1) * 128, :],
                        in_=xfin[xsl][:]), 16)
                    u_xfin[xsl].append((self.s_so[xsl], cnt))

        emit_stage(0)
        emit_bf(0)
        for q in range(4):
            if q + 1 < 4:
                emit_stage(q + 1)
                emit_bf(q + 1)
            emit_down(q)

    def emit_p1_taps(self, block):
        nc = self.nc
        o_qT = nc.dram_tensor("o_qT", [2, DH, L], BF16, kind="ExternalOutput").ap()
        o_kT = nc.dram_tensor("o_kT", [2, DH, L], BF16, kind="ExternalOutput").ap()
        o_V = nc.dram_tensor("o_V", [2, 128, L], BF16, kind="ExternalOutput").ap()
        W = self.W
        last_hb = NG * 2 - 1
        for h in range(2):
            for (src, dst) in [(self.qT[h], o_qT), (self.kT[h], o_kT),
                               (self.V[h], o_V)]:
                @block.scalar
                def _(sc, h=h, src=src, dst=dst):
                    W(sc, "scalar", self.s_pool, self.rope_after[last_hb])
                    W(sc, "scalar", self.s_dve, self.v_after[last_hb])
                    self.s_st.inc(sc.dma_start(out=dst[h, :, :], in_=src[:]), 16)

    def emit_final_wait(self, block):
        @block.sync
        def _(s):
            s.wait_ge(self.s_st.h, self.s_st.c)
            for c in range(8):
                if self.s_cc[c].c:
                    s.wait_ge(self.s_cc[c].h, self.s_cc[c].c)
            for c in range(8):
                if self.s_yw[c].c:
                    s.wait_ge(self.s_yw[c].h, self.s_yw[c].c)
            for i in range(2):
                if self.s_so[i].c:
                    s.wait_ge(self.s_so[i].h, self.s_so[i].c)



def hadamard(n):
    H = np.array([[1.0]], dtype=np.float32)
    while H.shape[0] < n:
        H = np.block([[H, H], [H, -H]])
    return H.astype(np.float32)


def make_consts():
    inv = 1.0 / (10000.0 ** (np.arange(0, DH, 2, dtype=np.float64) / DH))
    ang = np.arange(L, dtype=np.float64)[:, None] * inv[None, :]
    ang2 = np.repeat(ang, 2, axis=1)
    cos_t = np.cos(ang2).T.astype(bf)
    sin_t = np.sin(ang2).T.astype(bf)
    pmT = np.zeros((128, 128), dtype=np.float32)
    for j in range(64):
        pmT[2 * j + 1, 2 * j] = -1.0
        pmT[2 * j, 2 * j + 1] = 1.0
    ident = np.eye(128, dtype=np.float32)
    return cos_t, sin_t, pmT.astype(bf), ident.astype(bf)


def make_mask():
    m = np.zeros((128, 128), dtype=np.float32)
    for k in range(128):
        m[k, 0:k] = -1e9
    return m


def wqkv_slice(W_qkv, core):
    h0 = 2 * core
    rows = []
    for sec in range(3):
        for h in (h0, h0 + 1):
            rows.append(W_qkv[sec * D + h * DH: sec * D + (h + 1) * DH, :])
    return np.ascontiguousarray(np.concatenate(rows, axis=0))


_CACHE = {}


def _get_nc():
    if "nc" not in _CACHE:
        _CACHE["nc"] = Builder().build()
    return _CACHE["nc"]


def kernel(L0_attn_x, L0_attn_W_qkv, L0_attn_W_o, L0_ffn_SU_ug, L0_ffn_SU_dp,
           L0_ffn_Wscale_ug, L0_ffn_Wscale_dp, L0_ffn_W_up, L0_ffn_W_gate,
           L0_ffn_W_down):
    x = np.ascontiguousarray(np.asarray(L0_attn_x, dtype=np.float32).reshape(L, D))
    W_qkv = np.ascontiguousarray(np.asarray(L0_attn_W_qkv, dtype=np.float32))
    Wo = np.ascontiguousarray(np.asarray(L0_attn_W_o, dtype=np.float32))
    SUug = np.asarray(L0_ffn_SU_ug, dtype=np.float32)
    SUdp = np.asarray(L0_ffn_SU_dp, dtype=np.float32)
    WSug = np.asarray(L0_ffn_Wscale_ug, dtype=np.float32)
    WSdp = np.asarray(L0_ffn_Wscale_dp, dtype=np.float32)
    Wup = np.ascontiguousarray(np.asarray(L0_ffn_W_up, dtype=np.float32))
    Wga = np.ascontiguousarray(np.asarray(L0_ffn_W_gate, dtype=np.float32))
    Wdn = np.ascontiguousarray(np.asarray(L0_ffn_W_down, dtype=np.float32))
    cos_t, sin_t, pmT, ident = make_consts()
    mask = make_mask()
    h128 = hadamard(128).astype(bf)

    nc = _get_nc()
    in_maps = []
    rows_of = {}
    for c in range(NCORES):
        idx = np.zeros((128, 16), dtype=np.int32)
        for t8 in range(8):
            for h in range(2):
                idx[:, t8 * 2 + h] = (t8 * 128 + np.arange(128)) * 16 + c * 2 + h
        myrows = np.concatenate([np.arange(cc * 1024 + c * 128,
                                           cc * 1024 + c * 128 + 128)
                                 for cc in range(8)])
        rows_of[c] = myrows
        in_maps.append({
            "x": x, "wqkv": wqkv_slice(W_qkv, c),
            "cos_t": cos_t, "sin_t": sin_t, "pmT": pmT, "ident": ident,
            "mask": mask, "sidx": idx,
            "xres": np.ascontiguousarray(x[myrows]),
            "wo": Wo, "wup": Wup, "wga": Wga, "wdn": Wdn,
            "wscale_ug": np.ascontiguousarray(WSug.reshape(128, 128)),
            "wscale_dp": np.ascontiguousarray(WSdp.reshape(16, 128)),
            "su_ug": np.ascontiguousarray(SUug.reshape(16, 128)),
            "su_dp": np.ascontiguousarray(SUdp.reshape(64, 128)),
            "h128": h128,
        })
    res = run_bass_kernel_spmd(nc, in_maps, core_ids=list(range(NCORES)))
    out = np.empty((L, D), dtype=np.float32)
    for c in range(NCORES):
        out[rows_of[c]] = np.asarray(res.results[c]["out"], dtype=np.float32)
    return out.reshape(1, L, D)


# version 15
# speedup vs baseline: 1.0735x; 1.0201x over previous
import contextlib
import numpy as np
import ml_dtypes
import concourse.bass as bass
import concourse.mybir as mybir
from concourse.bass_utils import run_bass_kernel_spmd

F32 = mybir.dt.float32
BF16 = mybir.dt.bfloat16
AF = mybir.ActivationFunctionType
ALU = mybir.AluOpType
AX = mybir.AxisListType
I32 = mybir.dt.int32
from concourse.bass_core_rust import IndirectOffsetOnAxis
bf = ml_dtypes.bfloat16

L = 8192
D = 2048
DH = 128
NCORES = 8
TPC = L // NCORES
NT = L // 128
NG = 8
NDC = D // 128
INTER = 8192
EPS = 1e-5
CSC = float(1.0 / np.sqrt(DH))
B0 = 8.0


class Sem:
    def __init__(self, nc, es, name):
        self.h = es.enter_context(nc.semaphore(name))
        self.c = 0
        self.name = name

    def inc(self, instr, n=1):
        instr.then_inc(self.h, n)
        self.c += n
        return self.c


ENGN = {"sync": 0, "gpsimd": 1, "vector": 2, "tensor": 3, "scalar": 4}


class Builder:
    def __init__(self, debug_taps=(), stage="full", nocc=False, preload=False):
        self.preload = preload
        self.nocc = nocc
        self.stage = stage
        self.debug = set(debug_taps)
        nc = self.nc = bass.Bass("TRN2", target_bir_lowering=False, debug=False)
        self.es = contextlib.ExitStack()
        self.seen = {e: {} for e in ENGN}

        d = nc.dram_tensor
        self.x = d("x", [L, D], F32, kind="ExternalInput").ap()
        self.wqkv = d("wqkv", [768, D], F32, kind="ExternalInput").ap()
        self.cos_t = d("cos_t", [DH, L], BF16, kind="ExternalInput").ap()
        self.sin_t = d("sin_t", [DH, L], BF16, kind="ExternalInput").ap()
        self.pmT = d("pmT", [128, 128], BF16, kind="ExternalInput").ap()
        self.ident = d("ident", [128, 128], BF16, kind="ExternalInput").ap()

        self.wqkv16 = d("wqkv16", [768, D], BF16, kind="Internal").ap()

        a = nc.alloc_sbuf_tensor
        self.sident = a("sident", [128, 128], BF16, side="right")
        self.ceps = a("ceps", [128, 1], F32, side="right")
        self.cnegb0 = a("cnegb0", [128, 1], F32, side="right")
        self.ptstg = [a(f"ptstg{i}", [128, 512], BF16, side="right")
                      for i in range(4)]

        es, S = self.es, lambda n: Sem(nc, es, n)
        self.s_ldx = [S(f"s_ldx{i}") for i in range(4)]
        self.s_cs = [S(f"s_cs{i}") for i in range(2)]
        self.s_ldc = S("s_ldc")
        self.s_cast = S("s_cast")
        self.s_ldw = S("s_ldw")
        self.s_act = S("s_act")
        self.s_dve = S("s_dve")
        self.s_pe = S("s_pe")
        self.s_pool = S("s_pool")
        self.s_st = S("s_st")
        self.s_yw = [S(f"s_yw{c}") for c in range(8)]
        self.s_zf = S("s_zf")
        self.s_yd = S("s_yd")
        self.s_cc = [S(f"s_cc{c}") for c in range(8)]
        self.s_cw = S("s_cw")
        self.s_cu = S("s_cu")
        self.s_cg = S("s_cg")
        self.s_cd = S("s_cd")
        self.s_l3 = [S(f"s_l3{i}") for i in range(2)]
        self.s_x2 = [S(f"s_x2_{i}") for i in range(2)]
        self.s_w4 = [S(f"s_w4{i}") for i in range(2)]
        self.s_aw = [S(f"s_aw{i}") for i in range(2)]
        self.s_ar = [S(f"s_ar{i}") for i in range(8)]
        self.s_wd = [S(f"s_wd{i}") for i in range(2)]
        self.s_x2r = [S(f"s_x2r{i}") for i in range(2)]
        self.s_so = [S(f"s_so{i}") for i in range(2)]
        self.s_pt = [S(f"s_pt{i}") for i in range(4)]
        self.s_ps = [S(f"s_ps{i}") for i in range(4)]
        self.wp = {}

    def load_act_set(self, sc, name):
        from concourse.hw_specs import get_activation_tables
        for idx, nm in enumerate(get_activation_tables(self.nc.m.arch)):
            if nm == name:
                return sc.add_instruction(mybir.InstLoadActFuncSet(
                    name=self.nc.get_next_instruction_name(),
                    act_func_set_id=idx, ins=[], outs=[]))
        raise ValueError(name)

    def W(self, eng, engname, sem, val):
        if val is None or val <= 0:
            return
        if self.seen[engname].get(sem.name, 0) >= val:
            return
        eng.wait_ge(sem.h, val)
        self.seen[engname][sem.name] = val

    def build(self):
        nc = self.nc
        with self.es, nc.Block() as block:
            with contextlib.ExitStack() as qkv_scope:
                sbq = lambda *a, **k: qkv_scope.enter_context(nc.sbuf_tensor(*a, **k))
                self.qT = [sbq(f"qT{h}", [128, L], BF16) for h in range(2)]
                self.kT = [sbq(f"kT{h}", [128, L], BF16) for h in range(2)]
                self.V = [sbq(f"V{h}", [128, NT * 132], BF16) for h in range(2)]
                with contextlib.ExitStack() as p1_scope:
                    with nc.named_scope("p1"):
                        self.emit_p1(block, p1_scope)
                    if "p1" in self.debug and self.stage == "p1":
                        self.emit_p1_taps(block)
                    if self.stage == "ln":
                        self.emit_ln_taps(block)
                if self.stage in ("p2", "p3", "full"):
                    self.phase_barrier(block)
                    with contextlib.ExitStack() as p2_scope:
                        with nc.named_scope("p2"):
                            self.emit_p2(block, p2_scope)
            if self.stage in ("p3", "full"):
                self.phase_barrier(block)
                with contextlib.ExitStack() as s34:
                    with nc.named_scope("p3"):
                        self.emit_consts(block, s34)
                    with contextlib.ExitStack() as zscope:
                        self.zpT = zscope.enter_context(
                            nc.sbuf_tensor("zpT", [128, NDC * 1024], BF16))
                        with nc.named_scope("p3b"):
                            with contextlib.ExitStack() as p3_scope:
                                self.emit_p3(block, p3_scope)
                        if self.stage == "p3":
                            self.emit_p3_taps(block)
                        else:
                            with nc.named_scope("p4"):
                                self.emit_p4a(block)
                    if self.stage != "p3":
                        with contextlib.ExitStack() as p4_scope:
                            with nc.named_scope("p4b"):
                                self.emit_p4b(block, p4_scope)
            self.emit_final_wait(block)
        return nc

    def emit_p3_taps(self, block):
        nc, W = self.nc, self.W
        o_x2 = nc.dram_tensor("o_x2", [1024, D], F32, kind="ExternalOutput").ap()
        o_zpT = nc.dram_tensor("o_zpT", [128, NDC * 1024], BF16,
                               kind="ExternalOutput").ap()
        cvt3 = nc.alloc_sbuf_tensor("cvt3", [128, D], F32)

        @block.sync
        def _(sy):
            W(sy, "sync", self.s_x2[0], self.wp["x2_done0"])
            W(sy, "sync", self.s_x2[1], self.wp["x2_done1"])
            W(sy, "sync", self.s_dve, self.wp["zpT_done"])
            for c in range(8):
                self.s_st.inc(sy.dma_start(
                    out=o_x2[c * 128:(c + 1) * 128, :],
                    in_=self.x2dram[c * 128:(c + 1) * 128, :]), 16)
            self.s_st.inc(sy.dma_start(out=o_zpT, in_=self.zpT[:]), 16)

    def phase_barrier(self, block):
        W = self.W
        marks = [(self.s_pe, self.s_pe.c), (self.s_dve, self.s_dve.c),
                 (self.s_act, self.s_act.c), (self.s_pool, self.s_pool.c),
                 (self.s_st, self.s_st.c), (self.s_cast, self.s_cast.c),
                 (self.s_ldw, self.s_ldw.c), (self.s_zf, self.s_zf.c),
                 (self.s_yd, self.s_yd.c)]
        marks += [(sw, sw.c) for sw in self.s_yw]
        marks = [(sem, cnt) for (sem, cnt) in marks if cnt > 0]
        for engname, sec in (("sync", block.sync), ("gpsimd", block.gpsimd),
                             ("vector", block.vector), ("tensor", block.tensor),
                             ("scalar", block.scalar)):
            @sec
            def _(e, engname=engname):
                for sem, cnt in marks:
                    W(e, engname, sem, cnt)

    def emit_ln_taps(self, block):
        nc = self.nc
        o_lnxT = nc.dram_tensor("o_lnxT", [128, NDC * 1024], BF16,
                                kind="ExternalOutput").ap()
        @block.scalar
        def _(sc):
            self.W(sc, "scalar", self.s_dve, self.wp_trev_last)
            self.s_st.inc(sc.dma_start(out=o_lnxT, in_=self.lnxT[:]), 16)

    def emit_p1(self, block, scope):
        wp = self.wp
        W = self.W
        nc = self.nc
        sb = lambda *args, **kw: scope.enter_context(nc.sbuf_tensor(*args, **kw))
        ps = lambda *args, **kw: scope.enter_context(nc.psum_tensor(*args, **kw))
        self.xr = sb("xr", [128, 3 * D], F32)
        self.lnx = sb("lnx", [128, 2 * D], BF16)
        self.lnxT2 = [sb(f"lnxT{i}", [128, NDC * 512], BF16) for i in range(2)]
        self.wqkvT = sb("wqkvT", [128, NDC * 768], BF16)
        self.cosb = sb("cosb", [128, 2 * 512], BF16)
        self.sinb = sb("sinb", [128, 2 * 512], BF16)
        self.sums = sb("sums", [128, 8], F32)
        self.sumsq = sb("sumsq", [128, 8], F32)
        self.stats = sb("stats", [128, 2 * 16], F32)
        self.sqscrap = sb("sqscrap", [128, D], BF16)
        self.qc = [sb(f"qc{i}", [128, 512], BF16) for i in range(2)]
        self.rs = [sb(f"rs{i}", [128, 512], BF16) for i in range(2)]
        self.roplain = [sb(f"roplain{i}", [128, 512], BF16) for i in range(2)]
        self.spmT = sb("spmT", [128, 128], BF16)
        self.p_tr = [ps(f"p_tr{i}", [128, 512], BF16) for i in range(2)]
        self.p_trv = ps("p_trv", [128, 512], BF16)

        self.p_qkv = [ps(f"p_qkv{i}", [128, 512], F32) for i in range(3)]
        self.p_rot = ps("p_rot", [128, 512], F32)

        if self.preload:
            @block.scalar
            def _(sc):
                sc.preload_act_table(AF.Sqrt)

        @block.gpsimd
        def _(g):
            self.s_cast.inc(g.dma_start(out=self.wqkv16, in_=self.wqkv), 16)
        wp["wqkv16"] = self.s_cast.c

        @block.sync
        def _(s):
            self.s_ldc.inc(s.dma_start(out=self.sident[:], in_=self.ident), 16)
            self.s_ldc.inc(s.dma_start(out=self.spmT[:], in_=self.pmT), 16)
            W(s, "sync", self.s_cast, wp["wqkv16"])
            for et in range(6):
                buf = self.lnxT2[0] if et < 4 else self.lnxT2[1]
                off = (et if et < 4 else et - 4) * D
                self.s_ldw.inc(s.dma_start(
                    out=buf[:, off:off + D],
                    in_=self.wqkv16[et * 128:(et + 1) * 128, :]), 16)
        wp["wq_st"] = self.s_ldw.c
        wp["consts"] = self.s_ldc.c

        for et in range(6):
            for dc2 in range(4):
                @block.tensor
                def _(t, et=et, dc2=dc2):
                    W(t, "tensor", self.s_ldw, wp["wq_st"])
                    W(t, "tensor", self.s_ldc, wp["consts"])
                    k = f"wqev_{et * 4 + dc2 - 2}"
                    if k in wp:
                        W(t, "tensor", self.s_dve, wp[k])
                    buf = self.lnxT2[0] if et < 4 else self.lnxT2[1]
                    off = (et if et < 4 else et - 4) * D
                    for j in range(4):
                        dc = dc2 * 4 + j
                        last = t.transpose(
                            self.p_tr[dc2 % 2][:, j * 128:(j + 1) * 128],
                            buf[:, off + dc * 128:off + (dc + 1) * 128],
                            self.sident[:])
                    self.s_pe.inc(last)
                    wp[f"wqtr_{et * 4 + dc2}"] = self.s_pe.c

                @block.vector
                def _(v, et=et, dc2=dc2):
                    W(v, "vector", self.s_pe, wp[f"wqtr_{et * 4 + dc2}"])
                    out_ap = self.wqkvT.ap().rearrange(
                        "p (dc e) -> p dc e", e=768)[
                        :, dc2 * 4:(dc2 + 1) * 4, et * 128:(et + 1) * 128]
                    in_ap = self.p_tr[dc2 % 2].ap().rearrange(
                        "p (a b) -> p a b", a=4)
                    wp[f"wqev_{et * 4 + dc2}"] = self.s_dve.inc(
                        v.tensor_copy(out_ap, in_ap))
        wp["wqkvT"] = self.s_dve.c
        wp["wq_pe"] = self.s_pe.c
        wp["wqlast0"] = wp["wqev_22"]
        wp["wqlast1"] = wp["wqev_23"]

        @block.vector
        def _(v):
            v.memset(self.ceps[:], EPS)
            v.memset(self.cnegb0[:], -B0)
            for h in range(2):
                ones = self.V[h].ap().rearrange("p (t c) -> p t c", c=132)[:, :, 128:129]
                v.memset(ones, 1.0)
            self.s_dve.inc(v.engine_nop())
        self.wp["vones"] = self.s_dve.c

        ldx_after = {}
        norm_after = {}
        sum_after = {}
        sq_after = {}
        trev_after = {}
        cos_after = {}
        self.rope_after = {}
        self.v_after = {}
        tr_slot_user = {0: wp["wqlast0"], 1: wp["wqlast1"]}
        pqkv_users = {0: [], 1: [], 2: []}
        roplain_users = {0: [], 1: []}
        qcrs_users = {0: [], 1: []}
        stats_rd = {0: None, 1: None}
        stats_rd_prev = {0: None, 1: None}
        sumsq_rd = {0: None, 1: None}
        p_rot_user = None
        p_trv_user = None

        for g in range(NG):
            for tau in range(8):
                T = g * 8 + tau

                @block.sync
                def _(s, T=T):
                    if T >= 3:
                        W(s, "sync", self.s_act, norm_after[T - 3])
                    self.s_ldx[T % 4].inc(s.dma_start(
                        out=self.xr[0:64, (T % 3) * D:(T % 3 + 1) * D],
                        in_=self.x[T * 128:T * 128 + 64, :]), 16)

                @block.scalar
                def _(sc, T=T):
                    if T >= 3:
                        W(sc, "scalar", self.s_act, norm_after[T - 3])
                    ldx_after[T] = self.s_ldx[T % 4].inc(sc.dma_start(
                        out=self.xr[64:128, (T % 3) * D:(T % 3 + 1) * D],
                        in_=self.x[T * 128 + 64:(T + 1) * 128, :]), 16)

                @block.vector
                def _(v, T=T, tau=tau):
                    W(v, "vector", self.s_ldx[T % 4], ldx_after[T])
                    sum_after[T] = self.s_dve.inc(v.reduce_sum(
                        self.sums[:, tau:tau + 1],
                        self.xr[:, (T % 3) * D:(T % 3 + 1) * D], axis=AX.X))

                if tau % 2 != 1:
                    continue
                b = (tau // 2) % 2
                bt = tau - 1
                cols = slice(bt, bt + 2)
                mean = self.stats[:, 8 * b + 0:8 * b + 2]
                var = self.stats[:, 8 * b + 2:8 * b + 4]
                rstd = self.stats[:, 8 * b + 4:8 * b + 6]
                negmr = self.stats[:, 8 * b + 6:8 * b + 8]

                var_cnt = self.s_dve.c + 3
                negmr_cnt = self.s_dve.c + 6

                @block.scalar
                def _(sc, g=g, b=b, bt=bt, var=var, rstd=rstd, negmr=negmr,
                      var_cnt=var_cnt, negmr_cnt=negmr_cnt):
                    for tt in range(2):
                        TT = g * 8 + bt + tt
                        W(sc, "scalar", self.s_ldx[TT % 4], ldx_after[TT])
                        if TT > 0:
                            W(sc, "scalar", self.s_act, sq_after[TT - 1])
                        if sumsq_rd[b] is not None:
                            W(sc, "scalar", self.s_dve, sumsq_rd[b])
                        sq_after[TT] = self.s_act.inc(sc.activation(
                            self.sqscrap[:],
                            self.xr[:, (TT % 3) * D:(TT % 3 + 1) * D],
                            AF.Square,
                            accum_out=self.sumsq[:, bt + tt:bt + tt + 1]))
                    W(sc, "scalar", self.s_dve, var_cnt)
                    self.s_act.inc(sc.activation(rstd, var, AF.Sqrt,
                                                 bias=self.ceps[:, 0:1]))
                    wp[f"rstd_{g}_{tau}"] = self.s_act.c
                    W(sc, "scalar", self.s_dve, negmr_cnt)
                    for tt in range(2):
                        TT = g * 8 + bt + tt
                        if TT >= 2:
                            W(sc, "scalar", self.s_dve, trev_after[TT - 2])
                        norm_after[TT] = self.s_act.inc(sc.activation(
                            self.lnx[:, (TT % 2) * D:(TT % 2 + 1) * D],
                            self.xr[:, (TT % 3) * D:(TT % 3 + 1) * D],
                            AF.Identity, bias=negmr[:, tt:tt + 1],
                            scale=rstd[:, tt:tt + 1]))
                stats_rd[b] = self.s_act.c

                @block.vector
                def _(v, g=g, b=b, T=T, cols=cols, mean=mean, var=var):
                    W(v, "vector", self.s_dve, sum_after[T])
                    W(v, "vector", self.s_act, sq_after[T])
                    if stats_rd_prev[b] is not None:
                        W(v, "vector", self.s_act, stats_rd_prev[b])
                    c1 = self.s_dve.inc(v.tensor_scalar_mul(mean, self.sums[:, cols], 1.0 / D))
                    W(v, "vector", self.s_dve, c1)
                    c2 = self.s_dve.inc(v.tensor_tensor(self.sums[:, cols], mean, mean, ALU.mult))
                    W(v, "vector", self.s_dve, c2)
                    sumsq_rd[b] = self.s_dve.inc(v.scalar_tensor_tensor(
                        var, self.sumsq[:, cols], 1.0 / D, self.sums[:, cols],
                        ALU.mult, ALU.subtract))
                assert self.s_dve.c == var_cnt

                @block.vector
                def _(v, g=g, b=b, var=var, mean=mean, rstd=rstd, negmr=negmr):
                    W(v, "vector", self.s_act, wp[f"rstd_{g}_{tau}"])
                    c1 = self.s_dve.inc(v.reciprocal(var, rstd))
                    W(v, "vector", self.s_dve, c1)
                    c1 = self.s_dve.inc(v.tensor_copy(rstd, var))
                    W(v, "vector", self.s_dve, c1)
                    self.s_dve.inc(v.scalar_tensor_tensor(
                        negmr, mean, -1.0, rstd, ALU.mult, ALU.mult))
                assert self.s_dve.c == negmr_cnt
                stats_rd_prev[b] = stats_rd[b]

                for tt in range(2):
                    TT = g * 8 + bt + tt
                    tau2 = bt + tt

                    for pb in range(4):
                        @block.tensor
                        def _(t, TT=TT, pb=pb):
                            W(t, "tensor", self.s_ldc, wp["consts"])
                            W(t, "tensor", self.s_act, norm_after[TT])
                            u = tr_slot_user[pb % 2]
                            if u is not None:
                                W(t, "tensor", self.s_dve, u)
                            for j in range(4):
                                dc = pb * 4 + j
                                last = t.transpose(
                                    self.p_tr[pb % 2][:, j * 128:(j + 1) * 128],
                                    self.lnx[:, (TT % 2) * D + dc * 128:
                                             (TT % 2) * D + (dc + 1) * 128],
                                    self.sident[:])
                            self.s_pe.inc(last)
                            self.wp[f"tr_{TT}_{pb}"] = self.s_pe.c

                        @block.vector
                        def _(v, g=g, TT=TT, tau2=tau2, pb=pb):
                            W(v, "vector", self.s_pe, self.wp["wq_pe"])
                            W(v, "vector", self.s_pe, self.wp[f"tr_{TT}_{pb}"])
                            hb2 = g * 2 + tau2 // 4
                            if f"qkvmm_{hb2 - 2}_5" in self.wp:
                                W(v, "vector", self.s_pe,
                                  self.wp[f"qkvmm_{hb2 - 2}_5"])
                            out_ap = self.lnxT2[hb2 % 2].ap().rearrange(
                                "p (dc t) -> p dc t", dc=NDC)[
                                :, pb * 4:(pb + 1) * 4,
                                (tau2 % 4) * 128:(tau2 % 4 + 1) * 128]
                            in_ap = self.p_tr[pb % 2].ap().rearrange(
                                "p (a b) -> p a b", a=4)
                            tr_slot_user[pb % 2] = self.s_dve.inc(
                                v.tensor_copy(out_ap, in_ap))
                            trev_after[TT] = self.s_dve.c
                            self.wp_trev_last = self.s_dve.c

            if self.stage in ("p3", "full"):
                if g == 0:
                    self.emit_ffn_casts(block)
                    self.emit_pretranspose(block)
                self.pump_casts(block, 2)

            for b in (range(2) if self.stage != "ln" else range(0)):
                hb = g * 2 + b
                tok0 = g * 1024 + b * 512

                @block.sync
                def _(s, hb=hb, tok0=tok0):
                    if hb >= 2:
                        W(s, "sync", self.s_pool, self.rope_after.get(hb - 2))
                        W(s, "sync", self.s_dve, self.wp.get(f"qcrs_done_{hb - 2}"))
                    sl = (hb % 2) * 512
                    self.s_cs[hb % 2].inc(s.dma_start(out=self.cosb[:, sl:sl + 512],
                                                      in_=self.cos_t[:, tok0:tok0 + 512]), 16)
                    cos_after[hb] = self.s_cs[hb % 2].inc(
                        s.dma_start(out=self.sinb[:, sl:sl + 512],
                                    in_=self.sin_t[:, tok0:tok0 + 512]), 16)

                def rope_post(et):
                    slot = et % 3
                    rsl = et % 2
                    sec, h = et // 2, et % 2
                    if sec < 2:
                        dstT = (self.qT if sec == 0 else self.kT)[h]

                        @block.vector
                        def _(v, hb=hb, et=et, slot=slot, rsl=rsl):
                            W(v, "vector", self.s_pe, wp[f"qkvmm_{hb}_{et}"])
                            for (sem, cnt) in roplain_users[rsl]:
                                W(v, "vector", sem, cnt)
                            roplain_users[rsl] = []
                            cnt = self.s_dve.inc(v.tensor_copy(
                                self.roplain[rsl][:], self.p_qkv[slot][:]))
                            wp[f"qkvevac_{hb}_{et}"] = cnt

                        @block.tensor
                        def _(t, hb=hb, et=et, slot=slot, rsl=rsl):
                            nonlocal p_rot_user
                            W(t, "tensor", self.s_dve, wp[f"qkvevac_{hb}_{et}"])
                            if p_rot_user is not None:
                                W(t, "tensor", self.s_dve, p_rot_user)
                            mm = t.matmul(self.p_rot[:], self.spmT[:],
                                          self.roplain[rsl][:], start=True, stop=True)
                            cnt = self.s_pe.inc(mm)
                            roplain_users[rsl].append((self.s_pe, cnt))
                            wp[f"rot_{hb}_{et}"] = cnt

                        @block.vector
                        def _(v, hb=hb, et=et, slot=slot, rsl=rsl):
                            nonlocal p_rot_user
                            sl = (hb % 2) * 512
                            W(v, "vector", self.s_cs[hb % 2], cos_after[hb])
                            for (sem, cnt) in qcrs_users[rsl]:
                                W(v, "vector", sem, cnt)
                            qcrs_users[rsl] = []
                            cnt = self.s_dve.inc(v.tensor_tensor(
                                self.qc[rsl][:], self.p_qkv[slot][:],
                                self.cosb[:, sl:sl + 512], ALU.mult))
                            pqkv_users[slot].append((self.s_dve, cnt))
                            W(v, "vector", self.s_pe, wp[f"rot_{hb}_{et}"])
                            cnt = self.s_dve.inc(v.tensor_tensor(
                                self.rs[rsl][:], self.p_rot[:],
                                self.sinb[:, sl:sl + 512], ALU.mult))
                            p_rot_user = cnt
                            wp[f"rs_{hb}_{et}"] = cnt
                            wp[f"qcrs_done_{hb}"] = cnt

                        @block.gpsimd
                        def _(gp, hb=hb, et=et, rsl=rsl, dstT=dstT, tok0=tok0):
                            W(gp, "gpsimd", self.s_dve, wp[f"rs_{hb}_{et}"])
                            cnt = self.s_pool.inc(gp.tensor_tensor(
                                dstT[:, tok0:tok0 + 512], self.qc[rsl][:],
                                self.rs[rsl][:], ALU.add))
                            qcrs_users[rsl].append((self.s_pool, cnt))
                            self.rope_after[hb] = cnt
                    else:
                        h = et % 2

                        @block.vector
                        def _(v, hb=hb, et=et, slot=slot, rsl=rsl):
                            W(v, "vector", self.s_pe, wp[f"qkvmm_{hb}_{et}"])
                            for (sem, cnt) in roplain_users[rsl]:
                                W(v, "vector", sem, cnt)
                            roplain_users[rsl] = []
                            cnt = self.s_dve.inc(v.tensor_copy(
                                self.roplain[rsl][:], self.p_qkv[slot][:]))
                            pqkv_users[slot].append((self.s_dve, cnt))
                            wp[f"qkvevac_{hb}_{et}"] = cnt

                        @block.tensor
                        def _(t, hb=hb, et=et, rsl=rsl):
                            nonlocal p_trv_user
                            W(t, "tensor", self.s_dve, wp[f"qkvevac_{hb}_{et}"])
                            if p_trv_user is not None:
                                W(t, "tensor", self.s_dve, p_trv_user)
                            for j in range(4):
                                last = t.transpose(self.p_trv[:, j * 128:(j + 1) * 128],
                                                   self.roplain[rsl][:, j * 128:(j + 1) * 128],
                                                   self.sident[:])
                            cnt = self.s_pe.inc(last)
                            roplain_users[rsl].append((self.s_pe, cnt))
                            wp[f"vtr_{hb}_{et}"] = cnt

                        @block.vector
                        def _(v, hb=hb, et=et, h=h, tok0=tok0):
                            nonlocal p_trv_user
                            W(v, "vector", self.s_pe, wp[f"vtr_{hb}_{et}"])
                            t0 = tok0 // 128
                            dst = self.V[h].ap().rearrange(
                                "p (t c) -> p t c", c=132)[:, t0:t0 + 4, 0:128]
                            src = self.p_trv.ap().rearrange("p (a b) -> p a b", a=4)
                            p_trv_user = self.s_dve.inc(v.tensor_copy(dst, src))
                            self.v_after[hb] = p_trv_user

                for et in range(6):
                    slot = et % 3

                    @block.tensor
                    def _(t, g=g, b=b, hb=hb, et=et, slot=slot):
                        if et == 0:
                            W(t, "tensor", self.s_dve, wp["wqkvT"])
                            W(t, "tensor", self.s_dve, trev_after[g * 8 + b * 4 + 3])
                        for (sem, cnt) in pqkv_users[slot]:
                            W(t, "tensor", sem, cnt)
                        pqkv_users[slot] = []
                        for dc in range(NDC):
                            last = t.matmul(self.p_qkv[slot][:],
                                            self.wqkvT[:, dc * 768 + et * 128:
                                                       dc * 768 + (et + 1) * 128],
                                            self.lnxT2[hb % 2][:, dc * 512:
                                                               (dc + 1) * 512],
                                            start=(dc == 0), stop=(dc == NDC - 1))
                        self.s_pe.inc(last)
                        wp[f"qkvmm_{hb}_{et}"] = self.s_pe.c

                    if et >= 1:
                        rope_post(et - 1)
                rope_post(5)


    def emit_p2(self, block, scope):
        nc, wp, W = self.nc, self.wp, self.W
        sb = lambda *a, **k: scope.enter_context(nc.sbuf_tensor(*a, **k))
        ps = lambda *a, **k: scope.enter_context(nc.psum_tensor(*a, **k))

        p_S = [ps(f"p_S{i}", [128, 1024], F32) for i in range(2)]
        p_yq = [ps(f"p_yq{i}", [128, 132], F32) for i in range(2)]
        Pbuf = [sb(f"Pbuf{i}", [128, 1024], BF16) for i in range(2)]
        ystage = [sb(f"ystage{i}", [128, 128], BF16) for i in range(16)]
        sd_sb = [sb(f"sd_sb{i}", [128, 128], F32) for i in range(2)]
        mask_sb = sb("mask_sb", [128, 128], F32)
        st2 = sb("st2", [128, 4], F32)
        RSG = 0

        self.mask = nc.dram_tensor("mask", [128, 128], F32, kind="ExternalInput").ap()
        self.sidx = nc.dram_tensor("sidx", [128, 16], I32, kind="ExternalInput").ap()
        self.yin = [nc.dram_tensor(f"yin{c}", [1024, D], BF16, kind="Internal").ap()
                    for c in range(8)]
        self.rsout = [nc.dram_tensor(f"rsout{c}", [128, D], BF16, kind="Internal").ap()
                      for c in range(8)]
        si_sb = sb("si_sb", [128, 16], I32)
        zt_sb = sb("zt_sb", [128, D], BF16)

        if self.preload:
            @block.scalar
            def _(sc):
                W(sc, "scalar", self.s_act, self.s_act.c)
                sc.preload_act_table(AF.Exp)

        @block.sync
        def _(sy):
            self.s_ldc.inc(sy.dma_start(out=mask_sb[:], in_=self.mask), 16)
            self.s_ldc.inc(sy.dma_start(out=si_sb[:], in_=self.sidx), 16)
        wp["mask"] = self.s_ldc.c

        @block.vector
        def _(v):
            wp["zt"] = self.s_dve.inc(v.memset(zt_sb[:], 0.0))

        @block.scalar
        def _(sc):
            W(sc, "scalar", self.s_dve, wp["zt"])
            for c in range(8):
                for r in range(8):
                    self.s_zf.inc(sc.dma_start(
                        out=self.yin[c][r * 128:(r + 1) * 128, :], in_=zt_sb[:]), 16)
        wp["zf"] = self.s_zf.c

        u_pS = {0: [], 1: []}
        u_Pb = {0: [], 1: []}
        u_yq = {0: [], 1: []}
        u_sd = []
        self._last_yst = 0

        flat = []
        unit_of = []
        u = 0
        for tau in range(NT):
            for h in range(2):
                nfull = tau // 8
                for j in range(nfull + 1):
                    flat.append((tau, h, j, nfull + 1))
                    unit_of.append(u)
                u += 1

        def emit_S(idx):
            tau, h, j, nb = flat[idx]
            slot = idx % 2
            k0 = j * 1024
            Wd = 1024 if j < nb - 1 else (tau % 8 + 1) * 128
            nch = Wd // 128

            @block.tensor
            def _(t):
                for (sem, cnt) in u_pS[slot]:
                    W(t, "tensor", sem, cnt)
                u_pS[slot].clear()
                for jj in range(nch):
                    last = t.matmul(p_S[slot][:, jj * 128:(jj + 1) * 128],
                                    self.kT[h][:, k0 + jj * 128:k0 + (jj + 1) * 128],
                                    self.qT[h][:, tau * 128:(tau + 1) * 128],
                                    start=True, stop=True)
                self.s_pe.inc(last)
                wp[f"S_{idx}"] = self.s_pe.c

        def emit_sd(idx):
            tau, h, j, nb = flat[idx]
            slot = idx % 2
            is_last = (j == nb - 1)
            if not is_last:
                return
            Wd = (tau % 8 + 1) * 128

            @block.vector
            def _(v):
                W(v, "vector", self.s_ldc, wp["mask"])
                W(v, "vector", self.s_pe, wp[f"S_{idx}"])
                for (sem, cnt) in u_sd:
                    W(v, "vector", sem, cnt)
                u_sd.clear()
                wp[f"sd_{idx}"] = self.s_dve.inc(v.tensor_tensor(
                    sd_sb[idx % 2][:], p_S[slot][:, Wd - 128:Wd],
                    mask_sb[:], ALU.add))
                u_pS[slot].append((self.s_dve, wp[f"sd_{idx}"]))

        def emit_exp(sc, idx):
            tau, h, j, nb = flat[idx]
            slot = idx % 2
            is_last = (j == nb - 1)
            Wd = 1024 if not is_last else (tau % 8 + 1) * 128
            W(sc, "scalar", self.s_pe, wp[f"S_{idx}"])
            W(sc, "scalar", self.s_dve, self.wp["vones"])
            for (sem, cnt) in u_Pb[slot]:
                W(sc, "scalar", sem, cnt)
            u_Pb[slot].clear()
            bias = self.cnegb0[:, 0:1]
            if is_last:
                W(sc, "scalar", self.s_dve, wp[f"sd_{idx}"])
                if Wd > 128:
                    cnt = self.s_act.inc(sc.activation(
                        Pbuf[slot][:, 0:Wd - 128], p_S[slot][:, 0:Wd - 128],
                        AF.Exp, bias=bias, scale=CSC))
                    u_pS[slot].append((self.s_act, cnt))
                cnt = self.s_act.inc(sc.activation(
                    Pbuf[slot][:, Wd - 128:Wd], sd_sb[idx % 2][:],
                    AF.Exp, bias=bias, scale=CSC))
                u_sd.append((self.s_act, cnt))
            else:
                cnt = self.s_act.inc(sc.activation(
                    Pbuf[slot][:, 0:Wd], p_S[slot][:, 0:Wd],
                    AF.Exp, bias=bias, scale=CSC))
                u_pS[slot].append((self.s_act, cnt))
            wp[f"exp_{idx}"] = self.s_act.c

        def emit_chain(idx):
            nonlocal ycount
            tau, h, j, nb = flat[idx]
            up = unit_of[idx] % 2
            slot = idx % 2
            k0 = j * 1024
            is_last = (j == nb - 1)
            Wd = 1024 if not is_last else (tau % 8 + 1) * 128
            nch = Wd // 128

            @block.tensor
            def _(t):
                W(t, "tensor", self.s_act, wp[f"exp_{idx}"])
                yslot = unit_of[idx] % 2
                if j == 0:
                    for (sem, cnt) in u_yq[yslot]:
                        W(t, "tensor", sem, cnt)
                    u_yq[yslot].clear()
                for jj in range(nch):
                    kt = k0 // 128 + jj
                    last = t.matmul(
                        p_yq[yslot][:, 0:129],
                        Pbuf[slot][:, jj * 128:(jj + 1) * 128],
                        self.V[h].ap().rearrange("p (t c) -> p t c", c=132)[
                            :, kt, 0:129],
                        start=(j == 0 and jj == 0),
                        stop=(is_last and jj == nch - 1))
                cnt = self.s_pe.inc(last)
                u_Pb[slot].append((self.s_pe, cnt))
                wp[f"YQ_{idx}"] = cnt

            if is_last:
                ysl = (tau % 8) * 2 + h
                yslot = unit_of[idx] % 2
                @block.vector
                def _(v):
                    W(v, "vector", self.s_pe, wp[f"YQ_{idx}"])
                    W(v, "vector", self.s_dve, self._last_yst)
                    c1 = self.s_dve.inc(v.reciprocal(
                        st2[:, RSG:RSG + 1], p_yq[yslot][:, 128:129]))
                    W(v, "vector", self.s_dve, c1)
                    if tau >= 8:
                        W(v, "vector", self.s_yw[tau // 8 - 1],
                          self.s_yw[tau // 8 - 1].c)
                        if "p2y" in self.debug:
                            W(v, "vector", self.s_yd, self.s_yd.c)
                    cnt = self.s_dve.inc(v.tensor_scalar_mul(
                        ystage[ysl][:], p_yq[yslot][:, 0:128],
                        st2[:, RSG:RSG + 1]))
                    u_yq[yslot].append((self.s_dve, cnt))
                    wp[f"yst_{idx}"] = cnt
                    self._last_yst = cnt

                @block.gpsimd
                def _(gp):
                    W(gp, "gpsimd", self.s_dve, wp[f"yst_{idx}"])
                    W(gp, "gpsimd", self.s_ldc, wp["mask"])
                    W(gp, "gpsimd", self.s_zf, wp["zf"])
                    c = tau // 8
                    yview = self.yin[c].rearrange("a (s b) -> (a s) b", b=128)
                    kcol = (tau % 8) * 2 + h
                    self.s_yw[c].inc(gp.indirect_dma_start(
                        out=yview,
                        out_offset=IndirectOffsetOnAxis(si_sb[:, kcol:kcol + 1], 0),
                        in_=ystage[ysl][:], in_offset=None), 16)

                if "p2y" in self.debug:
                    @block.scalar
                    def _(sc):
                        W(sc, "scalar", self.s_dve, wp[f"yst_{idx}"])
                        self.s_yd.inc(sc.dma_start(
                            out=self.o_y[h, tau * 128:(tau + 1) * 128, :],
                            in_=ystage[ysl][:]), 16)
                ycount += 1

        if "p2y" in self.debug:
            self.o_y = nc.dram_tensor("o_y", [2, L, 128], BF16,
                                      kind="ExternalOutput").ap()

        ycount = 0
        nf = len(flat)
        emit_S(0)
        if nf > 1:
            emit_S(1)
        p = 0
        while p < nf:
            i0, i1 = p, (p + 1 if p + 1 < nf else None)
            emit_sd(i0)
            if i1 is not None:
                emit_sd(i1)

            @block.scalar
            def _(sc, i0=i0, i1=i1):
                emit_exp(sc, i0)
                if i1 is not None:
                    emit_exp(sc, i1)

            if p + 2 < nf:
                emit_S(p + 2)
            if p + 3 < nf:
                emit_S(p + 3)
            emit_chain(i0)
            if i1 is not None:
                emit_chain(i1)
            self.pump_pt(block, 3)
            p += 2
        self.finish_pt(block)

        nocc = self.nocc
        for c in range(8):
            @block.gpsimd
            def _(gp, c=c):
                W(gp, "gpsimd", self.s_yw[c], self.s_yw[c].c)
                if nocc:
                    gp.dma_start(out=self.rsout[c], in_=self.yin[c][0:128, :]
                                 ).then_inc(self.s_cc[c].h, 1, True)
                else:
                    gp.collective_compute(
                        "ReduceScatter", ALU.add, replica_groups=[list(range(8))],
                        ins=[self.yin[c]], outs=[self.rsout[c]],
                    ).then_inc(self.s_cc[c].h, 1)
                self.s_cc[c].c += 1

    def emit_consts(self, block, scope):
        nc, wp, W = self.nc, self.wp, self.W
        sb = lambda *a, **k: scope.enter_context(nc.sbuf_tensor(*a, **k))
        d = nc.dram_tensor
        self.i_sug = d("su_ug", [16, 128], F32, kind="ExternalInput").ap()
        self.i_sdp = d("su_dp", [64, 128], F32, kind="ExternalInput").ap()
        self.i_wug = d("wscale_ug", [128, 128], F32, kind="ExternalInput").ap()
        self.i_wdp = d("wscale_dp", [16, 128], F32, kind="ExternalInput").ap()
        self.i_h128 = d("h128", [128, 128], BF16, kind="ExternalInput").ap()

        self.sugT = sb("sugT", [128, 16], F32)
        self.sdpT = sb("sdpT", [128, 64], F32)
        self.wugT = sb("wugT", [128, 128], F32)
        self.wdpT = sb("wdpT", [128, 16], F32)
        self.h128_sb = sb("h128_sb", [128, 128], BF16)
        self.H1 = sb("H1", [128, 16 * 128], BF16)
        self.H2 = sb("H2", [128, 64 * 128], BF16)

        ld = sb("cstld", [128, 512], F32)
        with nc.psum_tensor("p_cst", [128, 128], F32) as p_cst:
            @block.sync
            def _(sy):
                self.s_ldc.inc(sy.dma_start(out=self.h128_sb[:], in_=self.i_h128), 16)
                self.s_ldc.inc(sy.dma_start(out=ld[0:16, 0:128], in_=self.i_sug), 16)
                self.s_ldc.inc(sy.dma_start(out=ld[0:64, 128:256],
                                            in_=self.i_sdp), 16)
                self.s_ldc.inc(sy.dma_start(out=ld[0:16, 256:384],
                                            in_=self.i_wdp), 16)
                self.s_ldc.inc(sy.dma_start(out=ld[0:128, 384:512],
                                            in_=self.i_wug), 16)
            wp["cst_ld"] = self.s_ldc.c

            idf = sb("idf", [128, 128], F32)

            @block.vector
            def _(v):
                W(v, "vector", self.s_ldc, wp["consts"])
                wp["idf"] = self.s_dve.inc(v.tensor_copy(idf[:], self.sident[:]))

            jobs = [
                (ld[0:16, 0:128], self.sugT, 16, 1.0),
                (ld[0:64, 128:256], self.sdpT, 64, 1.0),
                (ld[0:16, 256:384], self.wdpT, 16, float(1.0 / np.sqrt(8192.0))),
                (ld[0:128, 384:512], self.wugT, 128, float(1.0 / np.sqrt(2048.0))),
            ]
            for (src, dst, nrow, scl) in jobs:
                @block.tensor
                def _(t, src=src, nrow=nrow):
                    W(t, "tensor", self.s_ldc, wp["cst_ld"])
                    W(t, "tensor", self.s_dve, wp["idf"])
                    W(t, "tensor", self.s_dve, wp.get("cst_ev", 0))
                    wp["cst_tr"] = self.s_pe.inc(
                        t.transpose(p_cst[0:128, 0:nrow], src,
                                    idf[0:nrow, 0:nrow]))

                @block.vector
                def _(v, dst=dst, nrow=nrow, scl=scl):
                    W(v, "vector", self.s_pe, wp["cst_tr"])
                    wp["cst_ev"] = self.s_dve.inc(v.tensor_scalar_mul(
                        dst[:, 0:nrow], p_cst[:, 0:nrow], scl))
            wp["cst_done"] = self.s_dve.c

            @block.vector
            def _(v):
                W(v, "vector", self.s_dve, wp["cst_done"])
                W(v, "vector", self.s_ldc, wp["cst_ld"])
                for c in range(16):
                    v.tensor_scalar_mul(self.H1[:, c * 128:(c + 1) * 128],
                                        self.h128_sb[:], self.sugT[:, c:c + 1])
                for c in range(64):
                    last = v.tensor_scalar_mul(self.H2[:, c * 128:(c + 1) * 128],
                                               self.h128_sb[:], self.sdpT[:, c:c + 1])
                wp["Hmats"] = self.s_dve.inc(last)

    def emit_p3(self, block, scope):
        nc, wp, W = self.nc, self.wp, self.W
        sb = lambda *a, **k: scope.enter_context(nc.sbuf_tensor(*a, **k))
        ps = lambda *a, **k: scope.enter_context(nc.psum_tensor(*a, **k))
        d = nc.dram_tensor

        self.i_xres = d("xres", [1024, D], F32, kind="ExternalInput").ap()
        self.x2dram = d("x2dram", [1024, D], F32, kind="Internal").ap()

        woT = sb("woT", [128, NDC * D], BF16)
        ytm = [sb(f"ytm{i}", [128, D], BF16) for i in range(2)]
        yT = [sb(f"yT{i}", [128, D], BF16) for i in range(2)]
        xrs = [sb(f"xrs{i}", [128, D], F32) for i in range(2)]
        x2t = [sb(f"x2t{i}", [128, D], F32) for i in range(2)]
        zt = [sb(f"zt{i}", [128, D], BF16) for i in range(2)]
        zTt = [sb(f"zTt{i}", [128, D], BF16) for i in range(2)]
        s1 = sb("s1", [128, D], BF16)
        s2 = sb("s2", [128, D], BF16)
        st3 = sb("st3", [128, 8], F32)
        sq3 = sb("sq3", [128, D], BF16)

        p_tr3 = [ps(f"p_tr3{i}", [128, 512], BF16) for i in range(2)]
        p_wo = ps("p_wo", [128, D], F32)
        p_f1 = [ps(f"p_f1{i}", [128, 512], F32) for i in range(2)]

        @block.sync
        def _(sy):
            W(sy, "sync", self.s_cw, wp["wo16"])
            for dc in range(NDC):
                for q in range(4):
                    self.s_ldw.inc(sy.dma_start_transpose(
                        out=woT[:, dc * D + q * 512:dc * D + (q + 1) * 512],
                        in_=self.wo16[q * 512:(q + 1) * 512,
                                      dc * 128:(dc + 1) * 128]), 16)
        wp["woT"] = self.s_ldw.c

        u_ytm = {0: [], 1: []}
        u_yT = {0: [], 1: []}
        u_tr3 = {0: [], 1: []}
        u_xrs = {0: [], 1: []}
        u_x2t = {0: [], 1: []}
        u_zt = {0: [], 1: []}
        u_zTt = {0: [], 1: []}
        u_pwo = []
        u_f1 = {0: [], 1: []}
        u_s1 = []
        u_s2 = []

        for c in range(8):
            sl = c % 2
            @block.sync
            def _(sy, c=c, sl=sl):
                W(sy, "sync", self.s_cc[c], 1)
                for (sem, cnt) in u_ytm[sl]:
                    W(sy, "sync", sem, cnt)
                u_ytm[sl].clear()
                self.s_l3[sl].inc(sy.dma_start(out=ytm[sl][:], in_=self.rsout[c]), 16)
                for (sem, cnt) in u_xrs[sl]:
                    W(sy, "sync", sem, cnt)
                u_xrs[sl].clear()
                self.s_l3[sl].inc(sy.dma_start(
                    out=xrs[sl][:], in_=self.i_xres[c * 128:(c + 1) * 128, :]), 16)
                wp[f"l3_{c}"] = self.s_l3[sl].c

            for pb in range(4):
                @block.tensor
                def _(t, c=c, sl=sl, pb=pb):
                    W(t, "tensor", self.s_l3[sl], wp[f"l3_{c}"])
                    W(t, "tensor", self.s_dve, wp["Hmats"])
                    for (sem, cnt) in u_tr3[pb % 2]:
                        W(t, "tensor", sem, cnt)
                    u_tr3[pb % 2].clear()
                    if pb == 0:
                        for (sem, cnt) in u_yT[sl]:
                            W(t, "tensor", sem, cnt)
                        u_yT[sl].clear()
                    for j in range(4):
                        dc = pb * 4 + j
                        last = t.transpose(p_tr3[pb % 2][:, j * 128:(j + 1) * 128],
                                           ytm[sl][:, dc * 128:(dc + 1) * 128],
                                           self.sident[:])
                    cnt = self.s_pe.inc(last)
                    u_ytm[sl].append((self.s_pe, cnt))
                    wp[f"ytr_{c}_{pb}"] = cnt

                @block.vector
                def _(v, c=c, sl=sl, pb=pb):
                    W(v, "vector", self.s_pe, wp[f"ytr_{c}_{pb}"])
                    cnt = self.s_dve.inc(v.tensor_copy(
                        yT[sl][:, pb * 512:(pb + 1) * 512], p_tr3[pb % 2][:]))
                    u_tr3[pb % 2].append((self.s_dve, cnt))
                    wp[f"yev_{c}_{pb}"] = cnt

            @block.tensor
            def _(t, c=c, sl=sl):
                W(t, "tensor", self.s_ldw, wp["woT"])
                W(t, "tensor", self.s_dve, wp[f"yev_{c}_3"])
                for (sem, cnt) in u_pwo:
                    W(t, "tensor", sem, cnt)
                u_pwo.clear()
                for dc in range(NDC):
                    for ec in range(4):
                        last = t.matmul(
                            p_wo[:, ec * 512:(ec + 1) * 512],
                            yT[sl][:, dc * 128 + 0:dc * 128 + 128],
                            woT[:, dc * D + ec * 512:dc * D + (ec + 1) * 512],
                            start=(dc == 0), stop=(dc == NDC - 1))
                cnt = self.s_pe.inc(last)
                u_yT[sl].append((self.s_pe, cnt))
                wp[f"wo_{c}"] = cnt

            @block.vector
            def _(v, c=c, sl=sl):
                W(v, "vector", self.s_pe, wp[f"wo_{c}"])
                W(v, "vector", self.s_l3[sl], wp[f"l3_{c}"])
                for (sem, cnt) in u_x2t[sl]:
                    W(v, "vector", sem, cnt)
                u_x2t[sl].clear()
                cnt = self.s_dve.inc(v.tensor_tensor(
                    x2t[sl][:], xrs[sl][:], p_wo[:], ALU.add))
                u_pwo.append((self.s_dve, cnt))
                u_xrs[sl].append((self.s_dve, cnt))
                wp[f"x2_{c}"] = cnt

            @block.scalar
            def _(sc, c=c, sl=sl):
                W(sc, "scalar", self.s_dve, wp[f"x2_{c}"])
                cnt = self.s_x2[sl].inc(sc.dma_start(
                    out=self.x2dram[c * 128:(c + 1) * 128, :], in_=x2t[sl][:]), 16)
                u_x2t[sl].append((self.s_x2[sl], cnt))
                W(sc, "scalar", self.s_act, wp.get("sq3", 0))
                W(sc, "scalar", self.s_dve, wp.get(f"stats_rd_{c - 1}", 0))
                wp["sq3"] = self.s_act.inc(sc.activation(
                    sq3[:], x2t[sl][:], AF.Square,
                    accum_out=st3[:, 1:2]))

            @block.vector
            def _(v, c=c, sl=sl):
                W(v, "vector", self.s_dve, wp[f"x2_{c}"])
                W(v, "vector", self.s_act, wp.get(f"norm_{c - 1}", 0))
                c1 = self.s_dve.inc(v.reduce_sum(st3[:, 0:1], x2t[sl][:], axis=AX.X))
                W(v, "vector", self.s_dve, c1)
                c1 = self.s_dve.inc(v.tensor_scalar_mul(st3[:, 2:3], st3[:, 0:1],
                                                        1.0 / D))
                W(v, "vector", self.s_act, wp["sq3"])
                W(v, "vector", self.s_dve, c1)
                c1 = self.s_dve.inc(v.tensor_tensor(st3[:, 3:4], st3[:, 2:3],
                                                    st3[:, 2:3], ALU.mult))
                W(v, "vector", self.s_dve, c1)
                wp[f"var3_{c}"] = self.s_dve.inc(v.scalar_tensor_tensor(
                    st3[:, 3:4], st3[:, 1:2], 1.0 / D, st3[:, 3:4],
                    ALU.mult, ALU.subtract))

            @block.scalar
            def _(sc, c=c):
                W(sc, "scalar", self.s_dve, wp[f"var3_{c}"])
                wp[f"rstd3_{c}"] = self.s_act.inc(sc.activation(
                    st3[:, 4:5], st3[:, 3:4], AF.Sqrt, bias=self.ceps[:, 0:1]))

            @block.vector
            def _(v, c=c):
                W(v, "vector", self.s_act, wp[f"rstd3_{c}"])
                c1 = self.s_dve.inc(v.reciprocal(st3[:, 6:7], st3[:, 4:5]))
                W(v, "vector", self.s_dve, c1)
                c1 = self.s_dve.inc(v.tensor_copy(st3[:, 4:5], st3[:, 6:7]))
                W(v, "vector", self.s_dve, c1)
                wp[f"nmr3_{c}"] = self.s_dve.inc(v.scalar_tensor_tensor(
                    st3[:, 5:6], st3[:, 2:3], -1.0, st3[:, 4:5],
                    ALU.mult, ALU.mult))
                wp[f"stats_rd_{c}"] = self.s_dve.c

            @block.scalar
            def _(sc, c=c, sl=sl):
                W(sc, "scalar", self.s_dve, wp[f"nmr3_{c}"])
                for (sem, cnt) in u_zt[sl]:
                    W(sc, "scalar", sem, cnt)
                u_zt[sl].clear()
                wp[f"norm_{c}"] = self.s_act.inc(sc.activation(
                    zt[sl][:], x2t[sl][:], AF.Identity,
                    bias=st3[:, 5:6], scale=st3[:, 4:5]))

            for pb in range(4):
                @block.tensor
                def _(t, c=c, sl=sl, pb=pb):
                    W(t, "tensor", self.s_act, wp[f"norm_{c}"])
                    for (sem, cnt) in u_tr3[pb % 2]:
                        W(t, "tensor", sem, cnt)
                    u_tr3[pb % 2].clear()
                    if pb == 0:
                        for (sem, cnt) in u_zTt[sl]:
                            W(t, "tensor", sem, cnt)
                        u_zTt[sl].clear()
                    for j in range(4):
                        dc = pb * 4 + j
                        last = t.transpose(p_tr3[pb % 2][:, j * 128:(j + 1) * 128],
                                           zt[sl][:, dc * 128:(dc + 1) * 128],
                                           self.sident[:])
                    cnt = self.s_pe.inc(last)
                    u_zt[sl].append((self.s_pe, cnt))
                    wp[f"ztr_{c}_{pb}"] = cnt

                @block.vector
                def _(v, c=c, sl=sl, pb=pb):
                    W(v, "vector", self.s_pe, wp[f"ztr_{c}_{pb}"])
                    cnt = self.s_dve.inc(v.tensor_copy(
                        zTt[sl][:, pb * 512:(pb + 1) * 512], p_tr3[pb % 2][:]))
                    u_tr3[pb % 2].append((self.s_dve, cnt))
                    wp[f"zev_{c}_{pb}"] = cnt

            for pb in range(4):
                @block.tensor
                def _(t, c=c, sl=sl, pb=pb):
                    W(t, "tensor", self.s_dve, wp[f"zev_{c}_{pb}"])
                    W(t, "tensor", self.s_dve, self.wp["Hmats"])
                    for (sem, cnt) in u_f1[pb % 2]:
                        W(t, "tensor", sem, cnt)
                    u_f1[pb % 2].clear()
                    for j in range(4):
                        dc = pb * 4 + j
                        last = t.matmul(p_f1[pb % 2][:, j * 128:(j + 1) * 128],
                                        self.H1[:, dc * 128:(dc + 1) * 128],
                                        zTt[sl][:, dc * 128:(dc + 1) * 128],
                                        start=True, stop=True)
                    cnt = self.s_pe.inc(last)
                    u_zTt[sl].append((self.s_pe, cnt))
                    wp[f"f1_{c}_{pb}"] = cnt

                @block.vector
                def _(v, c=c, pb=pb):
                    W(v, "vector", self.s_pe, wp[f"f1_{c}_{pb}"])
                    if pb == 0:
                        for (sem, cnt) in u_s1:
                            W(v, "vector", sem, cnt)
                        u_s1.clear()
                    cnt = self.s_dve.inc(v.tensor_copy(
                        s1[:, pb * 512:(pb + 1) * 512], p_f1[pb % 2][:]))
                    u_f1[pb % 2].append((self.s_dve, cnt))
                    wp[f"f1ev_{c}"] = cnt

            def bf_level(eng, engname, src, dst, m, c, width=128, nch=16,
                         dst_slices=None):
                lastc = None
                for ch in range(nch):
                    a = ch & ~m
                    b_ = ch | m
                    sa = src[:, a * width:(a + 1) * width]
                    sbp = src[:, b_ * width:(b_ + 1) * width]
                    if dst_slices is None:
                        do = dst[:, ch * width:(ch + 1) * width]
                    else:
                        do = dst_slices(ch)
                    op = ALU.add if (ch & m) == 0 else ALU.subtract
                    if (ch & m) == 0:
                        lastc = eng.tensor_tensor(do, sa, sbp, ALU.add)
                    else:
                        lastc = eng.tensor_tensor(do, sa, sbp, ALU.subtract)
                return lastc

            @block.vector
            def _(v, c=c):
                W(v, "vector", self.s_dve, wp[f"f1ev_{c}"])
                for (sem, cnt) in u_s2:
                    W(v, "vector", sem, cnt)
                u_s2.clear()
                last = bf_level(v, "vector", s1, s2, 1, c)
                c1 = self.s_dve.inc(last)
                W(v, "vector", self.s_dve, c1)
                last = bf_level(v, "vector", s2, s1, 2, c)
                c1 = self.s_dve.inc(last)
                W(v, "vector", self.s_dve, c1)
                last = bf_level(v, "vector", s1, s2, 4, c)
                c1 = self.s_dve.inc(last)
                W(v, "vector", self.s_dve, c1)
                last = bf_level(
                    v, "vector", s2, None, 8, c,
                    dst_slices=lambda ch: self.zpT[:, ch * 1024 + c * 128:
                                                   ch * 1024 + c * 128 + 128])
                cnt = self.s_dve.inc(last)
                u_s1.append((self.s_dve, cnt))
                u_s2.append((self.s_dve, cnt))
                wp[f"zp_{c}"] = cnt
        wp["zpT_done"] = self.s_dve.c
        wp["p3_pe"] = self.s_pe.c
        wp["p3_act"] = self.s_act.c
        wp["x2_done0"] = self.s_x2[0].c
        wp["x2_done1"] = self.s_x2[1].c

    def emit_ffn_casts(self, block):
        nc, wp = self.nc, self.wp
        d = nc.dram_tensor
        self.i_wo = d("wo", [D, D], F32, kind="ExternalInput").ap()
        self.i_wup = d("wup", [INTER, D], F32, kind="ExternalInput").ap()
        self.i_wga = d("wga", [INTER, D], F32, kind="ExternalInput").ap()
        self.i_wdn = d("wdn", [D, INTER], F32, kind="ExternalInput").ap()
        self.wo16 = d("wo16", [D, D], BF16, kind="Internal").ap()
        self.wup16 = d("wup16", [INTER, D], BF16, kind="Internal").ap()
        self.wga16 = d("wga16", [INTER, D], BF16, kind="Internal").ap()
        self.wdn16 = d("wdn16", [D, INTER], BF16, kind="Internal").ap()

        ops = [(self.s_cw, self.wo16, self.i_wo)]
        for k in range(4):
            ops.append((self.s_cu, self.wup16[k * 2048:(k + 1) * 2048, :],
                        self.i_wup[k * 2048:(k + 1) * 2048, :]))
        for k in range(4):
            ops.append((self.s_cg, self.wga16[k * 2048:(k + 1) * 2048, :],
                        self.i_wga[k * 2048:(k + 1) * 2048, :]))
        for k in range(4):
            ops.append((self.s_cd, self.wdn16[k * 512:(k + 1) * 512, :],
                        self.i_wdn[k * 512:(k + 1) * 512, :]))
        self._cast_ops = ops
        self._cast_k = 0
        wp["wo16"] = self.s_cw.c + 16
        wp["wup16"] = self.s_cu.c + 64
        wp["wga16"] = self.s_cg.c + 64
        wp["wdn16"] = self.s_cd.c + 64

    def pump_casts(self, block, n):
        ops = self._cast_ops[self._cast_k:self._cast_k + n]
        if not ops:
            return
        self._cast_k += len(ops)

        @block.gpsimd
        def _(g):
            for sem, dst, src in ops:
                sem.inc(g.dma_start(out=dst, in_=src), 16)

    def emit_pretranspose(self, block):
        nc, wp, W = self.nc, self.wp, self.W
        d = nc.dram_tensor
        self.wupT16 = d("wupT16", [D, INTER], BF16, kind="Internal").ap()
        self.wgaT16 = d("wgaT16", [D, INTER], BF16, kind="Internal").ap()
        self.wdnT16 = d("wdnT16", [INTER, D], BF16, kind="Internal").ap()
        jobs = []
        for i4 in range(16):
            for dc in range(16):
                jobs.append((self.wup16[i4 * 512:(i4 + 1) * 512,
                                        dc * 128:(dc + 1) * 128],
                             self.wupT16[dc * 128:(dc + 1) * 128,
                                         i4 * 512:(i4 + 1) * 512], self.s_cu))
        for i4 in range(16):
            for dc in range(16):
                jobs.append((self.wga16[i4 * 512:(i4 + 1) * 512,
                                        dc * 128:(dc + 1) * 128],
                             self.wgaT16[dc * 128:(dc + 1) * 128,
                                         i4 * 512:(i4 + 1) * 512], self.s_cg))
        for ic in range(64):
            for eg in range(4):
                jobs.append((self.wdn16[eg * 512:(eg + 1) * 512,
                                        ic * 128:(ic + 1) * 128],
                             self.wdnT16[ic * 128:(ic + 1) * 128,
                                         eg * 512:(eg + 1) * 512], self.s_cd))
        self._pt_jobs = jobs
        self._pt_k = 0

    def pump_pt(self, block, n):
        nc, wp, W = self.nc, self.wp, self.W
        jobs = self._pt_jobs[self._pt_k:self._pt_k + n]
        if not jobs:
            return
        k0 = self._pt_k
        self._pt_k += len(jobs)
        nslot = 4

        @block.sync
        def _(sy):
            for i, (src, dst, gate) in enumerate(jobs):
                k = k0 + i
                sl = k % nslot
                W(sy, "sync", gate, gate.c)
                W(sy, "sync", self.s_ps[sl], self.s_ps[sl].c)
                self.s_pt[sl].inc(sy.dma_start_transpose(
                    out=self.ptstg[sl][:], in_=src), 16)
                wp[f"pt_{k}"] = self.s_pt[sl].c

        @block.gpsimd
        def _(gp):
            for i, (src, dst, gate) in enumerate(jobs):
                k = k0 + i
                sl = k % nslot
                W(gp, "gpsimd", self.s_pt[sl], wp[f"pt_{k}"])
                self.s_ps[sl].inc(gp.dma_start(out=dst, in_=self.ptstg[sl][:]), 16)

    def finish_pt(self, block):
        self.pump_pt(block, len(self._pt_jobs) - self._pt_k)
        self.wp["ptdone"] = [(self.s_ps[i], self.s_ps[i].c) for i in range(4)]

    def emit_p4a(self, block):
        nc, wp, W = self.nc, self.wp, self.W
        d = nc.dram_tensor
        self.adram = d("adram", [INTER, 1024], BF16, kind="Internal").ap()
        self.out = d("out", [1024, D], F32, kind="ExternalOutput").ap()

        if self.preload:
            @block.scalar
            def _(sc):
                W(sc, "scalar", self.s_act, wp["p3_act"])
                sc.preload_act_table(AF.Silu)

        with contextlib.ExitStack() as s4a:
            sb = lambda *a, **k: s4a.enter_context(nc.sbuf_tensor(*a, **k))
            ps = lambda *a, **k: s4a.enter_context(nc.psum_tensor(*a, **k))
            wT = [sb(f"wT{i}", [128, NDC * 512], BF16) for i in range(4)]
            atile = [sb(f"atile{i}", [128, 1024], BF16) for i in range(2)]
            usil = sb("usil", [128, 1024], BF16)
            gsil = sb("gsil", [128, 1024], BF16)
            p_u = [ps(f"p_u{i}", [128, 1024], F32) for i in range(2)]
            p_g = [ps(f"p_g{i}", [128, 1024], F32) for i in range(2)]

            u_wT = {i: [] for i in range(4)}
            u_at = {0: [], 1: []}
            u_pu = {0: [], 1: []}
            u_pg = {0: [], 1: []}
            u_us = []
            u_gs = []

            for i4 in range(16):
                dsl = i4 % 2
                @block.sync
                def _(sy, i4=i4, dsl=dsl):
                    if i4 == 0:
                        for (sem, cnt) in wp["ptdone"]:
                            W(sy, "sync", sem, cnt)
                        W(sy, "sync", self.s_pe, wp["p3_pe"])
                        W(sy, "sync", self.s_dve, wp["zpT_done"])
                        W(sy, "sync", self.s_act, wp["p3_act"])
                    for (sem, cnt) in u_wT[dsl * 2] + u_wT[dsl * 2 + 1]:
                        W(sy, "sync", sem, cnt)
                    u_wT[dsl * 2].clear()
                    u_wT[dsl * 2 + 1].clear()
                    src = self.wupT16[:, i4 * 512:(i4 + 1) * 512]
                    src = src.rearrange("(dc p) i -> p dc i", p=128)
                    dst = wT[dsl * 2].ap().rearrange("p (dc i) -> p dc i", dc=NDC)
                    self.s_w4[dsl].inc(sy.dma_start(out=dst, in_=src), 16)
                    src = self.wgaT16[:, i4 * 512:(i4 + 1) * 512]
                    src = src.rearrange("(dc p) i -> p dc i", p=128)
                    dst = wT[dsl * 2 + 1].ap().rearrange("p (dc i) -> p dc i", dc=NDC)
                    self.s_w4[dsl].inc(sy.dma_start(out=dst, in_=src), 16)
                    wp[f"w4_{i4}"] = self.s_w4[dsl].c

                for it4 in range(4):
                    it = i4 * 4 + it4
                    psl = it % 2
                    @block.tensor
                    def _(t, i4=i4, it4=it4, it=it, psl=psl, dsl=dsl):
                        W(t, "tensor", self.s_w4[dsl], wp[f"w4_{i4}"])
                        W(t, "tensor", self.s_dve, wp["zpT_done"])
                        for (sem, cnt) in u_pu[psl]:
                            W(t, "tensor", sem, cnt)
                        u_pu[psl].clear()
                        for dc in range(NDC):
                            for tb in range(2):
                                last = t.matmul(
                                    p_u[psl][:, tb * 512:(tb + 1) * 512],
                                    wT[dsl * 2][:, dc * 512 + it4 * 128:
                                                dc * 512 + (it4 + 1) * 128],
                                    self.zpT[:, dc * 1024 + tb * 512:
                                             dc * 1024 + (tb + 1) * 512],
                                    start=(dc == 0),
                                    stop=(dc == NDC - 1))
                        cnt = self.s_pe.inc(last)
                        u_wT[dsl * 2].append((self.s_pe, cnt))
                        wp[f"umm_{it}"] = cnt
                        for (sem, cnt) in u_pg[psl]:
                            W(t, "tensor", sem, cnt)
                        u_pg[psl].clear()
                        for dc in range(NDC):
                            for tb in range(2):
                                last = t.matmul(
                                    p_g[psl][:, tb * 512:(tb + 1) * 512],
                                    wT[dsl * 2 + 1][:, dc * 512 + it4 * 128:
                                                    dc * 512 + (it4 + 1) * 128],
                                    self.zpT[:, dc * 1024 + tb * 512:
                                             dc * 1024 + (tb + 1) * 512],
                                    start=(dc == 0),
                                    stop=(dc == NDC - 1))
                        cnt = self.s_pe.inc(last)
                        u_wT[dsl * 2 + 1].append((self.s_pe, cnt))
                        wp[f"gmm_{it}"] = cnt

                    @block.scalar
                    def _(sc, it=it, psl=psl):
                        W(sc, "scalar", self.s_pe, wp[f"umm_{it}"])
                        W(sc, "scalar", self.s_dve, wp["cst_done"])
                        for (sem, cnt) in u_us:
                            W(sc, "scalar", sem, cnt)
                        u_us.clear()
                        c1 = self.s_act.inc(sc.activation(
                            usil[:], p_u[psl][:], AF.Identity,
                            scale=self.wugT[:, it:it + 1]))
                        u_pu[psl].append((self.s_act, c1))
                        W(sc, "scalar", self.s_pe, wp[f"gmm_{it}"])
                        for (sem, cnt) in u_gs:
                            W(sc, "scalar", sem, cnt)
                        u_gs.clear()
                        c2 = self.s_act.inc(sc.activation(
                            gsil[:], p_g[psl][:], AF.Silu,
                            scale=self.wugT[:, 64 + it:64 + it + 1]))
                        u_pg[psl].append((self.s_act, c2))
                        wp[f"sil_{it}"] = self.s_act.c

                    asl = it % 2
                    @block.vector
                    def _(v, it=it, asl=asl):
                        W(v, "vector", self.s_act, wp[f"sil_{it}"])
                        for (sem, cnt) in u_at[asl]:
                            W(v, "vector", sem, cnt)
                        u_at[asl].clear()
                        cnt = self.s_dve.inc(v.tensor_tensor(
                            atile[asl][:], usil[:], gsil[:], ALU.mult))
                        u_us.append((self.s_dve, cnt))
                        u_gs.append((self.s_dve, cnt))
                        wp[f"amul_{it}"] = cnt

                    @block.scalar
                    def _(sc, it=it, asl=asl):
                        W(sc, "scalar", self.s_dve, wp[f"amul_{it}"])
                        cnt = self.s_aw[asl].inc(sc.dma_start(
                            out=self.adram[it * 128:(it + 1) * 128, :],
                            in_=atile[asl][:]), 16)
                        u_at[asl].append((self.s_aw[asl], cnt))
            wp["adram_done0"] = self.s_aw[0].c
            wp["adram_done1"] = self.s_aw[1].c
            wp["p4a_act"] = self.s_act.c
            wp["p4a_pe"] = self.s_pe.c
            wp["p4a_dve"] = self.s_dve.c

    def emit_p4b(self, block, scope):
        nc, wp, W = self.nc, self.wp, self.W
        sb = lambda *a, **k: scope.enter_context(nc.sbuf_tensor(*a, **k))
        ps = lambda *a, **k: scope.enter_context(nc.psum_tensor(*a, **k))
        IN = sb("IN", [128, 64 * 256], BF16)
        SC = sb("SC", [128, 64 * 256], BF16)
        OB = sb("OB", [128, 64 * 256], BF16)
        OC = sb("OC", [128, 64 * 256], BF16)
        wdT = [sb(f"wdT{i}", [128, 1024], BF16) for i in range(2)]
        xoT = sb("xoT", [128, 16 * 256], BF16)
        x2r = [sb(f"x2r{i}", [128, D], F32) for i in range(2)]
        xfin = [sb(f"xfin{i}", [128, D], F32) for i in range(2)]
        p_a = [ps(f"p_a{i}", [128, 256], F32) for i in range(2)]
        p_dn = ps("p_dn", [128, 8 * 256], F32)
        p_ft = [ps(f"p_ft{i}", [128, 512], BF16) for i in range(2)]

        u_pa = {0: [], 1: []}
        u_pdn = []
        u_pft = {0: [], 1: []}
        u_wdT = {0: [], 1: []}
        u_xoT = []
        u_x2r = {0: [], 1: []}
        u_xfin = {0: [], 1: []}
        NDVE = 46

        def emit_stage(q):
            X = OB if q % 2 == 0 else OC
            for g8 in range(8):
                @block.sync
                def _(sy, q=q, g8=g8):
                    W(sy, "sync", self.s_aw[0], wp["adram_done0"])
                    W(sy, "sync", self.s_aw[1], wp["adram_done1"])
                    if q == 0 and g8 == 0:
                        W(sy, "sync", self.s_act, wp["p4a_act"])
                        W(sy, "sync", self.s_pe, wp["p4a_pe"])
                        W(sy, "sync", self.s_dve, wp["p4a_dve"])
                        W(sy, "sync", self.s_x2[0], wp["x2_done0"])
                        W(sy, "sync", self.s_x2[1], wp["x2_done1"])
                    if q > 0:
                        W(sy, "sync", self.s_pe, wp[f"fa_{q - 1}"])
                    src = self.adram[g8 * 1024:(g8 + 1) * 1024,
                                     q * 256:(q + 1) * 256]
                    src = src.rearrange("(c p) t -> p c t", p=128)
                    dst = IN[:, g8 * 8 * 256:(g8 + 1) * 8 * 256]
                    dst = dst.rearrange("p (c t) -> p c t", c=8)
                    self.s_ar[g8].inc(sy.dma_start(out=dst, in_=src), 16)
                    wp[f"ar_{q}_{g8}"] = self.s_ar[g8].c

            for ic in range(64):
                psl = ic % 2
                @block.tensor
                def _(t, q=q, ic=ic, psl=psl):
                    W(t, "tensor", self.s_ar[ic // 8], wp[f"ar_{q}_{ic // 8}"])
                    W(t, "tensor", self.s_dve, wp["Hmats"])
                    if ic == 0 and q >= 2:
                        W(t, "tensor", self.s_pe, wp[f"dn_{q - 2}_1"])
                        for (sem, cnt) in u_pdn:
                            W(t, "tensor", sem, cnt)
                    for (sem, cnt) in u_pa[psl]:
                        W(t, "tensor", sem, cnt)
                    u_pa[psl].clear()
                    wp[f"fa_{q}_{ic}"] = self.s_pe.inc(t.matmul(
                        p_a[psl][:], self.H2[:, ic * 128:(ic + 1) * 128],
                        IN[:, ic * 256:(ic + 1) * 256], start=True, stop=True))
                    if ic == 63:
                        wp[f"fa_{q}"] = self.s_pe.c

                @block.vector
                def _(v, q=q, ic=ic, psl=psl, X=X):
                    W(v, "vector", self.s_pe, wp[f"fa_{q}_{ic}"])
                    if ic == 0 and q >= 2:
                        W(v, "vector", self.s_pe, wp[f"dn_{q - 2}_1"])
                    cnt = self.s_dve.inc(v.tensor_copy(
                        X[:, ic * 256:(ic + 1) * 256], p_a[psl][:]))
                    u_pa[psl].append((self.s_dve, cnt))
                    wp[f"faev_{q}"] = cnt

        def emit_bf(q):
            X = OB if q % 2 == 0 else OC
            levels = [(X, SC, 1), (SC, X, 2), (X, SC, 4), (SC, X, 8),
                      (X, SC, 16), (SC, X, 32)]
            for li, (src, dst, m) in enumerate(levels):
                def emit_half(sec, en, chunks, q=q, li=li, src=src, dst=dst, m=m):
                    hold = {}
                    @sec
                    def _(e):
                        if li == 0:
                            W(e, en, self.s_dve, wp[f"faev_{q}"])
                            if q > 0:
                                for (psem, pcnt) in wp[f"bf_{q - 1}_5"]:
                                    W(e, en, psem, pcnt)
                        else:
                            for (psem, pcnt) in wp[f"bf_{q}_{li - 1}"]:
                                W(e, en, psem, pcnt)
                        lastc = None
                        for ch in chunks:
                            a_ = ch & ~m
                            b_ = ch | m
                            sa = src[:, a_ * 256:(a_ + 1) * 256]
                            sbp = src[:, b_ * 256:(b_ + 1) * 256]
                            do = dst[:, ch * 256:(ch + 1) * 256]
                            op = ALU.add if (ch & m) == 0 else ALU.subtract
                            lastc = e.tensor_tensor(do, sa, sbp, op)
                        if en == "gpsimd":
                            hold["r"] = (self.s_pool, self.s_pool.inc(lastc))
                        else:
                            hold["r"] = (self.s_dve, self.s_dve.inc(lastc))
                    return hold["r"]
                r1 = emit_half(block.vector, "vector", range(NDVE))
                r2 = emit_half(block.gpsimd, "gpsimd", range(NDVE, 64))
                wp[f"bf_{q}_{li}"] = [r1, r2]

        def emit_down(q):
            X = OB if q % 2 == 0 else OC
            for eh in range(2):
                for ic in range(64):
                    wsl = ic % 2
                    @block.sync
                    def _(sy, q=q, eh=eh, ic=ic, wsl=wsl):
                        W(sy, "sync", self.s_aw[0], wp["adram_done0"])
                        W(sy, "sync", self.s_aw[1], wp["adram_done1"])
                        W(sy, "sync", self.s_x2[0], wp["x2_done0"])
                        W(sy, "sync", self.s_x2[1], wp["x2_done1"])
                        W(sy, "sync", self.s_act, wp["p4a_act"])
                        W(sy, "sync", self.s_pe, wp["p4a_pe"])
                        W(sy, "sync", self.s_dve, wp["p4a_dve"])
                        for (sem, cnt) in u_wdT[wsl]:
                            W(sy, "sync", sem, cnt)
                        u_wdT[wsl].clear()
                        self.s_wd[wsl].inc(sy.dma_start(
                            out=wdT[wsl][:],
                            in_=self.wdnT16[ic * 128:(ic + 1) * 128,
                                            eh * 1024:(eh + 1) * 1024]), 16)
                        wp[f"wd_{q}_{eh}_{ic}"] = self.s_wd[wsl].c

                    @block.tensor
                    def _(t, q=q, eh=eh, ic=ic, wsl=wsl, X=X):
                        W(t, "tensor", self.s_wd[wsl], wp[f"wd_{q}_{eh}_{ic}"])
                        for (psem, pcnt) in wp[f"bf_{q}_5"]:
                            W(t, "tensor", psem, pcnt)
                        if ic == 0:
                            for (sem, cnt) in u_pdn:
                                W(t, "tensor", sem, cnt)
                            u_pdn.clear()
                        for et in range(8):
                            last = t.matmul(
                                p_dn[:, et * 256:(et + 1) * 256],
                                wdT[wsl][:, et * 128:(et + 1) * 128],
                                X[:, ic * 256:(ic + 1) * 256],
                                start=(ic == 0 and et % 2 == 0),
                                stop=(ic == 63 and et % 2 == 1))
                        cnt = self.s_pe.inc(last)
                        u_wdT[wsl].append((self.s_pe, cnt))
                        wp[f"dn_{q}_{eh}"] = cnt

                @block.scalar
                def _(sc, q=q, eh=eh):
                    W(sc, "scalar", self.s_pe, wp[f"dn_{q}_{eh}"])
                    if eh == 0:
                        for (sem, cnt) in u_xoT:
                            W(sc, "scalar", sem, cnt)
                        u_xoT.clear()
                    for et8 in range(8):
                        et = eh * 8 + et8
                        cnt = self.s_act.inc(sc.activation(
                            xoT[:, et * 256:(et + 1) * 256],
                            p_dn[:, et8 * 256:(et8 + 1) * 256],
                            AF.Identity, scale=self.wdpT[:, et:et + 1]))
                    u_pdn.append((self.s_act, cnt))
                    wp[f"xoev_{q}"] = cnt

            for tt in range(2):
                gt = q * 2 + tt
                xsl = gt % 2
                @block.sync
                def _(sy, gt=gt, xsl=xsl):
                    for (sem, cnt) in u_x2r[xsl]:
                        W(sy, "sync", sem, cnt)
                    u_x2r[xsl].clear()
                    W(sy, "sync", self.s_x2[0], wp["x2_done0"])
                    W(sy, "sync", self.s_x2[1], wp["x2_done1"])
                    self.s_x2r[xsl].inc(sy.dma_start(
                        out=x2r[xsl][:],
                        in_=self.x2dram[gt * 128:(gt + 1) * 128, :]), 16)
                    wp[f"x2r_{gt}"] = self.s_x2r[xsl].c

                for pb in range(4):
                    @block.tensor
                    def _(t, q=q, tt=tt, pb=pb, gt=gt):
                        W(t, "tensor", self.s_act, wp[f"xoev_{q}"])
                        for (sem, cnt) in u_pft[pb % 2]:
                            W(t, "tensor", sem, cnt)
                        u_pft[pb % 2].clear()
                        for j in range(4):
                            et = pb * 4 + j
                            last = t.transpose(
                                p_ft[pb % 2][:, j * 128:(j + 1) * 128],
                                xoT[:, et * 256 + tt * 128:
                                    et * 256 + tt * 128 + 128],
                                self.sident[:])
                        cnt = self.s_pe.inc(last)
                        u_xoT.append((self.s_pe, cnt))
                        wp[f"ftr_{gt}_{pb}"] = cnt

                    @block.vector
                    def _(v, gt=gt, pb=pb, xsl=xsl):
                        W(v, "vector", self.s_pe, wp[f"ftr_{gt}_{pb}"])
                        W(v, "vector", self.s_x2r[xsl], wp[f"x2r_{gt}"])
                        if pb == 0:
                            for (sem, cnt) in u_xfin[xsl]:
                                W(v, "vector", sem, cnt)
                            u_xfin[xsl].clear()
                        cnt = self.s_dve.inc(v.tensor_tensor(
                            xfin[xsl][:, pb * 512:(pb + 1) * 512],
                            x2r[xsl][:, pb * 512:(pb + 1) * 512],
                            p_ft[pb % 2][:], ALU.add))
                        u_pft[pb % 2].append((self.s_dve, cnt))
                        u_x2r[xsl].append((self.s_dve, cnt))
                        wp[f"fin_{gt}"] = cnt

                @block.scalar
                def _(sc, gt=gt, xsl=xsl):
                    W(sc, "scalar", self.s_dve, wp[f"fin_{gt}"])
                    cnt = self.s_so[xsl].inc(sc.dma_start(
                        out=self.out[gt * 128:(gt + 1) * 128, :],
                        in_=xfin[xsl][:]), 16)
                    u_xfin[xsl].append((self.s_so[xsl], cnt))

        emit_stage(0)
        emit_bf(0)
        for q in range(4):
            if q + 1 < 4:
                emit_stage(q + 1)
                emit_bf(q + 1)
            emit_down(q)

    def emit_p1_taps(self, block):
        nc = self.nc
        o_qT = nc.dram_tensor("o_qT", [2, DH, L], BF16, kind="ExternalOutput").ap()
        o_kT = nc.dram_tensor("o_kT", [2, DH, L], BF16, kind="ExternalOutput").ap()
        o_V = nc.dram_tensor("o_V", [2, 128, L], BF16, kind="ExternalOutput").ap()
        W = self.W
        last_hb = NG * 2 - 1
        for h in range(2):
            for (src, dst) in [(self.qT[h], o_qT), (self.kT[h], o_kT),
                               (self.V[h], o_V)]:
                @block.scalar
                def _(sc, h=h, src=src, dst=dst):
                    W(sc, "scalar", self.s_pool, self.rope_after[last_hb])
                    W(sc, "scalar", self.s_dve, self.v_after[last_hb])
                    self.s_st.inc(sc.dma_start(out=dst[h, :, :], in_=src[:]), 16)

    def emit_final_wait(self, block):
        @block.sync
        def _(s):
            s.wait_ge(self.s_st.h, self.s_st.c)
            for c in range(8):
                if self.s_cc[c].c:
                    s.wait_ge(self.s_cc[c].h, self.s_cc[c].c)
            for c in range(8):
                if self.s_yw[c].c:
                    s.wait_ge(self.s_yw[c].h, self.s_yw[c].c)
            for i in range(2):
                if self.s_so[i].c:
                    s.wait_ge(self.s_so[i].h, self.s_so[i].c)



def hadamard(n):
    H = np.array([[1.0]], dtype=np.float32)
    while H.shape[0] < n:
        H = np.block([[H, H], [H, -H]])
    return H.astype(np.float32)


def make_consts():
    inv = 1.0 / (10000.0 ** (np.arange(0, DH, 2, dtype=np.float64) / DH))
    ang = np.arange(L, dtype=np.float64)[:, None] * inv[None, :]
    ang2 = np.repeat(ang, 2, axis=1)
    cos_t = np.cos(ang2).T.astype(bf)
    sin_t = np.sin(ang2).T.astype(bf)
    pmT = np.zeros((128, 128), dtype=np.float32)
    for j in range(64):
        pmT[2 * j + 1, 2 * j] = -1.0
        pmT[2 * j, 2 * j + 1] = 1.0
    ident = np.eye(128, dtype=np.float32)
    return cos_t, sin_t, pmT.astype(bf), ident.astype(bf)


def make_mask():
    m = np.zeros((128, 128), dtype=np.float32)
    for k in range(128):
        m[k, 0:k] = -1e9
    return m


def wqkv_slice(W_qkv, core):
    h0 = 2 * core
    rows = []
    for sec in range(3):
        for h in (h0, h0 + 1):
            rows.append(W_qkv[sec * D + h * DH: sec * D + (h + 1) * DH, :])
    return np.ascontiguousarray(np.concatenate(rows, axis=0))


_CACHE = {}


def _get_nc():
    if "nc" not in _CACHE:
        _CACHE["nc"] = Builder().build()
    return _CACHE["nc"]


def kernel(L0_attn_x, L0_attn_W_qkv, L0_attn_W_o, L0_ffn_SU_ug, L0_ffn_SU_dp,
           L0_ffn_Wscale_ug, L0_ffn_Wscale_dp, L0_ffn_W_up, L0_ffn_W_gate,
           L0_ffn_W_down):
    x = np.ascontiguousarray(np.asarray(L0_attn_x, dtype=np.float32).reshape(L, D))
    W_qkv = np.ascontiguousarray(np.asarray(L0_attn_W_qkv, dtype=np.float32))
    Wo = np.ascontiguousarray(np.asarray(L0_attn_W_o, dtype=np.float32))
    SUug = np.asarray(L0_ffn_SU_ug, dtype=np.float32)
    SUdp = np.asarray(L0_ffn_SU_dp, dtype=np.float32)
    WSug = np.asarray(L0_ffn_Wscale_ug, dtype=np.float32)
    WSdp = np.asarray(L0_ffn_Wscale_dp, dtype=np.float32)
    Wup = np.ascontiguousarray(np.asarray(L0_ffn_W_up, dtype=np.float32))
    Wga = np.ascontiguousarray(np.asarray(L0_ffn_W_gate, dtype=np.float32))
    Wdn = np.ascontiguousarray(np.asarray(L0_ffn_W_down, dtype=np.float32))
    cos_t, sin_t, pmT, ident = make_consts()
    mask = make_mask()
    h128 = hadamard(128).astype(bf)

    nc = _get_nc()
    in_maps = []
    rows_of = {}
    for c in range(NCORES):
        idx = np.zeros((128, 16), dtype=np.int32)
        for t8 in range(8):
            for h in range(2):
                idx[:, t8 * 2 + h] = (t8 * 128 + np.arange(128)) * 16 + c * 2 + h
        myrows = np.concatenate([np.arange(cc * 1024 + c * 128,
                                           cc * 1024 + c * 128 + 128)
                                 for cc in range(8)])
        rows_of[c] = myrows
        in_maps.append({
            "x": x, "wqkv": wqkv_slice(W_qkv, c),
            "cos_t": cos_t, "sin_t": sin_t, "pmT": pmT, "ident": ident,
            "mask": mask, "sidx": idx,
            "xres": np.ascontiguousarray(x[myrows]),
            "wo": Wo, "wup": Wup, "wga": Wga, "wdn": Wdn,
            "wscale_ug": np.ascontiguousarray(WSug.reshape(128, 128)),
            "wscale_dp": np.ascontiguousarray(WSdp.reshape(16, 128)),
            "su_ug": np.ascontiguousarray(SUug.reshape(16, 128)),
            "su_dp": np.ascontiguousarray(SUdp.reshape(64, 128)),
            "h128": h128,
        })
    res = run_bass_kernel_spmd(nc, in_maps, core_ids=list(range(NCORES)))
    out = np.empty((L, D), dtype=np.float32)
    for c in range(NCORES):
        out[rows_of[c]] = np.asarray(res.results[c]["out"], dtype=np.float32)
    return out.reshape(1, L, D)
